# version 03
# baseline (speedup 1.0000x reference)
import numpy as np
import ml_dtypes
from contextlib import ExitStack

import concourse.bass as bass
import concourse.mybir as mybir
from concourse.bass_utils import run_bass_kernel_spmd

F32 = mybir.dt.float32
BF16 = mybir.dt.bfloat16
AF = mybir.ActivationFunctionType
ALU = mybir.AluOpType
AXX = mybir.AxisListType.X
DBG_A = False


class CFG:
    def __init__(self, S=4096, E_R=64, G_GROUPS=8, TOPK_G=4, TOPK=6, GRP=8,
                 n_cores=8):
        self.S = S
        self.D = 2048
        self.DC = 16
        self.NH = 16
        self.E_R = E_R
        self.G_GROUPS = G_GROUPS
        self.GSZ = E_R // G_GROUPS
        self.TOPK_G = TOPK_G
        self.TOPK = TOPK
        self.RSF = 2.5
        self.NE = E_R + 4
        self.IC = 4
        self.MB = S // 512
        self.OWN = S // 2
        self.OC = self.OWN // 512
        self.TQ = self.OWN // 512
        self.TT = self.OWN // 128
        self.OMB = self.OWN // 512
        self.n_cores = n_cores
        self.groups = []
        e = 0
        while e < E_R:
            self.groups.append(list(range(e, min(e + GRP, E_R))))
            e += GRP
        self.groups.append(list(range(E_R, self.NE)))

    def own_chunks(self, hh):
        return [2 * i + (1 - hh) for i in range(self.OC)]


class Emitter:
    def __init__(self, sems):
        self.sems = sems
        self.pass1 = True
        self.marks = {}
        self.missing = {}
        self.reset()

    def reset(self):
        self.counts = {k: 0 for k in self.sems}
        self.dma_counts = {k: 0 for k in self.sems}

    def inc(self, ch, instr):
        self.counts[ch] += 1
        if not self.pass1 and instr is not None:
            instr.then_inc(self.sems[ch], 1)
        return self.counts[ch]

    def dinc(self, ch, instr):
        self.dma_counts[ch] += 16
        if not self.pass1 and instr is not None:
            instr.then_inc(self.sems[ch], 16)
        return self.dma_counts[ch]

    def mark(self, key, ch):
        if self.pass1:
            self.marks[key] = (ch, self.counts[ch])

    def mark_d(self, key, ch):
        if self.pass1:
            self.marks[key] = (ch, self.dma_counts[ch])

    def _miss(self, key):
        idxs = [k for k in key[1:] if isinstance(k, int)] if isinstance(key, tuple) else []
        if all(i >= 0 for i in idxs):
            self.missing[key] = self.missing.get(key, 0) + 1

    def wait(self, eng, key):
        if not self.pass1:
            if key not in self.marks:
                self._miss(key)
                return
            ch, v = self.marks[key]
            if v > 0:
                eng.wait_ge(self.sems[ch], v)

    def wait_ge(self, eng, ch, v):
        if not self.pass1 and v > 0:
            eng.wait_ge(self.sems[ch], v)


class _Noop:
    def __getattr__(self, name):
        def f(*a, **k):
            return None
        return f


def build_program(cfg: CFG):
    nc = bass.Bass("TRN2", target_bir_lowering=False,
                   detect_race_conditions=not DBG_A)
    S, D, DC, NH = cfg.S, cfg.D, cfg.DC, cfg.NH
    NE, IC, MB = cfg.NE, cfg.IC, cfg.MB
    OWN, TQ, TT, E_R, OMB, OC = cfg.OWN, cfg.TQ, cfg.TT, cfg.E_R, cfg.OMB, cfg.OC
    NTI = S // 128
    GSZ = cfg.GSZ
    SCALE = float(1.0 / np.sqrt(128.0))

    x_in = nc.dram_tensor("x_in", [S, D], F32, kind="ExternalInput").ap()
    xo_in = nc.dram_tensor("xo_in", [OWN, D], F32, kind="ExternalInput").ap()
    wq_in = nc.dram_tensor("wq_in", [NH, 128, D], F32, kind="ExternalInput").ap()
    wk_in = nc.dram_tensor("wk_in", [NH, 128, D], F32, kind="ExternalInput").ap()
    wv_in = nc.dram_tensor("wv_in", [NH, 128, D], F32, kind="ExternalInput").ap()
    wo_in = nc.dram_tensor("wo_in", [DC, 128, NH * 128], F32, kind="ExternalInput").ap()
    w1f_in = nc.dram_tensor("w1f_in", [NE, 2, 128, DC * 2 * 128], F32, kind="ExternalInput").ap()
    w2f_in = nc.dram_tensor("w2f_in", [DC, 128, NE * IC * 128], F32, kind="ExternalInput").ap()
    gwT_in = nc.dram_tensor("gwT_in", [128, DC * E_R], F32, kind="ExternalInput").ap()
    gb_in = nc.dram_tensor("gb_in", [1, E_R], F32, kind="ExternalInput").ap()
    bmask_in = nc.dram_tensor("bmask_in", [8, 128, 512], BF16, kind="ExternalInput").ap()
    idb_in = nc.dram_tensor("idb_in", [128, 128], BF16, kind="ExternalInput").ap()
    idf_in = nc.dram_tensor("idf_in", [128, 128], F32, kind="ExternalInput").ap()
    onesb_in = nc.dram_tensor("onesb_in", [128, 1], BF16, kind="ExternalInput").ap()
    onesf_in = nc.dram_tensor("onesf_in", [1, 128], F32, kind="ExternalInput").ap()
    ones8_in = nc.dram_tensor("ones8_in", [128, GSZ], F32, kind="ExternalInput").ap()
    out_t = nc.dram_tensor("out_half", [OWN, D], F32, kind="ExternalOutput").ap()

    kT_sp = nc.dram_tensor("kT_sp", [NH, 128, S], BF16, kind="Internal").ap()
    v_sp = nc.dram_tensor("v_sp", [NH, 128, S], BF16, kind="Internal").ap()
    q_sp = nc.dram_tensor("q_sp", [NH, 128, OWN], BF16, kind="Internal").ap()
    xnTo_sp = nc.dram_tensor("xnTo_sp", [128, DC * OWN], BF16, kind="Internal").ap()
    wdT_sp = nc.dram_tensor("wdT_sp", [E_R, OWN], F32, kind="Internal").ap()
    sT_sp = nc.dram_tensor("sT_sp", [128, DC * OWN], F32, kind="Internal").ap()
    wkvb_sp = nc.dram_tensor("wkvb_sp", [NH, 2, 128, D], BF16, kind="Internal").ap()
    wqb_sp = nc.dram_tensor("wqb_sp", [NH, 128, D], BF16, kind="Internal").ap()
    wob_sp = nc.dram_tensor("wob_sp", [DC, 128, NH * 128], BF16, kind="Internal").ap()
    w1b_sp = nc.dram_tensor("w1b_sp", [NE, 2, 128, DC * 2 * 128], BF16, kind="Internal").ap()
    w2b_sp = nc.dram_tensor("w2b_sp", [DC, 128, NE * IC * 128], BF16, kind="Internal").ap()

    ctx = ExitStack()
    sb = lambda n, s, d: ctx.enter_context(nc.sbuf_tensor(n, s, d))
    pst = lambda n, s, d=F32: ctx.enter_context(nc.psum_tensor(n, s, d))
    mksem = lambda n: ctx.enter_context(nc.semaphore(n))

    bmk = [sb(f"bmk{i}", [128, 512], BF16) for i in range(8)]
    idb = sb("idb", [128, 128], BF16)
    idf = sb("idf", [128, 128], F32)
    onesb = sb("onesb", [128, 1], BF16)
    onesf = sb("onesf", [1, 128], F32)
    ones8 = sb("ones8", [128, GSZ], F32)
    gwhi = sb("gwhi", [128, DC * E_R], BF16)
    gwlo = sb("gwlo", [128, DC * E_R], BF16)
    gbbc = sb("gbbc", [128, E_R], F32)
    gbrow = sb("gbrow", [1, E_R], F32)

    NBF = 57856
    NF32 = 19968
    ABF = sb("ABF", [128, NBF], BF16)
    AFL = sb("AFL", [128, NF32], F32)

    def bfv(off, n):
        assert off + n <= NBF, (off, n)
        return ABF[:, off:off + n], off + n

    def ffv(off, n):
        assert off + n <= NF32, (off, n)
        return AFL[:, off:off + n], off + n

    o = 0
    wkv_t = []
    for p in range(4):
        t_, o = bfv(o, 2 * D)
        wkv_t.append(t_)
    xnb = []
    for p in range(2):
        t_, o = bfv(o, D)
        xnb.append(t_)
    stg = []
    for p in range(2):
        t_, o = bfv(o, DC * 512)
        stg.append(t_)
    kvst = []
    for p in range(4):
        t_, o = bfv(o, 512)
        kvst.append(t_)
    vtst = []
    for p in range(2):
        t_, o = bfv(o, 512)
        vtst.append(t_)

    o = 0
    cp_hi = []
    for p in range(2):
        t_, o = bfv(o, D)
        cp_hi.append(t_)
    cp_lo = []
    for p in range(2):
        t_, o = bfv(o, D)
        cp_lo.append(t_)
    cp_stg = []
    for p in range(2):
        t_, o = bfv(o, DC * 512)
        cp_stg.append(t_)
    cp_loT, o = bfv(o, DC * 512)
    wq_t = []
    for p in range(2):
        t_, o = bfv(o, D)
        wq_t.append(t_)
    qst = []
    for p in range(2):
        t_, o = bfv(o, 512)
        qst.append(t_)

    o = 0
    wobd = []
    for p in range(2):
        t_, o = bfv(o, NH * 128)
        wobd.append(t_)
    kbuf, vbuf, qbuf, Ebuf = [], [], [], []
    for p in range(2):
        t_, o = bfv(o, S)
        kbuf.append(t_)
    for p in range(2):
        t_, o = bfv(o, S)
        vbuf.append(t_)
    for p in range(2):
        t_, o = bfv(o, 512)
        qbuf.append(t_)
    for p in range(2):
        t_, o = bfv(o, 1024)
        Ebuf.append(t_)
    zAll, o = bfv(o, NH * 512)
    assert o <= NBF, o

    o = 0
    xnTq, o = bfv(o, DC * 512)
    w1sb = []
    for j in range(4):
        t_, o = bfv(o, DC * 2 * 128)
        w1sb.append(t_)
    w2sbs = []
    for p in range(2):
        t_, o = bfv(o, 8 * IC * 128)
        w2sbs.append(t_)
    hsb = []
    for k in range(8):
        t_, o = bfv(o, IC * 512)
        hsb.append(t_)
    p0st = []
    for p in range(2):
        t_, o = bfv(o, 4352)
        p0st.append(t_)

    o = 0
    xt = []
    for p in range(2):
        t_, o = ffv(o, D)
        xt.append(t_)
    sqscr, o = ffv(o, D)
    xnf, o = ffv(o, D)
    lncol, o = ffv(o, 32)
    A_F32_END = o

    def lnc(j, i):
        return lncol[:, (j % 2) * 16 + i:(j % 2) * 16 + i + 1]

    o = A_F32_END
    rbc_t, o = ffv(o, 1024)
    rbc2 = [rbc_t[:, 0:512], rbc_t[:, 512:1024]]
    rsum_t, o = ffv(o, 512)
    rsum = rsum_t[0:1, :]
    sTst = []
    for p in range(2):
        t_, o = ffv(o, 512)
        sTst.append(t_)
    gwf = AFL[:, A_F32_END:A_F32_END + DC * E_R]

    o = A_F32_END
    lgT_t, o = ffv(o, 512)
    lgT = lgT_t[0:E_R, :]
    lgtok, o = ffv(o, E_R)
    cp_sc, o = ffv(o, E_R)
    cp_sb, o = ffv(o, E_R)
    cp_msk, o = ffv(o, E_R)
    cp_sel, o = ffv(o, E_R)
    cp_t1, o = ffv(o, E_R)
    cp_t8, o = ffv(o, GSZ)
    cp_g1, o = ffv(o, cfg.G_GROUPS)
    cp_g2, o = ffv(o, cfg.G_GROUPS)
    cp_gm, o = ffv(o, cfg.G_GROUPS)
    cp_cols, o = ffv(o, 8)
    wdT_t, o = ffv(o, OWN)
    wdT_sb = wdT_t[0:E_R, :]

    o = 0
    routedT, o = ffv(o, DC * 512)
    wbc = []
    for p in range(2):
        t_, o = ffv(o, 512)
        wbc.append(t_)
    rtA = []
    for p in range(2):
        t_, o = ffv(o, 512)
        rtA.append(t_)
    sTtt = []
    for p in range(2):
        t_, o = ffv(o, DC * 128)
        sTtt.append(t_)
    xct_t, o = ffv(o, D)
    xct = [xct_t, xct_t]
    outst_t, o = ffv(o, D)
    outst = [outst_t, outst_t]
    wdrow_t, o = ffv(o, 1024)
    wdrow = [wdrow_t[0:1, 0:512], wdrow_t[0:1, 512:1024]]
    assert o <= NF32, o

    pA = pst("pA", [128, 1024])
    pB = pst("pB", [128, 1024])
    pC = pst("pC", [128, 1024])
    pD = pst("pD", [128, 512])
    pE = pst("pE", [128, 512])
    vbfA = pD[:].bitcast(BF16)
    vbfB = pE[:].bitcast(BF16)
    vbfC = pB[:, 512:1024].bitcast(BF16)

    DSEM = ["s_const", "s_gw", "s_xld0", "s_xld1", "s_wkv0", "s_wkv1", "s_wkv2", "s_wkv3",
            "s_wq0", "s_wq1",
            "s_pi0", "s_pi1", "s_po0", "s_po1",
            "s_kv0", "s_kv1", "s_wo0", "s_wo1",
            "s_spk0", "s_spk1", "s_spv0", "s_spv1", "s_spq0", "s_spq1",
            "s_spt0", "s_spt1",
            "s_w1l0", "s_w1l1", "s_w1l2", "s_w1l3", "s_w2l0", "s_w2l1",
            "s_xnq", "s_sTl0", "s_sTl1", "s_xc0", "s_xc1",
            "s_out", "s_wd0", "s_wd1", "s_xnsp", "s_xnsp1", "s_wdsp"]
    CSEM = ["t_a", "v_a", "c_a", "t_b", "v_b", "c_b", "t_c", "v_c", "c_c",
            "t_p", "v_p", "c_p"]
    sems = {n: mksem(n) for n in DSEM + CSEM}
    E = Emitter(sems)

    def wdma(eng, ch, key):
        if not E.pass1:
            if key not in E.marks:
                E._miss(key)
                return
            chm, v = E.marks[key]
            if v > 0:
                eng.wait_ge(sems[chm], v)

    p0_slices = []
    for h in range(NH):
        p0_slices.append((("wkv", h, 0), 2048))
        p0_slices.append((("wkv", h, 1), 2048))
    for h in range(NH):
        p0_slices.append((("wq", h), 2048))
    for dc in range(DC):
        p0_slices.append((("wo", dc), NH * 128))
    for e in range(NE):
        for ih in range(2):
            p0_slices.append((("w1", e, ih), 4096))
    ncol2 = NE * IC * 128
    NSL2 = (ncol2 + 4351) // 4352
    for dc in range(DC):
        for s_ in range(NSL2):
            c0 = s_ * 4352
            c1 = min(ncol2, c0 + 4352)
            p0_slices.append((("w2", dc, s_, c0, c1), c1 - c0))

    def p0_aps(key):
        if key[0] == "wkv":
            _, h, j = key
            src = [wk_in, wv_in][j][h]
            dst = wkvb_sp[h, j]
        elif key[0] == "wq":
            src = wq_in[key[1]]
            dst = wqb_sp[key[1]]
        elif key[0] == "wo":
            src = wo_in[key[1]]
            dst = wob_sp[key[1]]
        elif key[0] == "w1":
            _, e, ih = key
            src = w1f_in[e, ih]
            dst = w1b_sp[e, ih]
        else:
            _, dc, s_, c0, c1 = key
            src = w2f_in[dc, :, c0:c1]
            dst = w2b_sp[dc, :, c0:c1]
        return dst, src

    def ext_of(qc):
        return 1024 * (qc + 1)

    def body_sync(sync):
        for i in range(8):
            E.dinc("s_const", sync.dma_start(out=bmk[i][:], in_=bmask_in[i]))
        E.dinc("s_const", sync.dma_start(out=idb[:], in_=idb_in))
        E.dinc("s_const", sync.dma_start(out=idf[:], in_=idf_in))
        E.dinc("s_const", sync.dma_start(out=onesb[:], in_=onesb_in))
        E.dinc("s_const", sync.dma_start(out=onesf[:], in_=onesf_in))
        E.dinc("s_const", sync.dma_start(out=ones8[:], in_=ones8_in))
        E.dinc("s_const", sync.dma_start(out=gbrow[:], in_=gb_in))
        E.dinc("s_gw", sync.dma_start(out=gwf[:], in_=gwT_in))

        for mb in range(MB):
            for jj in range(4):
                j = mb * 4 + jj
                E.wait(sync, ("a_tile_done", j - 2))
                E.wait(sync, ("a_xnb_s", j - 2))
                ch = f"s_xld{j % 2}"
                E.dinc(ch, sync.dma_start(
                    out=xt[j % 2][:], in_=x_in[j * 128:(j + 1) * 128, :]))
                if E.pass1:
                    E.marks[("xld", j)] = (ch, E.dma_counts[ch])
            for h in range(NH):
                gi = mb * NH + h
                E.wait(sync, ("a_vmm", gi - 4))
                wdma(sync, None, ("p0_wkv", h, 0))
                wdma(sync, None, ("p0_wkv", h, 1))
                ch = f"s_wkv{gi % 4}"
                E.dinc(ch, sync.dma_start(
                    out=wkv_t[gi % 4][:].rearrange("p (j d) -> p j d", j=2),
                    in_=wkvb_sp[h].rearrange("j p d -> p j d")))
                if E.pass1:
                    E.marks[("wkvl", gi)] = (ch, E.dma_counts[ch])
                if gi >= 1:
                    pg = gi - 1
                    pmb, ph_ = pg // NH, pg % NH
                    for wi, (which, spt) in enumerate((("k", kT_sp), ("v", v_sp))):
                        E.wait(sync, ("a_evac", pg, which))
                        st_idx = wi * 2 + pg % 2
                        ch = f"s_sp{which}{pg % 2}"
                        E.dinc(ch, sync.dma_start(
                            out=spt[ph_][:, pmb * 512:(pmb + 1) * 512],
                            in_=kvst[st_idx][:]))
                        E.mark_d(("a_spill", pg, which), ch)
        pg = MB * NH - 1
        pmb, ph_ = pg // NH, pg % NH
        for wi, (which, spt) in enumerate((("k", kT_sp), ("v", v_sp))):
            E.wait(sync, ("a_evac", pg, which))
            st_idx = wi * 2 + pg % 2
            ch = f"s_sp{which}{pg % 2}"
            E.dinc(ch, sync.dma_start(
                out=spt[ph_][:, pmb * 512:(pmb + 1) * 512], in_=kvst[st_idx][:]))
            E.mark_d(("a_spill", pg, which), ch)
        for which in ("k", "v"):
            for par in range(2):
                E.mark_d(("a_spfin", which, par), f"s_sp{which}{par}")

        E.wait(sync, ("a_xnb_s", NTI - 1))
        E.wait(sync, ("a_xnb_s", NTI - 2))
        for mb in range(OMB):
            for jj in range(4):
                j = mb * 4 + jj
                E.wait(sync, ("cp_tile_done", j - 2))
                E.wait(sync, ("cp_xnb_s", j - 2))
                ch = f"s_xld{j % 2}"
                E.dinc(ch, sync.dma_start(
                    out=xt[j % 2][:], in_=xo_in[j * 128:(j + 1) * 128, :]))
                if E.pass1:
                    E.marks[("cp_xld", j)] = (ch, E.dma_counts[ch])
            E.wait(sync, ("cp_stage_hl", mb))
            ch = "s_xnsp" if mb % 2 == 0 else "s_xnsp1"
            E.dinc(ch, sync.dma_start(
                out=xnTo_sp.rearrange("p (c s) -> p c s", c=DC)[:, :, mb * 512:(mb + 1) * 512],
                in_=cp_stg[mb % 2][:].rearrange("p (c s) -> p c s", c=DC)))
            E.mark_d(("cp_xnsp", mb), ch)
            for h in range(NH):
                qi = mb * NH + h
                E.wait(sync, ("a2_qmm", qi - 2))
                wdma(sync, None, ("p0_wqA", h, 0))
                wdma(sync, None, ("p0_wqA", h, 1))
                ch = f"s_wq{qi % 2}"
                E.dinc(ch, sync.dma_start(out=wq_t[qi % 2][:], in_=wqb_sp[h]))
                if E.pass1:
                    E.marks[("wql", qi)] = (ch, E.dma_counts[ch])
                if qi >= 1:
                    pq = qi - 1
                    pmbq, phq = pq // NH, pq % NH
                    E.wait(sync, ("a2_qevac", pq))
                    ch = f"s_spq{pq % 2}"
                    E.dinc(ch, sync.dma_start(
                        out=q_sp[phq][:, pmbq * 512:(pmbq + 1) * 512],
                        in_=qst[pq % 2][:]))
                    E.mark_d(("a2_qspill", pq), ch)
        pq = OMB * NH - 1
        pmbq, phq = pq // NH, pq % NH
        E.wait(sync, ("a2_qevac", pq))
        ch = f"s_spq{pq % 2}"
        E.dinc(ch, sync.dma_start(
            out=q_sp[phq][:, pmbq * 512:(pmbq + 1) * 512], in_=qst[pq % 2][:]))
        E.mark_d(("a2_qspill", pq), ch)
        for par in range(2):
            E.mark_d(("a2_qspfin", par), f"s_spq{par}")
        E.wait(sync, "cp_wd_done")
        E.dinc("s_wdsp", sync.dma_start(out=wdT_sp, in_=wdT_sb))
        E.mark_d("cp_wdsp", "s_wdsp")
        for chx in ("s_xnsp", "s_xnsp1", "s_wdsp", "s_xld0", "s_xld1"):
            E.mark_d(("cpdma", chx), chx)

        E.wait(sync, "fin_t_a2")
        E.wait(sync, "fin_v_a2")
        E.wait(sync, "fin_c_a2")
        for chx in ("s_xnsp", "s_xnsp1"):
            wdma(sync, None, ("cpdma", chx))
        wdma(sync, None, ("p0_wo", 0))
        wdma(sync, None, ("p0_wo", 1))
        gi = 0
        for qc in range(OC):
            ext = ext_of(qc)
            for h in range(NH):
                E.wait(sync, ("b_kv_done", gi - 2))
                if gi == 0:
                    for which in ("k", "v"):
                        for par in range(2):
                            wdma(sync, None, ("a_spfin", which, par))
                    for par in range(2):
                        wdma(sync, None, ("a2_qspfin", par))
                ch = f"s_kv{gi % 2}"
                E.dinc(ch, sync.dma_start(out=kbuf[gi % 2][:, 0:ext],
                                          in_=kT_sp[h][:, 0:ext]))
                E.dinc(ch, sync.dma_start(out=vbuf[gi % 2][:, 0:ext],
                                          in_=v_sp[h][:, 0:ext]))
                E.dinc(ch, sync.dma_start(
                    out=qbuf[gi % 2][:], in_=q_sp[h][:, qc * 512:(qc + 1) * 512]))
                if E.pass1:
                    E.marks[("b_kvl", gi)] = (ch, E.dma_counts[ch])
                gi += 1
            for dc in range(DC):
                gwo = qc * DC + dc - 2
                E.wait(sync, ("b_wo", gwo // DC, gwo % DC) if gwo >= 0 else ("x", -1))
                ch = f"s_wo{dc % 2}"
                E.dinc(ch, sync.dma_start(out=wobd[dc % 2][:], in_=wob_sp[dc]))
                if E.pass1:
                    E.marks[("b_wol", qc, dc)] = (ch, E.dma_counts[ch])
                E.wait(sync, ("b_sT_evac", qc, dc))
                col = dc * OWN + qc * 512
                ch = f"s_spt{dc % 2}"
                E.dinc(ch, sync.dma_start(out=sT_sp[:, col:col + 512],
                                          in_=sTst[dc % 2][:]))
                E.mark_d(("b_sT_sp", qc, dc), ch)
        for ch in ("s_kv0", "s_kv1", "s_spt0", "s_spt1", "s_wo0", "s_wo1"):
            E.mark_d(("bdma", ch), ch)

        E.wait(sync, "fin_t_b")
        E.wait(sync, "fin_v_b")
        E.wait(sync, "fin_c_b")
        for tq in range(TQ):
            E.wait(sync, ("c_p1_done", tq - 1))
            E.wait(sync, ("cp_xnsp", OMB - 1))
            if OMB >= 2:
                E.wait(sync, ("cp_xnsp", OMB - 2))
            E.dinc("s_xnq", sync.dma_start(
                out=xnTq[:].rearrange("p (c s) -> p c s", c=DC),
                in_=xnTo_sp.rearrange("p (c s) -> p c s", c=DC)[:, :, tq * 512:(tq + 1) * 512]))
            if E.pass1:
                E.marks[("c_xnq", tq)] = ("s_xnq", E.dma_counts["s_xnq"])
            for e in range(E_R):
                E.wait(sync, ("c_wbcmm", tq, e - 2))
                E.wait(sync, "cp_wdsp")
                ch = f"s_wd{e % 2}"
                E.dinc(ch, sync.dma_start(
                    out=wdrow[e % 2], in_=wdT_sp[e:e + 1, tq * 512:(tq + 1) * 512]))
                if E.pass1:
                    E.marks[("c_wdl", tq, e)] = (ch, E.dma_counts[ch])
            for tt in range(4):
                gtt3 = tq * 4 + tt - 2
                E.wait(sync, ("c_ep_mul", gtt3 // 4, gtt3 % 4) if gtt3 >= 0 else ("x", -1))
                for chx in ("s_spt0", "s_spt1"):
                    wdma(sync, None, ("bdma", chx))
                ch = f"s_sTl{tt % 2}"
                E.dinc(ch, sync.dma_start(
                    out=sTtt[tt % 2][:].rearrange("p (c s) -> p c s", c=DC),
                    in_=sT_sp.rearrange("p (c s) -> p c s", c=DC)[:, :, tq * 512 + tt * 128:tq * 512 + (tt + 1) * 128]))
                if E.pass1:
                    E.marks[("c_sTl", tq, tt)] = (ch, E.dma_counts[ch])
                gtt2 = tq * 4 + tt - 1
                E.wait(sync, ("c_ep_addlast", gtt2 // 4, gtt2 % 4) if gtt2 >= 0 else ("x", -1))
                ch = f"s_xc{tt % 2}"
                E.dinc(ch, sync.dma_start(
                    out=xct[tt % 2][:],
                    in_=xo_in[(tq * 4 + tt) * 128:(tq * 4 + tt + 1) * 128, :]))
                if E.pass1:
                    E.marks[("c_xc", tq, tt)] = (ch, E.dma_counts[ch])
                E.wait(sync, ("c_ep_out", tq, tt))
                E.dinc("s_out", sync.dma_start(
                    out=out_t[(tq * 4 + tt) * 128:(tq * 4 + tt + 1) * 128, :],
                    in_=outst[tt % 2][:]))
                E.mark_d(("c_out", tq, tt), "s_out")
        wdma(sync, None, ("c_out", TQ - 1, 3))

    def body_gpsimd(gp):
        for j, (key, n) in enumerate(p0_slices):
            par = j % 2
            dst, src = p0_aps(key)
            wdma(gp, None, ("p0_out", j - 2))
            E.dinc(f"s_pi{par}", gp.dma_start(out=p0st[par][:, 0:n], in_=src))
            E.wait_ge(gp, f"s_pi{par}", E.dma_counts[f"s_pi{par}"])
            E.dinc(f"s_po{par}", gp.dma_start(out=dst, in_=p0st[par][:, 0:n]))
            E.mark_d(("p0_out", j), f"s_po{par}")
            if key[0] == "wkv" and key[2] == 1:
                E.mark_d(("p0_wkv", key[1], 0), "s_po0")
                E.mark_d(("p0_wkv", key[1], 1), "s_po1")
            if key[0] == "wq":
                E.mark_d(("p0_wqA", key[1], 0), "s_po0")
                E.mark_d(("p0_wqA", key[1], 1), "s_po1")
            if key[0] == "wo" and key[1] == DC - 1:
                E.mark_d(("p0_wo", 0), "s_po0")
                E.mark_d(("p0_wo", 1), "s_po1")
        E.mark_d(("p0_all", 0), "s_po0")
        E.mark_d(("p0_all", 1), "s_po1")

        E.wait(gp, "fin_t_b")
        E.wait(gp, "fin_v_b")
        E.wait(gp, "fin_c_b")
        for ch in ("s_xnsp", "s_xnsp1", "s_wdsp"):
            wdma(gp, None, ("cpdma", ch))
        for ch in ("s_kv0", "s_kv1", "s_wo0", "s_wo1"):
            wdma(gp, None, ("bdma", ch))
        for par in range(2):
            wdma(gp, None, ("a2_qspfin", par))
        for tq in range(TQ):
            for grp_i, grp in enumerate(cfg.groups):
                for e in grp:
                    for ih in range(2):
                        jslot = (e % 2) * 2 + ih
                        E.wait(gp, ("c_w1_used", tq, e - 2, ih))
                        if tq == 0 and grp_i == 0 and e == grp[0] and ih == 0:
                            wdma(gp, None, ("p0_all", 0))
                            wdma(gp, None, ("p0_all", 1))
                        ch = f"s_w1l{jslot}"
                        E.dinc(ch, gp.dma_start(out=w1sb[jslot][:], in_=w1b_sp[e, ih]))
                        if E.pass1:
                            E.marks[("c_w1l", tq, e, ih)] = (ch, E.dma_counts[ch])
                for dc in range(DC):
                    E.wait(gp, ("c_w2_used", tq, grp_i, dc - 2))
                    e0, e1 = grp[0], grp[-1] + 1
                    ch = f"s_w2l{dc % 2}"
                    E.dinc(ch, gp.dma_start(
                        out=w2sbs[dc % 2][:, 0:(e1 - e0) * IC * 128],
                        in_=w2b_sp[dc, :, e0 * IC * 128:e1 * IC * 128]))
                    if E.pass1:
                        E.marks[("c_w2l", tq, grp_i, dc)] = (ch, E.dma_counts[ch])

    def body_tensor(t):
        E.wait_ge(t, "s_const", 224)
        E.wait_ge(t, "s_gw", 16)
        ins = t.matmul(pC[:, 0:E_R], onesf[:], gbrow[:], start=True, stop=True)
        E.inc("t_a", ins)
        E.mark("gbb_mm", "t_a")

        for mb in range(MB):
            for jj in range(4):
                j = mb * 4 + jj
                E.wait(t, ("a_xnb", j))
                E.wait(t, ("a_trevac", j - 1))
                for dc in range(DC):
                    dst = (vbfA if dc < 8 else vbfB)[:, (dc % 8) * 128:(dc % 8 + 1) * 128]
                    ins = t.transpose(dst, xnb[j % 2][:, dc * 128:(dc + 1) * 128], idb[:])
                    if dc == 7 or dc == 15:
                        E.inc("t_a", ins)
                        E.mark(("a_tr", j, dc // 8), "t_a")
                E.mark(("a_tile_done", j), "t_a")
            for h in range(NH):
                gi = mb * NH + h
                if gi > 0:
                    E.wait(t, ("a_vtevac", gi - 1))
                    E.wait(t, ("a_vtrevac", gi - 2))
                    for i4 in range(4):
                        ins = t.transpose(vbfC[:, i4 * 128:(i4 + 1) * 128],
                                          vtst[(gi - 1) % 2][:, i4 * 128:(i4 + 1) * 128],
                                          idb[:])
                    E.inc("t_a", ins)
                    E.mark(("a_vtr", gi - 1), "t_a")
                wdma(t, None, ("wkvl", gi))
                E.wait(t, ("a_stage", mb))
                wslc = wkv_t[gi % 4]
                for wi, which in enumerate(("k", "v")):
                    dstp = [pA[:, 0:512], pA[:, 512:1024]][wi]
                    E.wait(t, ("a_evacd", gi - 1, which))
                    for dc in range(DC):
                        ins = t.matmul(dstp,
                                       wslc[:, wi * D + dc * 128:wi * D + dc * 128 + 128],
                                       stg[mb % 2][:, dc * 512:(dc + 1) * 512],
                                       start=(dc == 0), stop=(dc == DC - 1))
                    E.inc("t_a", ins)
                    E.mark(("a_mm", gi, which), "t_a")
                E.mark(("a_vmm", gi), "t_a")
            E.mark(("a_blk_done", mb), "t_a")
        gi = MB * NH
        E.wait(t, ("a_vtevac", gi - 1))
        E.wait(t, ("a_vtrevac", gi - 2))
        for i4 in range(4):
            ins = t.transpose(vbfC[:, i4 * 128:(i4 + 1) * 128],
                              vtst[(gi - 1) % 2][:, i4 * 128:(i4 + 1) * 128], idb[:])
        E.inc("t_a", ins)
        E.mark(("a_vtr", gi - 1), "t_a")
        E.mark("fin_t_a", "t_a")

        E.wait(t, "fin_v_a")
        E.wait(t, "fin_c_a")
        for mb in range(OMB):
            for jj in range(4):
                j = mb * 4 + jj
                E.wait(t, ("cp_xnb", j))
                E.wait(t, ("cp_trevacL", j - 1))
                for dc in range(DC):
                    dst = (vbfA if dc < 8 else vbfB)[:, (dc % 8) * 128:(dc % 8 + 1) * 128]
                    ins = t.transpose(dst, cp_hi[j % 2][:, dc * 128:(dc + 1) * 128], idb[:])
                    if dc == 7 or dc == 15:
                        E.inc("t_p", ins)
                        E.mark(("cp_trh", j, dc // 8), "t_p")
                E.wait(t, ("cp_lo", j))
                E.wait(t, ("cp_trevacH", j))
                for dc in range(DC):
                    dst = (vbfA if dc < 8 else vbfB)[:, (dc % 8) * 128:(dc % 8 + 1) * 128]
                    ins = t.transpose(dst, cp_lo[j % 2][:, dc * 128:(dc + 1) * 128], idb[:])
                    if dc == 7 or dc == 15:
                        E.inc("t_p", ins)
                        E.mark(("cp_trl", j, dc // 8), "t_p")
                E.mark(("cp_tile_done", j), "t_p")
            E.wait(t, ("cp_stage_hl", mb))
            for h in range(NH):
                qi = mb * NH + h
                wdma(t, None, ("wql", qi))
                E.wait(t, ("a2_qevacd", qi - 1))
                for dc in range(DC):
                    ins = t.matmul(pA[:, 0:512],
                                   wq_t[qi % 2][:, dc * 128:(dc + 1) * 128],
                                   cp_stg[mb % 2][:, dc * 512:(dc + 1) * 512],
                                   start=(dc == 0), stop=(dc == DC - 1))
                E.inc("t_p", ins)
                E.mark(("a2_qmmD", qi), "t_p")
                E.mark(("a2_qmm", qi), "t_p")
            E.wait(t, ("cp_lgevac", mb - 1))
            E.wait(t, "gw_split")
            for dc in range(DC):
                t.matmul(pC[0:E_R, 0:512], gwhi[:, dc * E_R:(dc + 1) * E_R],
                         cp_stg[mb % 2][:, dc * 512:(dc + 1) * 512],
                         start=(dc == 0), stop=False)
                t.matmul(pC[0:E_R, 0:512], gwlo[:, dc * E_R:(dc + 1) * E_R],
                         cp_stg[mb % 2][:, dc * 512:(dc + 1) * 512],
                         start=False, stop=False)
                ins = t.matmul(pC[0:E_R, 0:512], gwhi[:, dc * E_R:(dc + 1) * E_R],
                               cp_loT[:, dc * 512:(dc + 1) * 512],
                               start=False, stop=(dc == DC - 1))
            E.inc("t_p", ins)
            E.mark(("cp_lg", mb), "t_p")
            for jj in range(4):
                j = mb * 4 + jj
                E.wait(t, ("cp_lgTevac", mb))
                E.wait(t, ("cp_lgtokevac", j - 1))
                ins = t.transpose(pB[0:128, 0:E_R], lgT[:, jj * 128:(jj + 1) * 128],
                                  idf[0:E_R, 0:E_R])
                E.inc("t_p", ins)
                E.mark(("cp_lgtr", j), "t_p")
                E.wait(t, ("cp_wd", j))
                E.wait(t, ("cp_wdevac", j - 1))
                ins = t.transpose(pB[0:E_R, 512:640], cp_t1[:, 0:E_R], idf[:])
                E.inc("t_p", ins)
                E.mark(("cp_wdtr", j), "t_p")
        E.mark("fin_t_a2", "t_p")
        E.mark("fin_t_cp", "t_p")

        E.wait(t, "fin_v_a2")
        E.wait(t, "fin_c_a2")
        gi = 0
        for qc in range(OC):
            ext = ext_of(qc)
            npairs = ext // 256
            for h in range(NH):
                wdma(t, None, ("b_kvl", gi))
                E.wait(t, ("b_zn", qc, h - 2))
                E.wait(t, ("b_recipd", qc, h - 1))
                if h == 0:
                    E.wait(t, ("b_woused", qc - 1))
                zbank = [pC[:, 0:512], pC[:, 512:1024]][h % 2]
                psc = [pA, pB]
                for jp in range(npairs):
                    bank = psc[jp % 2]
                    E.wait(t, ("b_expd", qc, h, jp - 2))
                    for mi, mt in enumerate((2 * jp, 2 * jp + 1)):
                        ins = t.matmul(bank[:, mi * 512:(mi + 1) * 512],
                                       kbuf[gi % 2][:, mt * 128:(mt + 1) * 128],
                                       qbuf[gi % 2][:], start=True, stop=True)
                    E.inc("t_b", ins)
                    E.mark(("b_sc", qc, h, jp), "t_b")
                    E.wait(t, ("b_expd", qc, h, jp))
                    for mi, mt in enumerate((2 * jp, 2 * jp + 1)):
                        st_ = (jp == 0 and mi == 0)
                        sp_ = (jp == npairs - 1 and mi == 1)
                        ins = t.matmul(zbank,
                                       vbuf[gi % 2][:, mt * 128:(mt + 1) * 128],
                                       Ebuf[jp % 2][:, mi * 512:(mi + 1) * 512],
                                       start=st_, stop=sp_)
                        ins2 = t.matmul(pD[0:1, 0:512], onesb[:],
                                        Ebuf[jp % 2][:, mi * 512:(mi + 1) * 512],
                                        start=st_, stop=sp_)
                    E.inc("t_b", ins2)
                    E.mark(("b_pv", qc, h, jp), "t_b")
                E.mark(("b_kv_done", gi), "t_b")
                E.wait(t, ("b_recip", qc, h))
                E.wait(t, ("b_rbccopy", qc, h - 1))
                ins = t.matmul(pE[:, 0:512], onesf[:], rsum, start=True, stop=True)
                E.inc("t_b", ins)
                E.mark(("b_bc", qc, h), "t_b")
                gi += 1
            E.wait(t, ("b_zn", qc, NH - 1))
            for dc in range(DC):
                bank = pA[:, 0:512]
                E.wait(t, ("b_sT_evac", qc, dc - 1))
                wdma(t, None, ("b_wol", qc, dc))
                for h in range(NH):
                    ins = t.matmul(bank, wobd[dc % 2][:, h * 128:(h + 1) * 128],
                                   zAll[:, h * 512:(h + 1) * 512],
                                   start=(h == 0), stop=(h == NH - 1))
                E.inc("t_b", ins)
                E.mark(("b_wo", qc, dc), "t_b")
        E.mark("fin_t_b", "t_b")

        E.wait(t, "fin_v_b")
        E.wait(t, "fin_c_b")
        for ch in ("s_xnsp", "s_xnsp1", "s_wdsp", "s_xld0", "s_xld1"):
            wdma(t, None, ("cpdma", ch))
        for ch in ("s_kv0", "s_kv1", "s_wo0", "s_wo1"):
            wdma(t, None, ("bdma", ch))
        for par in range(2):
            wdma(t, None, ("a2_qspfin", par))
        np1 = 0
        np2 = 0
        for tq in range(TQ):
            wdma(t, None, ("c_xnq", tq))
            for grp_i, grp in enumerate(cfg.groups):
                for e in grp:
                    routed = e < E_R
                    if routed:
                        wdma(t, None, ("c_wdl", tq, e))
                        E.wait(t, ("c_wbccopy", tq, e - 1))
                        ins = t.matmul(pC[:, 0:512], onesf[:], wdrow[e % 2],
                                       start=True, stop=True)
                        E.inc("t_c", ins)
                        E.mark(("c_wbcmm", tq, e), "t_c")
                    for i in range(IC):
                        ih = i // 2
                        jslot = (e % 2) * 2 + ih
                        wdma(t, None, ("c_w1l", tq, e, ih))
                        E.wait(t, ("c_h_evac", np1 - 2))
                        bank = [pA[:, 0:512], pA[:, 512:1024]][np1 % 2]
                        for dc in range(DC):
                            col = dc * 2 * 128 + (i % 2) * 128
                            ins = t.matmul(bank, w1sb[jslot][:, col:col + 128],
                                           xnTq[:, dc * 512:(dc + 1) * 512],
                                           start=(dc == 0), stop=(dc == DC - 1))
                        E.inc("t_c", ins)
                        E.mark(("c_p1", np1), "t_c")
                        if i == IC - 1:
                            E.mark(("c_w1_used", tq, e, 0), "t_c")
                            E.mark(("c_w1_used", tq, e, 1), "t_c")
                        np1 += 1
                for dc in range(DC):
                    wdma(t, None, ("c_w2l", tq, grp_i, dc))
                    E.wait(t, ("c_h_evac", np1 - 1))
                    E.wait(t, ("c_radd", np2 - 2))
                    bank = [pB[:, 0:512], pB[:, 512:1024]][np2 % 2]
                    for ki, e in enumerate(grp):
                        for i in range(IC):
                            col = (ki * IC + i) * 128
                            ins = t.matmul(bank, w2sbs[dc % 2][:, col:col + 128],
                                           hsb[ki][:, i * 512:(i + 1) * 512],
                                           start=(ki == 0 and i == 0),
                                           stop=(ki == len(grp) - 1 and i == IC - 1))
                    E.inc("t_c", ins)
                    E.mark(("c_p2", np2), "t_c")
                    E.mark(("c_w2_used", tq, grp_i, dc), "t_c")
                    np2 += 1
            E.mark(("c_p1_done", tq), "t_c")
            for tt in range(4):
                E.wait(t, ("c_ep_mul", tq, tt))
                for dcg in range(4):
                    E.wait(t, ("c_ep_add", tq, tt, dcg - 2))
                    bank = pD if dcg % 2 == 0 else pE
                    for q4 in range(4):
                        dc = dcg * 4 + q4
                        ins = t.transpose(bank[:, q4 * 128:(q4 + 1) * 128],
                                          sTtt[tt % 2][:, dc * 128:(dc + 1) * 128],
                                          idf[:])
                    E.inc("t_c", ins)
                    E.mark(("c_ep_tr", tq, tt, dcg), "t_c")

    def body_scalar(s):
        def ln_tile(j, ch, ldkey, tdkey, out_bf, out_f32, pfx):
            wdma(s, None, (ldkey, j))
            E.wait(s, (pfx + "_statsdone", j - 1))
            ins = s.activation(sqscr[:], xt[j % 2][:], AF.Square,
                               accum_out=lnc(j, 1))
            E.inc(ch, ins)
            E.mark((pfx + "_sq", j), ch)
            E.wait(s, (pfx + "_stats", j))
            ins = s.activation(lnc(j, 4), lnc(j, 3), AF.Sqrt)
            E.inc(ch, ins)
            E.mark((pfx + "_sqrt", j), ch)
            E.wait(s, (pfx + "_rnb", j))
            E.wait(s, (tdkey, j - 1))
            if out_f32 is not None:
                ins = s.activation(out_f32[:], xt[j % 2][:], AF.Identity,
                                   bias=lnc(j, 6), scale=lnc(j, 5))
                E.inc(ch, ins)
                E.mark((pfx + "_xnf", j), ch)
                E.wait_ge(s, ch, E.counts[ch])
                ins = s.copy(out_bf[:], out_f32[:])
                E.inc(ch, ins)
            else:
                ins = s.activation(out_bf[:], xt[j % 2][:], AF.Identity,
                                   bias=lnc(j, 6), scale=lnc(j, 5))
                E.inc(ch, ins)
            E.mark((pfx + "_xnb", j), ch)
            E.mark((pfx + "_xnb_s", j), ch)

        for j in range(NTI):
            ln_tile(j, "c_a", "xld", "a_tile_done", xnb[j % 2], None, "a")
            E.mark(("a_xnb", j), "c_a")
            E.mark(("a_xnb_s", j), "c_a")
        E.mark("fin_c_a", "c_a")

        E.wait(s, "fin_t_a")
        E.wait(s, "fin_v_a")
        for mb in range(OMB):
            for jj in range(4):
                j = mb * 4 + jj
                ln_tile(j, "c_p", "cp_xld", "cp_hi_done", cp_hi[j % 2], xnf, "cp")
                E.mark(("cp_xnb", j), "c_p")
                E.mark(("cp_xnb_s", j), "c_p")
            for jj in range(4):
                j = mb * 4 + jj
                E.wait(s, ("cp_lgtokevac", j))
                ins = s.activation(cp_sc[:], lgtok[:], AF.Sigmoid)
                E.inc("c_p", ins)
                E.mark(("cp_sig", j), "c_p")
        E.mark("fin_c_a2", "c_p")
        E.mark("fin_c_cp", "c_p")

        E.wait(s, "fin_t_a2")
        E.wait(s, "fin_v_a2")
        for chx in ("s_xnsp", "s_xnsp1", "s_wdsp"):
            wdma(s, None, ("cpdma", chx))
        for qc in range(OC):
            ext = ext_of(qc)
            npairs = ext // 256
            for h in range(NH):
                for jp in range(npairs):
                    if jp >= npairs - 4:
                        E.wait(s, ("b_mask", qc, h, jp))
                    else:
                        E.wait(s, ("b_sc", qc, h, jp))
                    E.wait(s, ("b_pv", qc, h, jp - 2))
                    bank = [pA, pB][jp % 2]
                    ins = s.activation(Ebuf[jp % 2][:], bank[:], AF.Exp, scale=SCALE)
                    E.inc("c_b", ins)
                    E.mark(("b_expd", qc, h, jp), "c_b")
                E.wait(s, ("b_bc", qc, h))
                E.wait(s, ("b_zn", qc, h - 2))
                ins = s.copy(rbc2[h % 2][:], pE[:, 0:512])
                E.inc("c_b", ins)
                E.mark(("b_rbccopy", qc, h), "c_b")
        E.mark("fin_c_b", "c_b")

        E.wait(s, "fin_t_cp")
        E.wait(s, "fin_v_cp")
        for ch in ("s_xnsp", "s_xnsp1", "s_wdsp", "s_xld0", "s_xld1"):
            wdma(s, None, ("cpdma", ch))
        for ch in ("s_spt0", "s_spt1"):
            wdma(s, None, ("bdma", ch))
        for par in range(2):
            wdma(s, None, ("a2_qspfin", par))
        for tq in range(TQ):
            for e in range(E_R):
                E.wait(s, ("c_wbcmm", tq, e))
                E.wait(s, ("c_wbc_used", tq, e - 2))
                ins = s.copy(wbc[e % 2][:], pC[:, 0:512])
                E.inc("c_c", ins)
                E.mark(("c_wbccopy", tq, e), "c_c")
            E.wait(s, ("c_radd_all", tq))
            ins = s.activation(routedT[:], routedT[:], AF.Silu)
            E.inc("c_c", ins)
            E.mark(("c_silu", tq), "c_c")

    def body_vector(v):
        E.wait_ge(v, "s_gw", 16)
        ins = v.tensor_copy(gwhi[:], gwf[:])
        c = E.inc("v_a", ins)
        E.wait_ge(v, "v_a", c)
        ins = v.tensor_tensor(gwlo[:], gwf[:], gwhi[:], ALU.subtract)
        E.inc("v_a", ins)
        E.mark("gw_split", "v_a")
        E.wait(v, "gbb_mm")
        ins = v.tensor_copy(gbbc[:], pC[:, 0:E_R])
        E.inc("v_a", ins)
        E.mark("gbb_done", "v_a")

        def ln_stats(j, ch, ldkey, pfx):
            wdma(v, None, (ldkey, j))
            E.wait(v, (pfx + "_xnb_s", j - 2))
            ins = v.reduce_sum(lnc(j, 0), xt[j % 2][:], axis=AXX)
            c = E.inc(ch, ins)
            E.wait(v, (pfx + "_sq", j))
            E.wait_ge(v, ch, c)
            ins = v.tensor_scalar(lnc(j, 2), lnc(j, 0), 1.0 / D, None, ALU.mult)
            c = E.inc(ch, ins)
            ins = v.tensor_scalar(lnc(j, 7), lnc(j, 1), 1.0 / D, None, ALU.mult)
            c = E.inc(ch, ins)
            E.wait_ge(v, ch, c)
            ins = v.tensor_scalar(lnc(j, 3), lnc(j, 2), lnc(j, 2), -1.0,
                                  ALU.mult, ALU.mult)
            c = E.inc(ch, ins)
            E.wait_ge(v, ch, c)
            ins = v.tensor_tensor(lnc(j, 3), lnc(j, 3), lnc(j, 7), ALU.add)
            c = E.inc(ch, ins)
            E.wait_ge(v, ch, c)
            ins = v.tensor_scalar(lnc(j, 3), lnc(j, 3), 1e-5, None, ALU.add)
            E.inc(ch, ins)
            E.mark((pfx + "_stats", j), ch)
            E.wait(v, (pfx + "_sqrt", j))
            ins = v.reciprocal(lnc(j, 5), lnc(j, 4))
            c = E.inc(ch, ins)
            E.wait_ge(v, ch, c)
            ins = v.tensor_scalar(lnc(j, 6), lnc(j, 2), lnc(j, 5), -1.0,
                                  ALU.mult, ALU.mult)
            E.inc(ch, ins)
            E.mark((pfx + "_rnb", j), ch)
            E.mark((pfx + "_statsdone", j), ch)

        for mb in range(MB):
            for jj in range(4):
                j = mb * 4 + jj
                ln_stats(j, "v_a", "xld", "a")
                for half in range(2):
                    E.wait(v, ("a_tr", j, half))
                    E.wait(v, ("a_blk_done", mb - 2))
                    src = [vbfA, vbfB][half]
                    dst = stg[mb % 2][:].rearrange("p (c s) -> p c s", c=DC)[
                        :, half * 8:(half + 1) * 8, jj * 128:(jj + 1) * 128]
                    ins = v.tensor_copy(dst, src[:].rearrange("p (c s) -> p c s", c=8))
                    E.inc("v_a", ins)
                E.mark(("a_trevac", j), "v_a")
            E.mark(("a_stage", mb), "v_a")
            for h in range(NH):
                gi = mb * NH + h
                for wi, which in enumerate(("k", "v")):
                    E.wait(v, ("a_mm", gi, which))
                    src = [pA[:, 0:512], pA[:, 512:1024]][wi]
                    if which == "v":
                        ins = v.tensor_copy(vtst[gi % 2][:], src)
                        E.inc("v_a", ins)
                        E.mark(("a_vtevac", gi), "v_a")
                    else:
                        wdma(v, None, ("a_spill", gi - 2, which))
                        st_idx = wi * 2 + gi % 2
                        ins = v.tensor_copy(kvst[st_idx][:], src)
                        E.inc("v_a", ins)
                        E.mark(("a_evac", gi, which), "v_a")
                    E.mark(("a_evacd", gi, which), "v_a")
                if gi > 0:
                    E.wait(v, ("a_vtr", gi - 1))
                    wdma(v, None, ("a_spill", gi - 3, "v"))
                    st_idx = 2 + (gi - 1) % 2
                    ins = v.tensor_copy(kvst[st_idx][:], vbfC[:, 0:512])
                    E.inc("v_a", ins)
                    E.mark(("a_evac", gi - 1, "v"), "v_a")
                    E.mark(("a_vtrevac", gi - 1), "v_a")
        gi = MB * NH
        E.wait(v, ("a_vtr", gi - 1))
        wdma(v, None, ("a_spill", gi - 3, "v"))
        ins = v.tensor_copy(kvst[2 + (gi - 1) % 2][:], vbfC[:, 0:512])
        E.inc("v_a", ins)
        E.mark(("a_evac", gi - 1, "v"), "v_a")
        E.mark("fin_v_a", "v_a")

        E.wait(v, "fin_t_a")
        E.wait(v, "fin_c_a")
        for which in ("k", "v"):
            for par in range(2):
                wdma(v, None, ("a_spfin", which, par))
        for mb in range(OMB):
            for jj in range(4):
                j = mb * 4 + jj
                ln_stats(j, "v_p", "cp_xld", "cp")
                E.wait(v, ("cp_xnb", j))
                E.wait(v, ("cp_tile_done", j - 1))
                ins = v.tensor_tensor(cp_lo[j % 2][:], xnf[:], cp_hi[j % 2][:],
                                      ALU.subtract)
                E.inc("v_p", ins)
                E.mark(("cp_lo", j), "v_p")
                E.mark(("cp_hi_done", j), "v_p")
                for half in range(2):
                    E.wait(v, ("cp_trh", j, half))
                    E.wait(v, ("cp_lg", mb - 2))
                    E.wait(v, ("a2_qmmD", (mb - 2) * NH + NH - 1))
                    wdma(v, None, ("cp_xnsp", mb - 2))
                    src = [vbfA, vbfB][half]
                    dst = cp_stg[mb % 2][:].rearrange("p (c s) -> p c s", c=DC)[
                        :, half * 8:(half + 1) * 8, jj * 128:(jj + 1) * 128]
                    ins = v.tensor_copy(dst, src[:].rearrange("p (c s) -> p c s", c=8))
                    E.inc("v_p", ins)
                E.mark(("cp_trevacH", j), "v_p")
                for half in range(2):
                    E.wait(v, ("cp_trl", j, half))
                    E.wait(v, ("cp_lg", mb - 1))
                    src = [vbfA, vbfB][half]
                    dst = cp_loT[:].rearrange("p (c s) -> p c s", c=DC)[
                        :, half * 8:(half + 1) * 8, jj * 128:(jj + 1) * 128]
                    ins = v.tensor_copy(dst, src[:].rearrange("p (c s) -> p c s", c=8))
                    E.inc("v_p", ins)
                E.mark(("cp_trevacL", j), "v_p")
                if jj == 3:
                    E.mark(("cp_stage_hl", mb), "v_p")
            for h in range(NH):
                qi = mb * NH + h
                E.wait(v, ("a2_qmm", qi))
                wdma(v, None, ("a2_qspill", qi - 2))
                ins = v.tensor_copy(qst[qi % 2][:], pA[:, 0:512])
                E.inc("v_p", ins)
                E.mark(("a2_qevac", qi), "v_p")
                E.mark(("a2_qevacd", qi), "v_p")
            E.wait(v, ("cp_lg", mb))
            E.wait(v, ("cp_lgtr", 4 * mb - 1))
            ins = v.tensor_copy(lgT[:], pC[0:E_R, 0:512])
            E.inc("v_p", ins)
            E.mark(("cp_lgTevac", mb), "v_p")
            E.mark(("cp_lgevac", mb), "v_p")
            for jj in range(4):
                j = mb * 4 + jj
                E.wait(v, ("cp_lgtr", j))
                E.wait(v, ("cp_sig", j - 1))
                ins = v.tensor_copy(lgtok[:], pB[0:128, 0:E_R])
                E.inc("v_p", ins)
                E.mark(("cp_lgtokevac", j), "v_p")

                def vch(ins_):
                    cc = E.inc("v_p", ins_)
                    E.wait_ge(v, "v_p", cc)
                    return cc

                E.wait(v, ("cp_sig", j))
                E.wait(v, "gbb_done")
                vch(v.tensor_tensor(cp_sb[:], cp_sc[:], gbbc[:], ALU.add))
                for g_ in range(cfg.G_GROUPS):
                    sl = cp_sb[:, g_ * GSZ:(g_ + 1) * GSZ]
                    vch(v.reduce_max(cp_cols[:, 0:1], sl, axis=AXX))
                    vch(v.tensor_scalar(cp_t8[:], sl, cp_cols[:, 0:1], None, ALU.is_ge))
                    vch(v.scalar_tensor_tensor(cp_t8[:], cp_t8[:], -1e9, sl,
                                               ALU.mult, ALU.add))
                    vch(v.reduce_max(cp_cols[:, 1:2], cp_t8[:], axis=AXX))
                    vch(v.tensor_tensor(cp_g1[:, g_:g_ + 1], cp_cols[:, 0:1],
                                        cp_cols[:, 1:2], ALU.add))
                vch(v.tensor_copy(cp_g2[:], cp_g1[:]))
                vch(v.memset(cp_gm[:], 0.0))
                for it in range(cfg.TOPK_G):
                    vch(v.reduce_max(cp_cols[:, 2:3], cp_g2[:], axis=AXX))
                    vch(v.tensor_scalar(cp_g1[:], cp_g2[:], cp_cols[:, 2:3], None,
                                        ALU.is_ge))
                    vch(v.tensor_tensor(cp_gm[:], cp_gm[:], cp_g1[:], ALU.max))
                    vch(v.scalar_tensor_tensor(cp_g2[:], cp_g1[:], -1e9, cp_g2[:],
                                               ALU.mult, ALU.add))
                for g_ in range(cfg.G_GROUPS):
                    vch(v.tensor_scalar(cp_msk[:, g_ * GSZ:(g_ + 1) * GSZ], ones8[:],
                                        cp_gm[:, g_:g_ + 1], None, ALU.mult))
                vch(v.tensor_scalar(cp_t1[:], cp_msk[:], 1.0, 1e9,
                                    ALU.subtract, ALU.mult))
                vch(v.tensor_tensor(cp_t1[:], cp_sb[:], cp_t1[:], ALU.add))
                vch(v.memset(cp_sel[:], 0.0))
                for it in range(cfg.TOPK):
                    vch(v.reduce_max(cp_cols[:, 3:4], cp_t1[:], axis=AXX))
                    vch(v.tensor_scalar(cp_msk[:], cp_t1[:], cp_cols[:, 3:4], None,
                                        ALU.is_ge))
                    vch(v.tensor_tensor(cp_sel[:], cp_sel[:], cp_msk[:], ALU.max))
                    vch(v.scalar_tensor_tensor(cp_t1[:], cp_msk[:], -1e9, cp_t1[:],
                                               ALU.mult, ALU.add))
                vch(v.tensor_tensor(cp_sel[:], cp_sc[:], cp_sel[:], ALU.mult))
                vch(v.reduce_sum(cp_cols[:, 4:5], cp_sel[:], axis=AXX))
                vch(v.tensor_scalar(cp_cols[:, 5:6], cp_cols[:, 4:5], 1e-20, None,
                                    ALU.add))
                vch(v.reciprocal(cp_cols[:, 6:7], cp_cols[:, 5:6]))
                ins = v.tensor_scalar(cp_t1[:], cp_sel[:], cp_cols[:, 6:7], cfg.RSF,
                                      ALU.mult, ALU.mult)
                E.inc("v_p", ins)
                E.mark(("cp_wd", j), "v_p")
                E.wait(v, ("cp_wdtr", j))
                ins = v.tensor_copy(wdT_sb[:, j * 128:(j + 1) * 128],
                                    pB[0:E_R, 512:640])
                E.inc("v_p", ins)
                E.mark(("cp_wdevac", j), "v_p")
        E.mark("cp_wd_done", "v_p")
        E.mark("fin_v_a2", "v_p")
        E.mark("fin_v_cp", "v_p")

        E.wait(v, "fin_t_a2")
        E.wait(v, "fin_c_a2")
        for chx in ("s_xnsp", "s_xnsp1", "s_wdsp"):
            wdma(v, None, ("cpdma", chx))
        for qc in range(OC):
            ext = ext_of(qc)
            npairs = ext // 256
            for h in range(NH):
                for jp in range(npairs):
                    if jp >= npairs - 4:
                        E.wait(v, ("b_sc", qc, h, jp))
                        midx = (jp - (npairs - 4)) * 2
                        bank = [pA, pB][jp % 2]
                        for mi in range(2):
                            ins = v.tensor_tensor(bank[:, mi * 512:(mi + 1) * 512],
                                                  bank[:, mi * 512:(mi + 1) * 512],
                                                  bmk[midx + mi][:], ALU.add)
                            E.inc("v_b", ins)
                        E.mark(("b_mask", qc, h, jp), "v_b")
                E.wait(v, ("b_pv", qc, h, npairs - 1))
                ins = v.reciprocal(rsum, pD[0:1, 0:512])
                E.inc("v_b", ins)
                E.mark(("b_recip", qc, h), "v_b")
                E.mark(("b_recipd", qc, h), "v_b")
                E.wait(v, ("b_rbccopy", qc, h))
                E.wait(v, ("b_woused", qc - 1))
                zbank = [pC[:, 0:512], pC[:, 512:1024]][h % 2]
                ins = v.tensor_tensor(zAll[:, h * 512:(h + 1) * 512],
                                      zbank, rbc2[h % 2][:], ALU.mult)
                E.inc("v_b", ins)
                E.mark(("b_zn", qc, h), "v_b")
            for dc in range(DC):
                E.wait(v, ("b_wo", qc, dc))
                gdc = qc * DC + dc - 2
                wdma(v, None, ("b_sT_sp", gdc // DC, gdc % DC) if gdc >= 0 else ("x", -1))
                bank = pA[:, 0:512]
                ins = v.tensor_copy(sTst[dc % 2][:], bank)
                E.inc("v_b", ins)
                E.mark(("b_sT_evac", qc, dc), "v_b")
            E.mark(("b_woused", qc), "v_b")
        E.mark("fin_v_b", "v_b")

        E.wait(v, "fin_t_cp")
        E.wait(v, "fin_c_cp")
        for ch in ("s_xnsp", "s_xnsp1", "s_wdsp", "s_xld0", "s_xld1"):
            wdma(v, None, ("cpdma", ch))
        for ch in ("s_spt0", "s_spt1"):
            wdma(v, None, ("bdma", ch))
        for par in range(2):
            wdma(v, None, ("a2_qspfin", par))
        np1 = 0
        np2 = 0
        for tq in range(TQ):
            for grp_i, grp in enumerate(cfg.groups):
                for ki, e in enumerate(grp):
                    routed = e < E_R
                    for i in range(IC):
                        E.wait(v, ("c_p1", np1))
                        if routed:
                            E.wait(v, ("c_wbccopy", tq, e))
                        E.wait(v, ("c_h_evacv", np1 - 2))
                        if grp_i > 0 and i == 0 and ki == 0:
                            E.wait(v, ("c_p2", np2 - 1))
                        bank = [pA[:, 0:512], pA[:, 512:1024]][np1 % 2]
                        a = rtA[np1 % 2]
                        if routed:
                            ins = v.scalar_tensor_tensor(
                                a[:], bank, 0.0, wbc[e % 2][:], ALU.max, ALU.mult)
                        else:
                            ins = v.tensor_scalar(a[:], bank, 0.0, None, ALU.max)
                        cc = E.inc("v_c", ins)
                        E.wait_ge(v, "v_c", cc)
                        ins = v.scalar_tensor_tensor(
                            hsb[ki][:, i * 512:(i + 1) * 512], bank, 0.0, a[:],
                            ALU.max, ALU.mult)
                        cc = E.inc("v_c", ins)
                        E.mark(("c_h_evac", np1), "v_c")
                        E.mark(("c_h_evacv", np1), "v_c")
                        if routed and i == IC - 1:
                            E.mark(("c_wbc_used", tq, e), "v_c")
                        np1 += 1
                for dc in range(DC):
                    E.wait(v, ("c_p2", np2))
                    bank = [pB[:, 0:512], pB[:, 512:1024]][np2 % 2]
                    dst = routedT[:, dc * 512:(dc + 1) * 512]
                    if grp_i == 0:
                        E.wait(v, ("c_ep_mul", tq - 1, 3))
                        E.wait(v, ("c_silu", tq - 1))
                        ins = v.tensor_copy(dst, bank)
                    else:
                        ins = v.tensor_tensor(dst, dst, bank, ALU.add)
                    E.inc("v_c", ins)
                    E.mark(("c_radd", np2), "v_c")
                    np2 += 1
            E.mark(("c_radd_all", tq), "v_c")
            for tt in range(4):
                E.wait(v, ("c_silu", tq))
                wdma(v, None, ("c_sTl", tq, tt))
                rsl = routedT[:].rearrange("p (c s) -> p c s", c=DC)[
                    :, :, tt * 128:(tt + 1) * 128]
                dtt = sTtt[tt % 2][:].rearrange("p (c s) -> p c s", c=DC)
                ins = v.tensor_tensor(dtt, rsl, dtt, ALU.mult)
                E.inc("v_c", ins)
                E.mark(("c_ep_mul", tq, tt), "v_c")
                for dcg in range(4):
                    E.wait(v, ("c_ep_tr", tq, tt, dcg))
                    wdma(v, None, ("c_xc", tq, tt))
                    gtt = tq * 4 + tt - 1
                    wdma(v, None, ("c_out", gtt // 4, gtt % 4) if gtt >= 0 else ("x", -1))
                    bank = pD if dcg % 2 == 0 else pE
                    ins = v.tensor_tensor(outst[tt % 2][:, dcg * 512:(dcg + 1) * 512],
                                          bank[:, 0:512],
                                          xct[tt % 2][:, dcg * 512:(dcg + 1) * 512],
                                          ALU.add)
                    E.inc("v_c", ins)
                    E.mark(("c_ep_add", tq, tt, dcg), "v_c")
                E.mark(("c_ep_addlast", tq, tt), "v_c")
                E.mark(("c_ep_out", tq, tt), "v_c")

    def run_all(engs):
        body_sync(engs["sync"])
        body_gpsimd(engs["gpsimd"])
        body_tensor(engs["tensor"])
        body_scalar(engs["scalar"])
        body_vector(engs["vector"])

    E.pass1 = True
    run_all({k: _Noop() for k in ("sync", "gpsimd", "tensor", "scalar", "vector")})
    E.pass1 = False
    E.reset()

    with nc.Block() as block:
        @block.sync
        def _(sync):
            body_sync(sync)

        @block.gpsimd
        def _(gp):
            body_gpsimd(gp)

        @block.tensor
        def _(t):
            body_tensor(t)

        @block.scalar
        def _(s):
            body_scalar(s)

        @block.vector
        def _(v):
            body_vector(v)

    ctx.close()
    if E.missing:
        import sys
        print("MISSING MARKS:", dict(list(E.missing.items())[:20]), file=sys.stderr)
    return nc



def host_images(cfg, W_Q, W_K, W_V, W_O, gate_w, w1, w2, shared_up, shared_down):
    DC, IC, NE, E_R = cfg.DC, cfg.IC, cfg.NE, cfg.E_R
    D = cfg.D

    def qkv_img(W):
        Hn = W.shape[0]
        r = W.reshape(Hn, 128, DC, 128)
        return np.ascontiguousarray(r.transpose(0, 3, 2, 1).reshape(Hn, 128, DC * 128))

    wq_img = qkv_img(W_Q)
    wk_img = qkv_img(W_K)
    wv_img = qkv_img(W_V)
    Hn = W_O.shape[0]
    r = W_O.reshape(Hn, DC, 128, 128)
    wo_img = np.ascontiguousarray(r.transpose(1, 3, 0, 2).reshape(DC, 128, Hn * 128))

    w1a = np.concatenate([w1, shared_up.reshape(4, 512, D)], axis=0)
    r = w1a.transpose(0, 2, 1)
    r = r.reshape(NE, DC, 128, 2, 2, 128)
    w1f = np.ascontiguousarray(
        r.transpose(0, 3, 2, 1, 4, 5).reshape(NE, 2, 128, DC * 2 * 128))

    w2a = np.concatenate(
        [w2, shared_down.reshape(D, 4, 512).transpose(1, 0, 2)], axis=0)
    r = w2a.transpose(0, 2, 1)
    r = r.reshape(NE, IC, 128, DC, 128)
    w2f = np.ascontiguousarray(
        r.transpose(3, 2, 0, 1, 4).reshape(DC, 128, NE * IC * 128))

    r = gate_w.reshape(E_R, DC, 128)
    gw_img = np.ascontiguousarray(r.transpose(2, 1, 0).reshape(128, DC * E_R))

    pp, ff = np.meshgrid(np.arange(128), np.arange(512), indexing="ij")
    diag = [np.where(pp + 128 * k > ff, np.float32(-1e9), np.float32(0.0))
            for k in range(4)]
    zero = np.zeros((128, 512), np.float32)
    full = np.full((128, 512), np.float32(-1e9), np.float32)
    bmask = {
        0: np.ascontiguousarray(np.stack([zero] * 4 + diag)).astype(ml_dtypes.bfloat16),
        1: np.ascontiguousarray(np.stack(diag + [full] * 4)).astype(ml_dtypes.bfloat16),
    }
    return dict(wq=wq_img, wk=wk_img, wv=wv_img, wo=wo_img, w1f=w1f, w2f=w2f,
                gw=gw_img, bmask=bmask)


_BUILD_CACHE = {}


def prepare_in_maps(cfg, x, W_Q, W_K, W_V, W_O, gw, gb, w1, w2, shu, shd):
    img = host_images(cfg, W_Q, W_K, W_V, W_O, gw, w1, w2, shu, shd)
    idb = np.eye(128).astype(ml_dtypes.bfloat16)
    idf = np.eye(128).astype(np.float32)
    onesb = np.ones((128, 1), ml_dtypes.bfloat16)
    onesf = np.ones((1, 128), np.float32)
    ones8 = np.ones((128, cfg.GSZ), np.float32)
    in_maps = []
    for core in range(cfg.n_cores):
        b = core // 2
        hh = core % 2
        chunks = cfg.own_chunks(hh)
        xo = np.concatenate([x[b, c * 512:(c + 1) * 512] for c in chunks], axis=0)
        in_maps.append({
            "x_in": np.ascontiguousarray(x[b]),
            "xo_in": np.ascontiguousarray(xo),
            "wq_in": img["wq"],
            "wk_in": img["wk"],
            "wv_in": img["wv"],
            "wo_in": img["wo"],
            "w1f_in": img["w1f"],
            "w2f_in": img["w2f"],
            "gwT_in": img["gw"],
            "gb_in": gb.reshape(1, -1).astype(np.float32),
            "bmask_in": img["bmask"][hh],
            "idb_in": idb,
            "idf_in": idf,
            "onesb_in": onesb,
            "onesf_in": onesf,
            "ones8_in": ones8,
        })
    return in_maps


def assemble_out(cfg, results, B):
    out = np.empty((B, cfg.S, cfg.D), np.float32)
    for core in range(cfg.n_cores):
        b = core // 2
        hh = core % 2
        for i, c in enumerate(cfg.own_chunks(hh)):
            out[b, c * 512:(c + 1) * 512] = \
                results[core]["out_half"][i * 512:(i + 1) * 512]
    return out


def kernel(L0_attn_x, L0_attn_W_Q, L0_attn_W_K, L0_attn_W_V, L0_attn_W_O,
           L0_ffn_gate_w, L0_ffn_gate_bias, L0_ffn_w1, L0_ffn_w2,
           L0_ffn_shared_up, L0_ffn_shared_down):
    cfg = CFG()
    x = np.ascontiguousarray(np.asarray(L0_attn_x, np.float32))
    W_Q = np.asarray(L0_attn_W_Q, np.float32)
    W_K = np.asarray(L0_attn_W_K, np.float32)
    W_V = np.asarray(L0_attn_W_V, np.float32)
    W_O = np.asarray(L0_attn_W_O, np.float32)
    gw = np.asarray(L0_ffn_gate_w, np.float32)
    gb = np.asarray(L0_ffn_gate_bias, np.float32)
    w1 = np.asarray(L0_ffn_w1, np.float32)
    w2 = np.asarray(L0_ffn_w2, np.float32)
    shu = np.asarray(L0_ffn_shared_up, np.float32)
    shd = np.asarray(L0_ffn_shared_down, np.float32)

    if "nc" not in _BUILD_CACHE:
        _BUILD_CACHE["nc"] = build_program(cfg)
    nc = _BUILD_CACHE["nc"]
    in_maps = prepare_in_maps(cfg, x, W_Q, W_K, W_V, W_O, gw, gb, w1, w2, shu, shd)
    res = run_bass_kernel_spmd(nc, in_maps, core_ids=list(range(cfg.n_cores)))
    return assemble_out(cfg, res.results, x.shape[0])


# version 04
# speedup vs baseline: 1.1763x; 1.1763x over previous
import numpy as np
import ml_dtypes
from contextlib import ExitStack

import concourse.bass as bass
import concourse.mybir as mybir
from concourse.bass_utils import run_bass_kernel_spmd

F32 = mybir.dt.float32
BF16 = mybir.dt.bfloat16
AF = mybir.ActivationFunctionType
ALU = mybir.AluOpType
AXX = mybir.AxisListType.X
DBG_A = False


class CFG:
    def __init__(self, S=4096, E_R=64, G_GROUPS=8, TOPK_G=4, TOPK=6, GRP=4,
                 n_cores=8):
        self.S = S
        self.D = 2048
        self.DC = 16
        self.NH = 16
        self.E_R = E_R
        self.G_GROUPS = G_GROUPS
        self.GSZ = E_R // G_GROUPS
        self.TOPK_G = TOPK_G
        self.TOPK = TOPK
        self.RSF = 2.5
        self.NE = E_R + 4
        self.IC = 4
        self.MB = S // 512
        self.OWN = S // 2
        self.OC = self.OWN // 512
        self.TQ = self.OWN // 512
        self.TT = self.OWN // 128
        self.OMB = self.OWN // 512
        self.n_cores = n_cores
        self.groups = []
        e = 0
        while e < E_R:
            self.groups.append(list(range(e, min(e + GRP, E_R))))
            e += GRP
        self.groups.append(list(range(E_R, self.NE)))

    def own_chunks(self, hh):
        return [2 * i + (1 - hh) for i in range(self.OC)]


class Emitter:
    def __init__(self, sems):
        self.sems = sems
        self.pass1 = True
        self.marks = {}
        self.missing = {}
        self.reset()

    def reset(self):
        self.counts = {k: 0 for k in self.sems}
        self.dma_counts = {k: 0 for k in self.sems}

    def inc(self, ch, instr):
        self.counts[ch] += 1
        if not self.pass1 and instr is not None:
            instr.then_inc(self.sems[ch], 1)
        return self.counts[ch]

    def dinc(self, ch, instr):
        self.dma_counts[ch] += 16
        if not self.pass1 and instr is not None:
            instr.then_inc(self.sems[ch], 16)
        return self.dma_counts[ch]

    def mark(self, key, ch):
        if self.pass1:
            self.marks[key] = (ch, self.counts[ch])

    def mark_d(self, key, ch):
        if self.pass1:
            self.marks[key] = (ch, self.dma_counts[ch])

    def _miss(self, key):
        idxs = [k for k in key[1:] if isinstance(k, int)] if isinstance(key, tuple) else []
        if all(i >= 0 for i in idxs):
            self.missing[key] = self.missing.get(key, 0) + 1

    def wait(self, eng, key):
        if not self.pass1:
            if key not in self.marks:
                self._miss(key)
                return
            ch, v = self.marks[key]
            if v > 0:
                eng.wait_ge(self.sems[ch], v)

    def wait_ge(self, eng, ch, v):
        if not self.pass1 and v > 0:
            eng.wait_ge(self.sems[ch], v)


class _Noop:
    def __getattr__(self, name):
        def f(*a, **k):
            return None
        return f


def build_program(cfg: CFG):
    nc = bass.Bass("TRN2", target_bir_lowering=False,
                   detect_race_conditions=not DBG_A)
    S, D, DC, NH = cfg.S, cfg.D, cfg.DC, cfg.NH
    NE, IC, MB = cfg.NE, cfg.IC, cfg.MB
    OWN, TQ, TT, E_R, OMB, OC = cfg.OWN, cfg.TQ, cfg.TT, cfg.E_R, cfg.OMB, cfg.OC
    NTI = S // 128
    GSZ = cfg.GSZ
    SCALE = float(1.0 / np.sqrt(128.0))

    x_in = nc.dram_tensor("x_in", [S, D], F32, kind="ExternalInput").ap()
    xo_in = nc.dram_tensor("xo_in", [OWN, D], F32, kind="ExternalInput").ap()
    wq_in = nc.dram_tensor("wq_in", [NH, 128, D], F32, kind="ExternalInput").ap()
    wk_in = nc.dram_tensor("wk_in", [NH, 128, D], F32, kind="ExternalInput").ap()
    wv_in = nc.dram_tensor("wv_in", [NH, 128, D], F32, kind="ExternalInput").ap()
    wo_in = nc.dram_tensor("wo_in", [DC, 128, NH * 128], F32, kind="ExternalInput").ap()
    w1f_in = nc.dram_tensor("w1f_in", [NE, 2, 128, DC * 2 * 128], F32, kind="ExternalInput").ap()
    w2f_in = nc.dram_tensor("w2f_in", [DC, 128, NE * IC * 128], F32, kind="ExternalInput").ap()
    gwT_in = nc.dram_tensor("gwT_in", [128, DC * E_R], F32, kind="ExternalInput").ap()
    gb_in = nc.dram_tensor("gb_in", [1, E_R], F32, kind="ExternalInput").ap()
    bmask_in = nc.dram_tensor("bmask_in", [8, 128, 512], BF16, kind="ExternalInput").ap()
    idb_in = nc.dram_tensor("idb_in", [128, 128], BF16, kind="ExternalInput").ap()
    idf_in = nc.dram_tensor("idf_in", [128, 128], F32, kind="ExternalInput").ap()
    onesb_in = nc.dram_tensor("onesb_in", [128, 1], BF16, kind="ExternalInput").ap()
    onesf_in = nc.dram_tensor("onesf_in", [1, 128], F32, kind="ExternalInput").ap()
    ones8_in = nc.dram_tensor("ones8_in", [128, GSZ], F32, kind="ExternalInput").ap()
    out_t = nc.dram_tensor("out_half", [OWN, D], F32, kind="ExternalOutput").ap()

    kT_sp = nc.dram_tensor("kT_sp", [NH, 128, S], BF16, kind="Internal").ap()
    v_sp = nc.dram_tensor("v_sp", [NH, 128, S], BF16, kind="Internal").ap()
    q_sp = nc.dram_tensor("q_sp", [NH, 128, OWN], BF16, kind="Internal").ap()
    xnTo_sp = nc.dram_tensor("xnTo_sp", [128, DC * OWN], BF16, kind="Internal").ap()
    wdT_sp = nc.dram_tensor("wdT_sp", [E_R, OWN], F32, kind="Internal").ap()
    sT_sp = nc.dram_tensor("sT_sp", [128, DC * OWN], F32, kind="Internal").ap()
    wkvb_sp = nc.dram_tensor("wkvb_sp", [NH, 2, 128, D], BF16, kind="Internal").ap()
    wqb_sp = nc.dram_tensor("wqb_sp", [NH, 128, D], BF16, kind="Internal").ap()
    wob_sp = nc.dram_tensor("wob_sp", [DC, 128, NH * 128], BF16, kind="Internal").ap()
    w1b_sp = nc.dram_tensor("w1b_sp", [NE, 2, 128, DC * 2 * 128], BF16, kind="Internal").ap()
    w2b_sp = nc.dram_tensor("w2b_sp", [DC, 128, NE * IC * 128], BF16, kind="Internal").ap()

    ctx = ExitStack()
    sb = lambda n, s, d: ctx.enter_context(nc.sbuf_tensor(n, s, d))
    pst = lambda n, s, d=F32: ctx.enter_context(nc.psum_tensor(n, s, d))
    mksem = lambda n: ctx.enter_context(nc.semaphore(n))

    DCER = DC * E_R
    CB = sb("CB", [128, 4352 + 2 * DCER], BF16)
    bmk = [CB[:, i * 512:(i + 1) * 512] for i in range(8)]
    idb = CB[:, 4096:4224]
    onesb = CB[:, 4224:4225]
    gwhi = CB[:, 4352:4352 + DCER]
    gwlo = CB[:, 4352 + DCER:4352 + 2 * DCER]
    CF = sb("CF", [128, 256 + GSZ + 2 * E_R], F32)
    idf = CF[:, 0:128]
    onesf = CF[0:1, 128:256]
    ones8 = CF[:, 256:256 + GSZ]
    gbbc = CF[:, 256 + GSZ:256 + GSZ + E_R]
    gbrow = CF[0:1, 256 + GSZ + E_R:256 + GSZ + 2 * E_R]

    NBF = 55296
    NF32 = 21504
    ABF = sb("ABF", [128, NBF], BF16)
    AFL = sb("AFL", [128, NF32], F32)

    def bfv(off, n):
        assert off + n <= NBF, (off, n)
        return ABF[:, off:off + n], off + n

    def ffv(off, n):
        assert off + n <= NF32, (off, n)
        return AFL[:, off:off + n], off + n

    o = 0
    wkv_t = []
    for p in range(4):
        t_, o = bfv(o, 2 * D)
        wkv_t.append(t_)
    xnb = []
    for p in range(2):
        t_, o = bfv(o, D)
        xnb.append(t_)
    stg = []
    for p in range(2):
        t_, o = bfv(o, DC * 512)
        stg.append(t_)
    kvst = []
    for p in range(4):
        t_, o = bfv(o, 512)
        kvst.append(t_)
    vtst = []
    for p in range(2):
        t_, o = bfv(o, 512)
        vtst.append(t_)

    o = 0
    cp_hi = []
    for p in range(2):
        t_, o = bfv(o, D)
        cp_hi.append(t_)
    cp_lo = []
    for p in range(2):
        t_, o = bfv(o, D)
        cp_lo.append(t_)
    cp_stg = []
    for p in range(2):
        t_, o = bfv(o, DC * 512)
        cp_stg.append(t_)
    cp_loT, o = bfv(o, DC * 512)
    wq_t = []
    for p in range(2):
        t_, o = bfv(o, D)
        wq_t.append(t_)
    qst = []
    for p in range(2):
        t_, o = bfv(o, 512)
        qst.append(t_)

    n_hsb = max(len(g) for g in cfg.groups)
    o = 0
    kqb, vqb = [], []
    for p in range(4):
        t_, o = bfv(o, 512)
        kqb.append(t_)
    for p in range(4):
        t_, o = bfv(o, 512)
        vqb.append(t_)
    qbuf, Ebuf = [], []
    for p in range(2):
        t_, o = bfv(o, 512)
        qbuf.append(t_)
    for p in range(2):
        t_, o = bfv(o, 512)
        Ebuf.append(t_)
    zAll, o = bfv(o, NH * 512)
    wobd = []
    for p in range(2):
        t_, o = bfv(o, NH * 128)
        wobd.append(t_)
    xnTq, o = bfv(o, DC * 512)
    w1sb = []
    for j in range(4):
        t_, o = bfv(o, DC * 2 * 128)
        w1sb.append(t_)
    w2sbs = []
    for p in range(2):
        t_, o = bfv(o, n_hsb * IC * 128)
        w2sbs.append(t_)
    hsb = []
    for k in range(n_hsb):
        t_, o = bfv(o, IC * 512)
        hsb.append(t_)
    assert o <= NBF, o
    assert NBF - 8704 >= 39936
    p0st = [ABF[:, NBF - 8704:NBF - 4352], ABF[:, NBF - 4352:NBF]]

    o = 0
    xt = []
    for p in range(2):
        t_, o = ffv(o, D)
        xt.append(t_)
    sqscr, o = ffv(o, D)
    xnf, o = ffv(o, D)
    lncol, o = ffv(o, 32)
    A_F32_END = o

    def lnc(j, i):
        return lncol[:, (j % 2) * 16 + i:(j % 2) * 16 + i + 1]

    gwf = AFL[:, A_F32_END:A_F32_END + DC * E_R]

    o = A_F32_END
    lgT_t, o = ffv(o, 512)
    lgT = lgT_t[0:E_R, :]
    lgtok, o = ffv(o, E_R)
    cp_sc, o = ffv(o, E_R)
    cp_sb, o = ffv(o, E_R)
    cp_msk, o = ffv(o, E_R)
    cp_sel, o = ffv(o, E_R)
    cp_t1, o = ffv(o, E_R)
    cp_t8, o = ffv(o, GSZ)
    cp_g1, o = ffv(o, cfg.G_GROUPS)
    cp_g2, o = ffv(o, cfg.G_GROUPS)
    cp_gm, o = ffv(o, cfg.G_GROUPS)
    cp_cols, o = ffv(o, 8)
    wdT_t, o = ffv(o, OWN)
    wdT_sb = wdT_t[0:E_R, :]

    o = 0
    rbc_t, o = ffv(o, 1024)
    rbc2 = [rbc_t[:, 0:512], rbc_t[:, 512:1024]]
    rsum_t, o = ffv(o, 512)
    rsum = rsum_t[0:1, :]
    sTst = []
    for p in range(2):
        t_, o = ffv(o, 512)
        sTst.append(t_)
    routedT, o = ffv(o, DC * 512)
    wbc = []
    for p in range(2):
        t_, o = ffv(o, 512)
        wbc.append(t_)
    rtA_t, o = ffv(o, 512)
    rtA = [rtA_t, rtA_t]
    sTtt = []
    for p in range(2):
        t_, o = ffv(o, DC * 128)
        sTtt.append(t_)
    xct = []
    for p in range(2):
        t_, o = ffv(o, D)
        xct.append(t_)
    wdrow_t, o = ffv(o, 1024)
    wdrow = [wdrow_t[0:1, 0:512], wdrow_t[0:1, 512:1024]]
    assert o <= NF32, o

    pA = pst("pA", [128, 1024])
    pB = pst("pB", [128, 1024])
    pC = pst("pC", [128, 1024])
    pD = pst("pD", [128, 512])
    pE = pst("pE", [128, 512])
    vbfA = pD[:].bitcast(BF16)
    vbfB = pE[:].bitcast(BF16)
    vbfC = pB[:, 512:1024].bitcast(BF16)

    DSEM = ["s_const", "s_gw", "s_xld0", "s_xld1", "s_wkv0", "s_wkv1", "s_wkv2", "s_wkv3",
            "s_wq0", "s_wq1",
            "s_pi0", "s_pi1", "s_po0", "s_po1",
            "s_kv0", "s_kv1", "s_kv2", "s_kv3", "s_q0", "s_q1", "s_wo0", "s_wo1",
            "s_spk0", "s_spk1", "s_spv0", "s_spv1", "s_spq0", "s_spq1",
            "s_spt0", "s_spt1",
            "s_w1l0", "s_w1l1", "s_w1l2", "s_w1l3", "s_w2l0", "s_w2l1",
            "s_xnq", "s_sTl0", "s_sTl1", "s_xc0", "s_xc1",
            "s_out", "s_out1", "s_wd0", "s_wd1", "s_xnsp", "s_xnsp1", "s_wdsp"]
    CSEM = ["t_a", "v_a", "c_a", "t_b", "v_b", "c_b", "t_c", "v_c", "c_c",
            "t_p", "v_p", "c_p"]
    sems = {n: mksem(n) for n in DSEM + CSEM}
    E = Emitter(sems)

    def wdma(eng, ch, key):
        if not E.pass1:
            if key not in E.marks:
                E._miss(key)
                return
            chm, v = E.marks[key]
            if v > 0:
                eng.wait_ge(sems[chm], v)

    p0_slices = []
    for h in range(NH):
        p0_slices.append((("wkv", h, 0), 2048))
        p0_slices.append((("wkv", h, 1), 2048))
    for h in range(NH):
        p0_slices.append((("wq", h), 2048))
    for dc in range(DC):
        p0_slices.append((("wo", dc), NH * 128))
    for e in range(NE):
        for ih in range(2):
            p0_slices.append((("w1", e, ih), 4096))
    ncol2 = NE * IC * 128
    NSL2 = (ncol2 + 4351) // 4352
    for dc in range(DC):
        for s_ in range(NSL2):
            c0 = s_ * 4352
            c1 = min(ncol2, c0 + 4352)
            p0_slices.append((("w2", dc, s_, c0, c1), c1 - c0))

    def p0_aps(key):
        if key[0] == "wkv":
            _, h, j = key
            src = [wk_in, wv_in][j][h]
            dst = wkvb_sp[h, j]
        elif key[0] == "wq":
            src = wq_in[key[1]]
            dst = wqb_sp[key[1]]
        elif key[0] == "wo":
            src = wo_in[key[1]]
            dst = wob_sp[key[1]]
        elif key[0] == "w1":
            _, e, ih = key
            src = w1f_in[e, ih]
            dst = w1b_sp[e, ih]
        else:
            _, dc, s_, c0, c1 = key
            src = w2f_in[dc, :, c0:c1]
            dst = w2b_sp[dc, :, c0:c1]
        return dst, src

    def ext_of(qc):
        return 1024 * (qc + 1)


    def b_items(qc):
        out = []
        nmt = ext_of(qc) // 128
        nq = ext_of(qc) // 512
        for h in range(NH):
            out.append(("b_q", qc, h))
            out.append(("b_kvq", qc, h, 0))
            if nq > 1:
                out.append(("b_kvq", qc, h, 1))
            for mt in range(nmt):
                qq = mt // 4 + 2
                if mt % 4 == 0 and qq < nq:
                    out.append(("b_kvq", qc, h, qq))
                out.append(("b_mt", qc, h, mt))
            out.append(("b_fin1", qc, h))
            out.append(("b_fin2", qc, h))
        for dc in range(DC):
            out.append(("b_wo", qc, dc))
        return out

    def c_items(tq):
        out = [("c_xnq", tq)]
        for grp_i, grp in enumerate(cfg.groups):
            for e in grp:
                if e < E_R:
                    out.append(("c_wbc", tq, e))
                for i in range(IC):
                    out.append(("c_p1", tq, grp_i, e, i))
            for dc in range(DC):
                out.append(("c_p2", tq, grp_i, dc))
        for tt in range(4):
            out.append(("c_ep", tq, tt))
        return out

    BC_SCHED = list(b_items(0))
    for tq in range(TQ):
        ci = c_items(tq)
        bi = b_items(tq + 1) if tq + 1 < OC else []
        r = len(bi) / max(1, len(ci))
        acc = 0.0
        k = 0
        for it in ci:
            BC_SCHED.append(it)
            acc += r
            while k < len(bi) and acc >= 1.0:
                BC_SCHED.append(bi[k])
                k += 1
                acc -= 1.0
        BC_SCHED.extend(bi[k:])

    B_STEP = {}
    B_LAST = {}
    GQ = {}
    _n = _ng = 0
    for qc in range(OC):
        nmt = ext_of(qc) // 128
        for h in range(NH):
            for qq in range(ext_of(qc) // 512):
                GQ[(qc, h, qq)] = _ng
                _ng += 1
            for mt in range(nmt):
                B_STEP[(qc, h, mt)] = _n
                _n += 1
            B_LAST[(qc, h)] = _n - 1
    C_P1 = {}
    GWB = {}
    GE = {}
    GW2 = {}
    P1_FIRST = {}
    P1_LAST = {}
    _n1 = _nwb = _nge = _ng2 = 0
    for it in BC_SCHED:
        if it[0] == "c_p1":
            C_P1[it] = _n1
            P1_FIRST.setdefault(it[1], _n1)
            P1_LAST[it[1]] = _n1
            if it[4] == 0:
                GE[(it[1], it[3])] = _nge
                _nge += 1
            _n1 += 1
        elif it[0] == "c_wbc":
            GWB[(it[1], it[2])] = _nwb
            _nwb += 1
        elif it[0] == "c_p2":
            GW2[(it[1], it[2], it[3])] = _ng2
            _ng2 += 1

    def body_sync(sync):
        for i in range(8):
            E.dinc("s_const", sync.dma_start(out=bmk[i][:], in_=bmask_in[i]))
        E.dinc("s_const", sync.dma_start(out=idb[:], in_=idb_in))
        E.dinc("s_const", sync.dma_start(out=idf[:], in_=idf_in))
        E.dinc("s_const", sync.dma_start(out=onesb[:], in_=onesb_in))
        E.dinc("s_const", sync.dma_start(out=onesf[:], in_=onesf_in))
        E.dinc("s_const", sync.dma_start(out=ones8[:], in_=ones8_in))
        E.dinc("s_const", sync.dma_start(out=gbrow[:], in_=gb_in))
        E.dinc("s_gw", sync.dma_start(out=gwf[:], in_=gwT_in))

        for mb in range(MB):
            for jj in range(4):
                j = mb * 4 + jj
                E.wait(sync, ("a_tile_done", j - 2))
                E.wait(sync, ("a_xnb_s", j - 2))
                ch = f"s_xld{j % 2}"
                E.dinc(ch, sync.dma_start(
                    out=xt[j % 2][:], in_=x_in[j * 128:(j + 1) * 128, :]))
                if E.pass1:
                    E.marks[("xld", j)] = (ch, E.dma_counts[ch])
            for h in range(NH):
                gi = mb * NH + h
                E.wait(sync, ("a_vmm", gi - 4))
                wdma(sync, None, ("p0_wkv", h, 0))
                wdma(sync, None, ("p0_wkv", h, 1))
                ch = f"s_wkv{gi % 4}"
                E.dinc(ch, sync.dma_start(
                    out=wkv_t[gi % 4][:].rearrange("p (j d) -> p j d", j=2),
                    in_=wkvb_sp[h].rearrange("j p d -> p j d")))
                if E.pass1:
                    E.marks[("wkvl", gi)] = (ch, E.dma_counts[ch])
                if gi >= 1:
                    pg = gi - 1
                    pmb, ph_ = pg // NH, pg % NH
                    for wi, (which, spt) in enumerate((("k", kT_sp), ("v", v_sp))):
                        E.wait(sync, ("a_evac", pg, which))
                        st_idx = wi * 2 + pg % 2
                        ch = f"s_sp{which}{pg % 2}"
                        E.dinc(ch, sync.dma_start(
                            out=spt[ph_][:, pmb * 512:(pmb + 1) * 512],
                            in_=kvst[st_idx][:]))
                        E.mark_d(("a_spill", pg, which), ch)
        pg = MB * NH - 1
        pmb, ph_ = pg // NH, pg % NH
        for wi, (which, spt) in enumerate((("k", kT_sp), ("v", v_sp))):
            E.wait(sync, ("a_evac", pg, which))
            st_idx = wi * 2 + pg % 2
            ch = f"s_sp{which}{pg % 2}"
            E.dinc(ch, sync.dma_start(
                out=spt[ph_][:, pmb * 512:(pmb + 1) * 512], in_=kvst[st_idx][:]))
            E.mark_d(("a_spill", pg, which), ch)
        for which in ("k", "v"):
            for par in range(2):
                E.mark_d(("a_spfin", which, par), f"s_sp{which}{par}")

        E.wait(sync, ("a_xnb_s", NTI - 1))
        E.wait(sync, ("a_xnb_s", NTI - 2))
        for mb in range(OMB):
            for jj in range(4):
                j = mb * 4 + jj
                E.wait(sync, ("cp_tile_done", j - 2))
                E.wait(sync, ("cp_xnb_s", j - 2))
                ch = f"s_xld{j % 2}"
                E.dinc(ch, sync.dma_start(
                    out=xt[j % 2][:], in_=xo_in[j * 128:(j + 1) * 128, :]))
                if E.pass1:
                    E.marks[("cp_xld", j)] = (ch, E.dma_counts[ch])
            E.wait(sync, ("cp_stage_hl", mb))
            ch = "s_xnsp" if mb % 2 == 0 else "s_xnsp1"
            E.dinc(ch, sync.dma_start(
                out=xnTo_sp.rearrange("p (c s) -> p c s", c=DC)[:, :, mb * 512:(mb + 1) * 512],
                in_=cp_stg[mb % 2][:].rearrange("p (c s) -> p c s", c=DC)))
            E.mark_d(("cp_xnsp", mb), ch)
            for h in range(NH):
                qi = mb * NH + h
                E.wait(sync, ("a2_qmm", qi - 2))
                wdma(sync, None, ("p0_wqA", h, 0))
                wdma(sync, None, ("p0_wqA", h, 1))
                ch = f"s_wq{qi % 2}"
                E.dinc(ch, sync.dma_start(out=wq_t[qi % 2][:], in_=wqb_sp[h]))
                if E.pass1:
                    E.marks[("wql", qi)] = (ch, E.dma_counts[ch])
                if qi >= 1:
                    pq = qi - 1
                    pmbq, phq = pq // NH, pq % NH
                    E.wait(sync, ("a2_qevac", pq))
                    ch = f"s_spq{pq % 2}"
                    E.dinc(ch, sync.dma_start(
                        out=q_sp[phq][:, pmbq * 512:(pmbq + 1) * 512],
                        in_=qst[pq % 2][:]))
                    E.mark_d(("a2_qspill", pq), ch)
        pq = OMB * NH - 1
        pmbq, phq = pq // NH, pq % NH
        E.wait(sync, ("a2_qevac", pq))
        ch = f"s_spq{pq % 2}"
        E.dinc(ch, sync.dma_start(
            out=q_sp[phq][:, pmbq * 512:(pmbq + 1) * 512], in_=qst[pq % 2][:]))
        E.mark_d(("a2_qspill", pq), ch)
        for par in range(2):
            E.mark_d(("a2_qspfin", par), f"s_spq{par}")
        E.wait(sync, "cp_wd_done")
        E.dinc("s_wdsp", sync.dma_start(out=wdT_sp, in_=wdT_sb))
        E.mark_d("cp_wdsp", "s_wdsp")
        for chx in ("s_xnsp", "s_xnsp1", "s_wdsp", "s_xld0", "s_xld1"):
            E.mark_d(("cpdma", chx), chx)

        E.wait(sync, "fin_t_a2")
        E.wait(sync, "fin_v_a2")
        E.wait(sync, "fin_c_a2")
        for chx in ("s_xnsp", "s_xnsp1", "s_wdsp", "s_xld0", "s_xld1"):
            wdma(sync, None, ("cpdma", chx))
        wdma(sync, None, ("p0_wo", 0))
        wdma(sync, None, ("p0_wo", 1))
        for which in ("k", "v"):
            for par in range(2):
                wdma(sync, None, ("a_spfin", which, par))
        for par in range(2):
            wdma(sync, None, ("a2_qspfin", par))
        nwo = 0
        for it in BC_SCHED:
            kind = it[0]
            if kind == "b_q":
                _, qc, h = it
                gh = qc * NH + h
                E.wait(sync, ("b_scdone", gh - 2))
                ch = f"s_q{gh % 2}"
                E.dinc(ch, sync.dma_start(
                    out=qbuf[gh % 2][:], in_=q_sp[h][:, qc * 512:(qc + 1) * 512]))
                if E.pass1:
                    E.marks[("b_ql", gh)] = (ch, E.dma_counts[ch])
            elif kind == "b_kvq":
                _, qc, h, qq = it
                g = GQ[(qc, h, qq)]
                E.wait(sync, ("b_qdone", g - 4))
                ch = f"s_kv{g % 4}"
                E.dinc(ch, sync.dma_start(
                    out=kqb[g % 4][:],
                    in_=kT_sp[h][:, qq * 512:(qq + 1) * 512]))
                E.dinc(ch, sync.dma_start(
                    out=vqb[g % 4][:],
                    in_=v_sp[h][:, qq * 512:(qq + 1) * 512]))
                if E.pass1:
                    E.marks[("b_kvl", g)] = (ch, E.dma_counts[ch])
            elif kind == "b_wo":
                _, qc, dc = it
                E.wait(sync, ("b_woN", nwo - 2))
                ch = f"s_wo{dc % 2}"
                E.dinc(ch, sync.dma_start(out=wobd[dc % 2][:], in_=wob_sp[dc]))
                if E.pass1:
                    E.marks[("b_wol", qc, dc)] = (ch, E.dma_counts[ch])
                E.wait(sync, ("b_sT_evac", qc, dc))
                col = dc * OWN + qc * 512
                ch = f"s_spt{dc % 2}"
                E.dinc(ch, sync.dma_start(out=sT_sp[:, col:col + 512],
                                          in_=sTst[dc % 2][:]))
                E.mark_d(("b_sT_sp", qc, dc), ch)
                nwo += 1
            elif kind == "c_xnq":
                _, tq = it
                E.wait(sync, ("c_p1_done", tq - 1))
                E.wait(sync, ("cp_xnsp", OMB - 1))
                if OMB >= 2:
                    E.wait(sync, ("cp_xnsp", OMB - 2))
                E.dinc("s_xnq", sync.dma_start(
                    out=xnTq[:].rearrange("p (c s) -> p c s", c=DC),
                    in_=xnTo_sp.rearrange("p (c s) -> p c s", c=DC)[:, :, tq * 512:(tq + 1) * 512]))
                if E.pass1:
                    E.marks[("c_xnq", tq)] = ("s_xnq", E.dma_counts["s_xnq"])
            elif kind == "c_wbc":
                _, tq, e = it
                E.wait(sync, ("c_wbcmmG", GWB[(tq, e)] - 2))
                E.wait(sync, "cp_wdsp")
                ch = f"s_wd{e % 2}"
                E.dinc(ch, sync.dma_start(
                    out=wdrow[e % 2], in_=wdT_sp[e:e + 1, tq * 512:(tq + 1) * 512]))
                if E.pass1:
                    E.marks[("c_wdl", tq, e)] = (ch, E.dma_counts[ch])
            elif kind == "c_ep":
                _, tq, tt = it
                gtt = tq * 4 + tt
                E.wait(sync, ("c_ep_tr", (gtt - 2) // 4, (gtt - 2) % 4, 3)
                       if gtt >= 2 else ("x", -1))
                wdma(sync, None, ("b_sT_sp", tq, DC - 1))
                wdma(sync, None, ("b_sT_sp", tq, DC - 2))
                ch = f"s_sTl{tt % 2}"
                E.dinc(ch, sync.dma_start(
                    out=sTtt[tt % 2][:].rearrange("p (c s) -> p c s", c=DC),
                    in_=sT_sp.rearrange("p (c s) -> p c s", c=DC)[:, :, tq * 512 + tt * 128:tq * 512 + (tt + 1) * 128]))
                if E.pass1:
                    E.marks[("c_sTl", tq, tt)] = (ch, E.dma_counts[ch])
                wdma(sync, None, ("c_out", (gtt - 2) // 4, (gtt - 2) % 4)
                     if gtt >= 2 else ("x", -1))
                ch = f"s_xc{tt % 2}"
                E.dinc(ch, sync.dma_start(
                    out=xct[tt % 2][:],
                    in_=xo_in[gtt * 128:(gtt + 1) * 128, :]))
                if E.pass1:
                    E.marks[("c_xc", tq, tt)] = (ch, E.dma_counts[ch])
                E.wait(sync, ("c_ep_out", tq, tt))
                cho = "s_out" if gtt % 2 == 0 else "s_out1"
                E.dinc(cho, sync.dma_start(
                    out=out_t[gtt * 128:(gtt + 1) * 128, :],
                    in_=xct[tt % 2][:]))
                E.mark_d(("c_out", tq, tt), cho)
        wdma(sync, None, ("c_out", TQ - 1, 3))
        wdma(sync, None, ("c_out", TQ - 1, 2))

    def body_gpsimd(gp):
        for j, (key, n) in enumerate(p0_slices):
            par = j % 2
            dst, src = p0_aps(key)
            wdma(gp, None, ("p0_out", j - 2))
            E.dinc(f"s_pi{par}", gp.dma_start(out=p0st[par][:, 0:n], in_=src))
            E.wait_ge(gp, f"s_pi{par}", E.dma_counts[f"s_pi{par}"])
            E.dinc(f"s_po{par}", gp.dma_start(out=dst, in_=p0st[par][:, 0:n]))
            E.mark_d(("p0_out", j), f"s_po{par}")
            if key[0] == "wkv" and key[2] == 1:
                E.mark_d(("p0_wkv", key[1], 0), "s_po0")
                E.mark_d(("p0_wkv", key[1], 1), "s_po1")
            if key[0] == "wq":
                E.mark_d(("p0_wqA", key[1], 0), "s_po0")
                E.mark_d(("p0_wqA", key[1], 1), "s_po1")
            if key[0] == "wo" and key[1] == DC - 1:
                E.mark_d(("p0_wo", 0), "s_po0")
                E.mark_d(("p0_wo", 1), "s_po1")
        E.mark_d(("p0_all", 0), "s_po0")
        E.mark_d(("p0_all", 1), "s_po1")

        E.wait(gp, "fin_t_a2")
        E.wait(gp, "fin_v_a2")
        E.wait(gp, "fin_c_a2")
        for ch in ("s_xnsp", "s_xnsp1", "s_wdsp", "s_xld0", "s_xld1"):
            wdma(gp, None, ("cpdma", ch))
        for which in ("k", "v"):
            for par in range(2):
                wdma(gp, None, ("a_spfin", which, par))
        for par in range(2):
            wdma(gp, None, ("a2_qspfin", par))
        for tq in range(TQ):
            for grp_i, grp in enumerate(cfg.groups):
                for e in grp:
                    for ih in range(2):
                        jslot = (e % 2) * 2 + ih
                        E.wait(gp, ("c_w1_usedG", GE[(tq, e)] - 2))
                        if tq == 0 and grp_i == 0 and e == grp[0] and ih == 0:
                            wdma(gp, None, ("p0_all", 0))
                            wdma(gp, None, ("p0_all", 1))
                        ch = f"s_w1l{jslot}"
                        E.dinc(ch, gp.dma_start(out=w1sb[jslot][:], in_=w1b_sp[e, ih]))
                        if E.pass1:
                            E.marks[("c_w1l", tq, e, ih)] = (ch, E.dma_counts[ch])
                for dc in range(DC):
                    E.wait(gp, ("c_w2_usedG", GW2[(tq, grp_i, dc)] - 2))
                    e0, e1 = grp[0], grp[-1] + 1
                    ch = f"s_w2l{dc % 2}"
                    E.dinc(ch, gp.dma_start(
                        out=w2sbs[dc % 2][:, 0:(e1 - e0) * IC * 128],
                        in_=w2b_sp[dc, :, e0 * IC * 128:e1 * IC * 128]))
                    if E.pass1:
                        E.marks[("c_w2l", tq, grp_i, dc)] = (ch, E.dma_counts[ch])

    def body_tensor(t):
        E.wait_ge(t, "s_const", 224)
        E.wait_ge(t, "s_gw", 16)
        ins = t.matmul(pC[:, 0:E_R], onesf[:], gbrow[:], start=True, stop=True)
        E.inc("t_a", ins)
        E.mark("gbb_mm", "t_a")

        for mb in range(MB):
            for jj in range(4):
                j = mb * 4 + jj
                E.wait(t, ("a_xnb", j))
                E.wait(t, ("a_trevac", j - 1))
                for dc in range(DC):
                    dst = (vbfA if dc < 8 else vbfB)[:, (dc % 8) * 128:(dc % 8 + 1) * 128]
                    ins = t.transpose(dst, xnb[j % 2][:, dc * 128:(dc + 1) * 128], idb[:])
                    if dc == 7 or dc == 15:
                        E.inc("t_a", ins)
                        E.mark(("a_tr", j, dc // 8), "t_a")
                E.mark(("a_tile_done", j), "t_a")
            for h in range(NH):
                gi = mb * NH + h
                if gi > 0:
                    E.wait(t, ("a_vtevac", gi - 1))
                    E.wait(t, ("a_vtrevac", gi - 2))
                    for i4 in range(4):
                        ins = t.transpose(vbfC[:, i4 * 128:(i4 + 1) * 128],
                                          vtst[(gi - 1) % 2][:, i4 * 128:(i4 + 1) * 128],
                                          idb[:])
                    E.inc("t_a", ins)
                    E.mark(("a_vtr", gi - 1), "t_a")
                wdma(t, None, ("wkvl", gi))
                E.wait(t, ("a_stage", mb))
                wslc = wkv_t[gi % 4]
                for wi, which in enumerate(("k", "v")):
                    dstp = [pA[:, 0:512], pA[:, 512:1024]][wi]
                    E.wait(t, ("a_evacd", gi - 1, which))
                    for dc in range(DC):
                        ins = t.matmul(dstp,
                                       wslc[:, wi * D + dc * 128:wi * D + dc * 128 + 128],
                                       stg[mb % 2][:, dc * 512:(dc + 1) * 512],
                                       start=(dc == 0), stop=(dc == DC - 1))
                    E.inc("t_a", ins)
                    E.mark(("a_mm", gi, which), "t_a")
                E.mark(("a_vmm", gi), "t_a")
            E.mark(("a_blk_done", mb), "t_a")
        gi = MB * NH
        E.wait(t, ("a_vtevac", gi - 1))
        E.wait(t, ("a_vtrevac", gi - 2))
        for i4 in range(4):
            ins = t.transpose(vbfC[:, i4 * 128:(i4 + 1) * 128],
                              vtst[(gi - 1) % 2][:, i4 * 128:(i4 + 1) * 128], idb[:])
        E.inc("t_a", ins)
        E.mark(("a_vtr", gi - 1), "t_a")
        E.mark("fin_t_a", "t_a")

        E.wait(t, "fin_v_a")
        E.wait(t, "fin_c_a")
        for mb in range(OMB):
            for jj in range(4):
                j = mb * 4 + jj
                E.wait(t, ("cp_xnb", j))
                E.wait(t, ("cp_trevacL", j - 1))
                for dc in range(DC):
                    dst = (vbfA if dc < 8 else vbfB)[:, (dc % 8) * 128:(dc % 8 + 1) * 128]
                    ins = t.transpose(dst, cp_hi[j % 2][:, dc * 128:(dc + 1) * 128], idb[:])
                    if dc == 7 or dc == 15:
                        E.inc("t_p", ins)
                        E.mark(("cp_trh", j, dc // 8), "t_p")
                E.wait(t, ("cp_lo", j))
                E.wait(t, ("cp_trevacH", j))
                for dc in range(DC):
                    dst = (vbfA if dc < 8 else vbfB)[:, (dc % 8) * 128:(dc % 8 + 1) * 128]
                    ins = t.transpose(dst, cp_lo[j % 2][:, dc * 128:(dc + 1) * 128], idb[:])
                    if dc == 7 or dc == 15:
                        E.inc("t_p", ins)
                        E.mark(("cp_trl", j, dc // 8), "t_p")
                E.mark(("cp_tile_done", j), "t_p")
            E.wait(t, ("cp_stage_hl", mb))
            for h in range(NH):
                qi = mb * NH + h
                wdma(t, None, ("wql", qi))
                E.wait(t, ("a2_qevacd", qi - 1))
                for dc in range(DC):
                    ins = t.matmul(pA[:, 0:512],
                                   wq_t[qi % 2][:, dc * 128:(dc + 1) * 128],
                                   cp_stg[mb % 2][:, dc * 512:(dc + 1) * 512],
                                   start=(dc == 0), stop=(dc == DC - 1))
                E.inc("t_p", ins)
                E.mark(("a2_qmmD", qi), "t_p")
                E.mark(("a2_qmm", qi), "t_p")
            E.wait(t, ("cp_lgevac", mb - 1))
            E.wait(t, "gw_split")
            for dc in range(DC):
                t.matmul(pC[0:E_R, 0:512], gwhi[:, dc * E_R:(dc + 1) * E_R],
                         cp_stg[mb % 2][:, dc * 512:(dc + 1) * 512],
                         start=(dc == 0), stop=False)
                t.matmul(pC[0:E_R, 0:512], gwlo[:, dc * E_R:(dc + 1) * E_R],
                         cp_stg[mb % 2][:, dc * 512:(dc + 1) * 512],
                         start=False, stop=False)
                ins = t.matmul(pC[0:E_R, 0:512], gwhi[:, dc * E_R:(dc + 1) * E_R],
                               cp_loT[:, dc * 512:(dc + 1) * 512],
                               start=False, stop=(dc == DC - 1))
            E.inc("t_p", ins)
            E.mark(("cp_lg", mb), "t_p")
            for jj in range(4):
                j = mb * 4 + jj
                E.wait(t, ("cp_lgTevac", mb))
                E.wait(t, ("cp_lgtokevac", j - 1))
                ins = t.transpose(pB[0:128, 0:E_R], lgT[:, jj * 128:(jj + 1) * 128],
                                  idf[0:E_R, 0:E_R])
                E.inc("t_p", ins)
                E.mark(("cp_lgtr", j), "t_p")
                E.wait(t, ("cp_wd", j))
                E.wait(t, ("cp_wdevac", j - 1))
                ins = t.transpose(pB[0:E_R, 512:640], cp_t1[:, 0:E_R], idf[:])
                E.inc("t_p", ins)
                E.mark(("cp_wdtr", j), "t_p")
        E.mark("fin_t_a2", "t_p")
        E.mark("fin_t_cp", "t_p")

        E.wait(t, "fin_v_a2")
        E.wait(t, "fin_c_a2")
        for ch in ("s_xnsp", "s_xnsp1", "s_wdsp", "s_xld0", "s_xld1"):
            wdma(t, None, ("cpdma", ch))
        for par in range(2):
            wdma(t, None, ("a2_qspfin", par))
        st = {"np2": 0, "npe": 0, "nwo": 0, "np1_last": -1}

        def emit_pv(qc, h, mt):
            nbs = B_STEP[(qc, h, mt)]
            gh = qc * NH + h
            nmt = ext_of(qc) // 128
            st_ = mt == 0
            sp_ = mt == nmt - 1
            E.wait(t, ("b_expd", nbs))
            if st_:
                E.wait(t, ("b_zn", gh - 1))
                E.wait(t, ("b_recipd", gh - 1))
            g4 = GQ[(qc, h, mt // 4)]
            t.matmul(pC[:, 512:1024],
                     vqb[g4 % 4][:, (mt % 4) * 128:(mt % 4 + 1) * 128],
                     Ebuf[nbs % 2][:], start=st_, stop=sp_)
            ins2 = t.matmul(pD[0:1, 0:512], onesb[:],
                            Ebuf[nbs % 2][:], start=st_, stop=sp_)
            E.inc("t_b", ins2)
            E.mark(("b_pv", nbs), "t_b")
            if sp_ or mt % 4 == 3:
                E.mark(("b_qdone", g4), "t_b")

        for it in BC_SCHED:
            kind = it[0]
            if kind == "b_mt":
                _, qc, h, mt = it
                gh = qc * NH + h
                nbs = B_STEP[(qc, h, mt)]
                nmt = ext_of(qc) // 128
                g4 = GQ[(qc, h, mt // 4)]
                if mt % 4 == 0:
                    wdma(t, None, ("b_kvl", g4))
                if mt == 0:
                    wdma(t, None, ("b_ql", gh))
                    if h == 0 and qc > 0:
                        E.wait(t, ("b_sT_evac", qc - 1, DC - 1))
                E.wait(t, ("b_expd", nbs - 1))
                ins = t.matmul(pC[:, 0:512],
                               kqb[g4 % 4][:, (mt % 4) * 128:(mt % 4 + 1) * 128],
                               qbuf[gh % 2][:], start=True, stop=True)
                E.inc("t_b", ins)
                E.mark(("b_sc", nbs), "t_b")
                if mt == nmt - 1:
                    E.mark(("b_scdone", gh), "t_b")
                if mt > 0:
                    emit_pv(qc, h, mt - 1)
            elif kind == "b_fin1":
                _, qc, h = it
                emit_pv(qc, h, ext_of(qc) // 128 - 1)
            elif kind == "b_fin2":
                _, qc, h = it
                gh = qc * NH + h
                E.wait(t, ("b_recip", gh))
                E.wait(t, ("pe_copy", st["npe"] - 1))
                ins = t.matmul(pE[:, 0:512], onesf[:], rsum, start=True, stop=True)
                E.inc("t_b", ins)
                E.mark(("pe_mm", st["npe"]), "t_b")
                st["npe"] += 1
            elif kind == "b_wo":
                _, qc, dc = it
                E.wait(t, ("b_zn", qc * NH + NH - 1))
                if st["nwo"] > 0:
                    pw = st["nwo"] - 1
                    E.wait(t, ("b_sT_evac", pw // DC, pw % DC))
                wdma(t, None, ("b_wol", qc, dc))
                for h in range(NH):
                    ins = t.matmul(pC[:, 0:512],
                                   wobd[dc % 2][:, h * 128:(h + 1) * 128],
                                   zAll[:, h * 512:(h + 1) * 512],
                                   start=(h == 0), stop=(h == NH - 1))
                E.inc("t_b", ins)
                E.mark(("b_wo", qc, dc), "t_b")
                E.mark(("b_woN", st["nwo"]), "t_b")
                st["nwo"] += 1
            elif kind == "c_xnq":
                wdma(t, None, ("c_xnq", it[1]))
            elif kind == "c_wbc":
                _, tq, e = it
                wdma(t, None, ("c_wdl", tq, e))
                E.wait(t, ("pe_copy", st["npe"] - 1))
                ins = t.matmul(pE[:, 0:512], onesf[:], wdrow[e % 2],
                               start=True, stop=True)
                E.inc("t_c", ins)
                E.mark(("pe_mm", st["npe"]), "t_c")
                E.mark(("c_wbcmmG", GWB[(tq, e)]), "t_c")
                st["npe"] += 1
            elif kind == "c_p1":
                _, tq, grp_i, e, i = it
                np1 = C_P1[it]
                ih = i // 2
                jslot = (e % 2) * 2 + ih
                wdma(t, None, ("c_w1l", tq, e, ih))
                E.wait(t, ("c_h_evac", np1 - 2))
                if np1 == P1_FIRST[tq] and tq > 0:
                    E.wait(t, ("c_ep_add", tq - 1, 3, 3))
                bank = [pA[:, 0:512], pA[:, 512:1024]][np1 % 2]
                for dc in range(DC):
                    col = dc * 2 * 128 + (i % 2) * 128
                    ins = t.matmul(bank, w1sb[jslot][:, col:col + 128],
                                   xnTq[:, dc * 512:(dc + 1) * 512],
                                   start=(dc == 0), stop=(dc == DC - 1))
                E.inc("t_c", ins)
                E.mark(("c_p1", np1), "t_c")
                if i == IC - 1:
                    E.mark(("c_w1_usedG", GE[(tq, e)]), "t_c")
                if np1 == P1_LAST[tq]:
                    E.mark(("c_p1_done", tq), "t_c")
                st["np1_last"] = np1
            elif kind == "c_p2":
                _, tq, grp_i, dc = it
                grp = cfg.groups[grp_i]
                np2 = st["np2"]
                wdma(t, None, ("c_w2l", tq, grp_i, dc))
                E.wait(t, ("c_h_evac", st["np1_last"]))
                E.wait(t, ("c_radd", np2 - 2))
                bank = [pB[:, 0:512], pB[:, 512:1024]][np2 % 2]
                for ki, e in enumerate(grp):
                    for i in range(IC):
                        col = (ki * IC + i) * 128
                        ins = t.matmul(bank, w2sbs[dc % 2][:, col:col + 128],
                                       hsb[ki][:, i * 512:(i + 1) * 512],
                                       start=(ki == 0 and i == 0),
                                       stop=(ki == len(grp) - 1 and i == IC - 1))
                E.inc("t_c", ins)
                E.mark(("c_p2", np2), "t_c")
                E.mark(("c_w2_usedG", GW2[(tq, grp_i, dc)]), "t_c")
                st["np2"] += 1
            elif kind == "c_ep":
                _, tq, tt = it
                gtt = tq * 4 + tt
                E.wait(t, ("c_ep_mul", tq, tt))
                if tt == 0:
                    E.wait(t, ("c_h_evac", P1_LAST[tq]))
                for dcg in range(4):
                    gg = gtt * 4 + dcg
                    E.wait(t, ("c_ep_add", (gg - 2) // 16, ((gg - 2) % 16) // 4,
                               (gg - 2) % 4) if gg >= 2 else ("x", -1))
                    bank = [pA[:, 0:512], pA[:, 512:1024]][dcg % 2]
                    for q4 in range(4):
                        dc = dcg * 4 + q4
                        ins = t.transpose(bank[:, q4 * 128:(q4 + 1) * 128],
                                          sTtt[tt % 2][:, dc * 128:(dc + 1) * 128],
                                          idf[:])
                    E.inc("t_c", ins)
                    E.mark(("c_ep_tr", tq, tt, dcg), "t_c")

    def body_scalar(s):
        def ln_tile(j, ch, ldkey, tdkey, out_bf, out_f32, pfx):
            wdma(s, None, (ldkey, j))
            E.wait(s, (pfx + "_statsdone", j - 1))
            ins = s.activation(sqscr[:], xt[j % 2][:], AF.Square,
                               accum_out=lnc(j, 1))
            E.inc(ch, ins)
            E.mark((pfx + "_sq", j), ch)
            E.wait(s, (pfx + "_stats", j))
            ins = s.activation(lnc(j, 4), lnc(j, 3), AF.Sqrt)
            E.inc(ch, ins)
            E.mark((pfx + "_sqrt", j), ch)
            E.wait(s, (pfx + "_rnb", j))
            E.wait(s, (tdkey, j - 1))
            if out_f32 is not None:
                ins = s.activation(out_f32[:], xt[j % 2][:], AF.Identity,
                                   bias=lnc(j, 6), scale=lnc(j, 5))
                E.inc(ch, ins)
                E.mark((pfx + "_xnf", j), ch)
                E.wait_ge(s, ch, E.counts[ch])
                ins = s.copy(out_bf[:], out_f32[:])
                E.inc(ch, ins)
            else:
                ins = s.activation(out_bf[:], xt[j % 2][:], AF.Identity,
                                   bias=lnc(j, 6), scale=lnc(j, 5))
                E.inc(ch, ins)
            E.mark((pfx + "_xnb", j), ch)
            E.mark((pfx + "_xnb_s", j), ch)

        for j in range(NTI):
            ln_tile(j, "c_a", "xld", "a_tile_done", xnb[j % 2], None, "a")
            E.mark(("a_xnb", j), "c_a")
            E.mark(("a_xnb_s", j), "c_a")
        E.mark("fin_c_a", "c_a")

        E.wait(s, "fin_t_a")
        E.wait(s, "fin_v_a")
        for mb in range(OMB):
            for jj in range(4):
                j = mb * 4 + jj
                ln_tile(j, "c_p", "cp_xld", "cp_hi_done", cp_hi[j % 2], xnf, "cp")
                E.mark(("cp_xnb", j), "c_p")
                E.mark(("cp_xnb_s", j), "c_p")
            for jj in range(4):
                j = mb * 4 + jj
                E.wait(s, ("cp_lgtokevac", j))
                ins = s.activation(cp_sc[:], lgtok[:], AF.Sigmoid)
                E.inc("c_p", ins)
                E.mark(("cp_sig", j), "c_p")
        E.mark("fin_c_a2", "c_p")
        E.mark("fin_c_cp", "c_p")

        E.wait(s, "fin_t_a2")
        E.wait(s, "fin_v_a2")
        for chx in ("s_xnsp", "s_xnsp1", "s_wdsp", "s_xld0", "s_xld1"):
            wdma(s, None, ("cpdma", chx))
        npe = 0
        for it in BC_SCHED:
            kind = it[0]
            if kind == "b_mt":
                _, qc, h, mt = it
                nbs = B_STEP[(qc, h, mt)]
                nmt = ext_of(qc) // 128
                if mt >= nmt - 8:
                    E.wait(s, ("b_mask", nbs))
                else:
                    E.wait(s, ("b_sc", nbs))
                E.wait(s, ("b_pv", nbs - 2))
                ins = s.activation(Ebuf[nbs % 2][:], pC[:, 0:512], AF.Exp,
                                   scale=SCALE)
                E.inc("c_b", ins)
                E.mark(("b_expd", nbs), "c_b")
            elif kind == "b_fin2":
                _, qc, h = it
                gh = qc * NH + h
                E.wait(s, ("pe_mm", npe))
                E.wait(s, ("b_zn", gh - 2))
                ins = s.copy(rbc2[h % 2][:], pE[:, 0:512])
                E.inc("c_b", ins)
                E.mark(("b_rbccopy", gh), "c_b")
                E.mark(("pe_copy", npe), "c_b")
                npe += 1
            elif kind == "c_wbc":
                _, tq, e = it
                gwb = GWB[(tq, e)]
                E.wait(s, ("pe_mm", npe))
                E.wait(s, ("c_wbc_usedG", gwb - 2))
                ins = s.copy(wbc[e % 2][:], pE[:, 0:512])
                E.inc("c_c", ins)
                E.mark(("c_wbccopy", tq, e), "c_c")
                E.mark(("pe_copy", npe), "c_c")
                npe += 1
            elif kind == "c_ep" and it[2] == 0:
                tq = it[1]
                E.wait(s, ("c_radd_all", tq))
                ins = s.activation(routedT[:], routedT[:], AF.Silu)
                E.inc("c_c", ins)
                E.mark(("c_silu", tq), "c_c")

    def body_vector(v):
        E.wait_ge(v, "s_gw", 16)
        ins = v.tensor_copy(gwhi[:], gwf[:])
        c = E.inc("v_a", ins)
        E.wait_ge(v, "v_a", c)
        ins = v.tensor_tensor(gwlo[:], gwf[:], gwhi[:], ALU.subtract)
        E.inc("v_a", ins)
        E.mark("gw_split", "v_a")
        E.wait(v, "gbb_mm")
        ins = v.tensor_copy(gbbc[:], pC[:, 0:E_R])
        E.inc("v_a", ins)
        E.mark("gbb_done", "v_a")

        def ln_stats(j, ch, ldkey, pfx):
            wdma(v, None, (ldkey, j))
            E.wait(v, (pfx + "_xnb_s", j - 2))
            ins = v.reduce_sum(lnc(j, 0), xt[j % 2][:], axis=AXX)
            c = E.inc(ch, ins)
            E.wait(v, (pfx + "_sq", j))
            E.wait_ge(v, ch, c)
            ins = v.tensor_scalar(lnc(j, 2), lnc(j, 0), 1.0 / D, None, ALU.mult)
            c = E.inc(ch, ins)
            ins = v.tensor_scalar(lnc(j, 7), lnc(j, 1), 1.0 / D, None, ALU.mult)
            c = E.inc(ch, ins)
            E.wait_ge(v, ch, c)
            ins = v.tensor_scalar(lnc(j, 3), lnc(j, 2), lnc(j, 2), -1.0,
                                  ALU.mult, ALU.mult)
            c = E.inc(ch, ins)
            E.wait_ge(v, ch, c)
            ins = v.tensor_tensor(lnc(j, 3), lnc(j, 3), lnc(j, 7), ALU.add)
            c = E.inc(ch, ins)
            E.wait_ge(v, ch, c)
            ins = v.tensor_scalar(lnc(j, 3), lnc(j, 3), 1e-5, None, ALU.add)
            E.inc(ch, ins)
            E.mark((pfx + "_stats", j), ch)
            E.wait(v, (pfx + "_sqrt", j))
            ins = v.reciprocal(lnc(j, 5), lnc(j, 4))
            c = E.inc(ch, ins)
            E.wait_ge(v, ch, c)
            ins = v.tensor_scalar(lnc(j, 6), lnc(j, 2), lnc(j, 5), -1.0,
                                  ALU.mult, ALU.mult)
            E.inc(ch, ins)
            E.mark((pfx + "_rnb", j), ch)
            E.mark((pfx + "_statsdone", j), ch)

        for mb in range(MB):
            for jj in range(4):
                j = mb * 4 + jj
                ln_stats(j, "v_a", "xld", "a")
                for half in range(2):
                    E.wait(v, ("a_tr", j, half))
                    E.wait(v, ("a_blk_done", mb - 2))
                    src = [vbfA, vbfB][half]
                    dst = stg[mb % 2][:].rearrange("p (c s) -> p c s", c=DC)[
                        :, half * 8:(half + 1) * 8, jj * 128:(jj + 1) * 128]
                    ins = v.tensor_copy(dst, src[:].rearrange("p (c s) -> p c s", c=8))
                    E.inc("v_a", ins)
                E.mark(("a_trevac", j), "v_a")
            E.mark(("a_stage", mb), "v_a")
            for h in range(NH):
                gi = mb * NH + h
                for wi, which in enumerate(("k", "v")):
                    E.wait(v, ("a_mm", gi, which))
                    src = [pA[:, 0:512], pA[:, 512:1024]][wi]
                    if which == "v":
                        ins = v.tensor_copy(vtst[gi % 2][:], src)
                        E.inc("v_a", ins)
                        E.mark(("a_vtevac", gi), "v_a")
                    else:
                        wdma(v, None, ("a_spill", gi - 2, which))
                        st_idx = wi * 2 + gi % 2
                        ins = v.tensor_copy(kvst[st_idx][:], src)
                        E.inc("v_a", ins)
                        E.mark(("a_evac", gi, which), "v_a")
                    E.mark(("a_evacd", gi, which), "v_a")
                if gi > 0:
                    E.wait(v, ("a_vtr", gi - 1))
                    wdma(v, None, ("a_spill", gi - 3, "v"))
                    st_idx = 2 + (gi - 1) % 2
                    ins = v.tensor_copy(kvst[st_idx][:], vbfC[:, 0:512])
                    E.inc("v_a", ins)
                    E.mark(("a_evac", gi - 1, "v"), "v_a")
                    E.mark(("a_vtrevac", gi - 1), "v_a")
        gi = MB * NH
        E.wait(v, ("a_vtr", gi - 1))
        wdma(v, None, ("a_spill", gi - 3, "v"))
        ins = v.tensor_copy(kvst[2 + (gi - 1) % 2][:], vbfC[:, 0:512])
        E.inc("v_a", ins)
        E.mark(("a_evac", gi - 1, "v"), "v_a")
        E.mark("fin_v_a", "v_a")

        E.wait(v, "fin_t_a")
        E.wait(v, "fin_c_a")
        for which in ("k", "v"):
            for par in range(2):
                wdma(v, None, ("a_spfin", which, par))
        for mb in range(OMB):
            for jj in range(4):
                j = mb * 4 + jj
                ln_stats(j, "v_p", "cp_xld", "cp")
                E.wait(v, ("cp_xnb", j))
                E.wait(v, ("cp_tile_done", j - 1))
                ins = v.tensor_tensor(cp_lo[j % 2][:], xnf[:], cp_hi[j % 2][:],
                                      ALU.subtract)
                E.inc("v_p", ins)
                E.mark(("cp_lo", j), "v_p")
                E.mark(("cp_hi_done", j), "v_p")
                for half in range(2):
                    E.wait(v, ("cp_trh", j, half))
                    E.wait(v, ("cp_lg", mb - 2))
                    E.wait(v, ("a2_qmmD", (mb - 2) * NH + NH - 1))
                    wdma(v, None, ("cp_xnsp", mb - 2))
                    src = [vbfA, vbfB][half]
                    dst = cp_stg[mb % 2][:].rearrange("p (c s) -> p c s", c=DC)[
                        :, half * 8:(half + 1) * 8, jj * 128:(jj + 1) * 128]
                    ins = v.tensor_copy(dst, src[:].rearrange("p (c s) -> p c s", c=8))
                    E.inc("v_p", ins)
                E.mark(("cp_trevacH", j), "v_p")
                for half in range(2):
                    E.wait(v, ("cp_trl", j, half))
                    E.wait(v, ("cp_lg", mb - 1))
                    src = [vbfA, vbfB][half]
                    dst = cp_loT[:].rearrange("p (c s) -> p c s", c=DC)[
                        :, half * 8:(half + 1) * 8, jj * 128:(jj + 1) * 128]
                    ins = v.tensor_copy(dst, src[:].rearrange("p (c s) -> p c s", c=8))
                    E.inc("v_p", ins)
                E.mark(("cp_trevacL", j), "v_p")
                if jj == 3:
                    E.mark(("cp_stage_hl", mb), "v_p")
            for h in range(NH):
                qi = mb * NH + h
                E.wait(v, ("a2_qmm", qi))
                wdma(v, None, ("a2_qspill", qi - 2))
                ins = v.tensor_copy(qst[qi % 2][:], pA[:, 0:512])
                E.inc("v_p", ins)
                E.mark(("a2_qevac", qi), "v_p")
                E.mark(("a2_qevacd", qi), "v_p")
            E.wait(v, ("cp_lg", mb))
            E.wait(v, ("cp_lgtr", 4 * mb - 1))
            ins = v.tensor_copy(lgT[:], pC[0:E_R, 0:512])
            E.inc("v_p", ins)
            E.mark(("cp_lgTevac", mb), "v_p")
            E.mark(("cp_lgevac", mb), "v_p")
            for jj in range(4):
                j = mb * 4 + jj
                E.wait(v, ("cp_lgtr", j))
                E.wait(v, ("cp_sig", j - 1))
                ins = v.tensor_copy(lgtok[:], pB[0:128, 0:E_R])
                E.inc("v_p", ins)
                E.mark(("cp_lgtokevac", j), "v_p")

                def vch(ins_):
                    cc = E.inc("v_p", ins_)
                    E.wait_ge(v, "v_p", cc)
                    return cc

                E.wait(v, ("cp_sig", j))
                E.wait(v, "gbb_done")
                vch(v.tensor_tensor(cp_sb[:], cp_sc[:], gbbc[:], ALU.add))
                for g_ in range(cfg.G_GROUPS):
                    sl = cp_sb[:, g_ * GSZ:(g_ + 1) * GSZ]
                    vch(v.reduce_max(cp_cols[:, 0:1], sl, axis=AXX))
                    vch(v.tensor_scalar(cp_t8[:], sl, cp_cols[:, 0:1], None, ALU.is_ge))
                    vch(v.scalar_tensor_tensor(cp_t8[:], cp_t8[:], -1e9, sl,
                                               ALU.mult, ALU.add))
                    vch(v.reduce_max(cp_cols[:, 1:2], cp_t8[:], axis=AXX))
                    vch(v.tensor_tensor(cp_g1[:, g_:g_ + 1], cp_cols[:, 0:1],
                                        cp_cols[:, 1:2], ALU.add))
                vch(v.tensor_copy(cp_g2[:], cp_g1[:]))
                vch(v.memset(cp_gm[:], 0.0))
                for it in range(cfg.TOPK_G):
                    vch(v.reduce_max(cp_cols[:, 2:3], cp_g2[:], axis=AXX))
                    vch(v.tensor_scalar(cp_g1[:], cp_g2[:], cp_cols[:, 2:3], None,
                                        ALU.is_ge))
                    vch(v.tensor_tensor(cp_gm[:], cp_gm[:], cp_g1[:], ALU.max))
                    vch(v.scalar_tensor_tensor(cp_g2[:], cp_g1[:], -1e9, cp_g2[:],
                                               ALU.mult, ALU.add))
                for g_ in range(cfg.G_GROUPS):
                    vch(v.tensor_scalar(cp_msk[:, g_ * GSZ:(g_ + 1) * GSZ], ones8[:],
                                        cp_gm[:, g_:g_ + 1], None, ALU.mult))
                vch(v.tensor_scalar(cp_t1[:], cp_msk[:], 1.0, 1e9,
                                    ALU.subtract, ALU.mult))
                vch(v.tensor_tensor(cp_t1[:], cp_sb[:], cp_t1[:], ALU.add))
                vch(v.memset(cp_sel[:], 0.0))
                for it in range(cfg.TOPK):
                    vch(v.reduce_max(cp_cols[:, 3:4], cp_t1[:], axis=AXX))
                    vch(v.tensor_scalar(cp_msk[:], cp_t1[:], cp_cols[:, 3:4], None,
                                        ALU.is_ge))
                    vch(v.tensor_tensor(cp_sel[:], cp_sel[:], cp_msk[:], ALU.max))
                    vch(v.scalar_tensor_tensor(cp_t1[:], cp_msk[:], -1e9, cp_t1[:],
                                               ALU.mult, ALU.add))
                vch(v.tensor_tensor(cp_sel[:], cp_sc[:], cp_sel[:], ALU.mult))
                vch(v.reduce_sum(cp_cols[:, 4:5], cp_sel[:], axis=AXX))
                vch(v.tensor_scalar(cp_cols[:, 5:6], cp_cols[:, 4:5], 1e-20, None,
                                    ALU.add))
                vch(v.reciprocal(cp_cols[:, 6:7], cp_cols[:, 5:6]))
                ins = v.tensor_scalar(cp_t1[:], cp_sel[:], cp_cols[:, 6:7], cfg.RSF,
                                      ALU.mult, ALU.mult)
                E.inc("v_p", ins)
                E.mark(("cp_wd", j), "v_p")
                E.wait(v, ("cp_wdtr", j))
                ins = v.tensor_copy(wdT_sb[:, j * 128:(j + 1) * 128],
                                    pB[0:E_R, 512:640])
                E.inc("v_p", ins)
                E.mark(("cp_wdevac", j), "v_p")
        E.mark("cp_wd_done", "v_p")
        E.mark("fin_v_a2", "v_p")
        E.mark("fin_v_cp", "v_p")

        E.wait(v, "fin_t_a2")
        E.wait(v, "fin_c_a2")
        for chx in ("s_xnsp", "s_xnsp1", "s_wdsp", "s_xld0", "s_xld1"):
            wdma(v, None, ("cpdma", chx))
        for par in range(2):
            wdma(v, None, ("p0_all", par))
        np1 = 0
        np2 = 0
        nwo = 0
        for it in BC_SCHED:
            kind = it[0]
            if kind == "b_mt":
                _, qc, h, mt = it
                nbs = B_STEP[(qc, h, mt)]
                nmt = ext_of(qc) // 128
                if mt >= nmt - 8:
                    E.wait(v, ("b_sc", nbs))
                    ins = v.tensor_tensor(pC[:, 0:512], pC[:, 0:512],
                                          bmk[mt - (nmt - 8)][:], ALU.add)
                    E.inc("v_b", ins)
                    E.mark(("b_mask", nbs), "v_b")
            elif kind == "b_fin1":
                _, qc, h = it
                gh = qc * NH + h
                E.wait(v, ("b_pv", B_LAST[(qc, h)]))
                ins = v.reciprocal(rsum, pD[0:1, 0:512])
                E.inc("v_b", ins)
                E.mark(("b_recip", gh), "v_b")
                E.mark(("b_recipd", gh), "v_b")
            elif kind == "b_fin2":
                _, qc, h = it
                gh = qc * NH + h
                E.wait(v, ("b_rbccopy", gh))
                if qc > 0:
                    E.wait(v, ("b_woN", qc * DC - 1))
                ins = v.tensor_tensor(zAll[:, h * 512:(h + 1) * 512],
                                      pC[:, 512:1024], rbc2[h % 2][:], ALU.mult)
                E.inc("v_b", ins)
                E.mark(("b_zn", gh), "v_b")
            elif kind == "b_wo":
                _, qc, dc = it
                E.wait(v, ("b_wo", qc, dc))
                gw = nwo - 2
                wdma(v, None, ("b_sT_sp", gw // DC, gw % DC) if gw >= 0 else ("x", -1))
                ins = v.tensor_copy(sTst[dc % 2][:], pC[:, 0:512])
                E.inc("v_b", ins)
                E.mark(("b_sT_evac", qc, dc), "v_b")
                nwo += 1
            elif kind == "c_p1":
                _, tq, grp_i, e, i = it
                routed = e < E_R
                E.wait(v, ("c_p1", np1))
                E.wait(v, ("c_h_evac", np1 - 1))
                if routed:
                    E.wait(v, ("c_wbccopy", tq, e))
                bank = [pA[:, 0:512], pA[:, 512:1024]][np1 % 2]
                a = rtA[np1 % 2]
                if routed:
                    ins = v.scalar_tensor_tensor(
                        a[:], bank, 0.0, wbc[e % 2][:], ALU.max, ALU.mult)
                else:
                    ins = v.tensor_scalar(a[:], bank, 0.0, None, ALU.max)
                cc = E.inc("v_c", ins)
                E.wait_ge(v, "v_c", cc)
                ki = cfg.groups[grp_i].index(e)
                ins = v.scalar_tensor_tensor(
                    hsb[ki][:, i * 512:(i + 1) * 512], bank, 0.0, a[:],
                    ALU.max, ALU.mult)
                E.inc("v_c", ins)
                E.mark(("c_h_evac", np1), "v_c")
                if routed and i == IC - 1:
                    E.mark(("c_wbc_usedG", GWB[(tq, e)]), "v_c")
                np1 += 1
            elif kind == "c_p2":
                _, tq, grp_i, dc = it
                E.wait(v, ("c_p2", np2))
                bank = [pB[:, 0:512], pB[:, 512:1024]][np2 % 2]
                dst = routedT[:, dc * 512:(dc + 1) * 512]
                if grp_i == 0:
                    E.wait(v, ("c_ep_mul", tq - 1, 3))
                    E.wait(v, ("c_silu", tq - 1))
                    ins = v.tensor_copy(dst, bank)
                else:
                    ins = v.tensor_tensor(dst, dst, bank, ALU.add)
                E.inc("v_c", ins)
                E.mark(("c_radd", np2), "v_c")
                np2 += 1
                if grp_i == len(cfg.groups) - 1 and dc == DC - 1:
                    E.mark(("c_radd_all", tq), "v_c")
            elif kind == "c_ep":
                _, tq, tt = it
                E.wait(v, ("c_silu", tq))
                wdma(v, None, ("c_sTl", tq, tt))
                rsl = routedT[:].rearrange("p (c s) -> p c s", c=DC)[
                    :, :, tt * 128:(tt + 1) * 128]
                dtt = sTtt[tt % 2][:].rearrange("p (c s) -> p c s", c=DC)
                ins = v.tensor_tensor(dtt, rsl, dtt, ALU.mult)
                E.inc("v_c", ins)
                E.mark(("c_ep_mul", tq, tt), "v_c")
                wdma(v, None, ("c_xc", tq, tt))
                for dcg in range(4):
                    E.wait(v, ("c_ep_tr", tq, tt, dcg))
                    bank = [pA[:, 0:512], pA[:, 512:1024]][dcg % 2]
                    xsl = xct[tt % 2][:, dcg * 512:(dcg + 1) * 512]
                    ins = v.tensor_tensor(xsl, bank, xsl, ALU.add)
                    E.inc("v_c", ins)
                    E.mark(("c_ep_add", tq, tt, dcg), "v_c")
                E.mark(("c_ep_out", tq, tt), "v_c")

    def run_all(engs):
        body_sync(engs["sync"])
        body_gpsimd(engs["gpsimd"])
        body_tensor(engs["tensor"])
        body_scalar(engs["scalar"])
        body_vector(engs["vector"])

    E.pass1 = True
    run_all({k: _Noop() for k in ("sync", "gpsimd", "tensor", "scalar", "vector")})
    E.pass1 = False
    E.reset()

    with nc.Block() as block:
        @block.sync
        def _(sync):
            body_sync(sync)

        @block.gpsimd
        def _(gp):
            body_gpsimd(gp)

        @block.tensor
        def _(t):
            body_tensor(t)

        @block.scalar
        def _(s):
            body_scalar(s)

        @block.vector
        def _(v):
            body_vector(v)

    ctx.close()
    if E.missing:
        import sys
        print("MISSING MARKS:", dict(list(E.missing.items())[:20]), file=sys.stderr)
    return nc



def host_images(cfg, W_Q, W_K, W_V, W_O, gate_w, w1, w2, shared_up, shared_down):
    DC, IC, NE, E_R = cfg.DC, cfg.IC, cfg.NE, cfg.E_R
    D = cfg.D

    def qkv_img(W):
        Hn = W.shape[0]
        r = W.reshape(Hn, 128, DC, 128)
        return np.ascontiguousarray(r.transpose(0, 3, 2, 1).reshape(Hn, 128, DC * 128))

    wq_img = qkv_img(W_Q)
    wk_img = qkv_img(W_K)
    wv_img = qkv_img(W_V)
    Hn = W_O.shape[0]
    r = W_O.reshape(Hn, DC, 128, 128)
    wo_img = np.ascontiguousarray(r.transpose(1, 3, 0, 2).reshape(DC, 128, Hn * 128))

    w1a = np.concatenate([w1, shared_up.reshape(4, 512, D)], axis=0)
    r = w1a.transpose(0, 2, 1)
    r = r.reshape(NE, DC, 128, 2, 2, 128)
    w1f = np.ascontiguousarray(
        r.transpose(0, 3, 2, 1, 4, 5).reshape(NE, 2, 128, DC * 2 * 128))

    w2a = np.concatenate(
        [w2, shared_down.reshape(D, 4, 512).transpose(1, 0, 2)], axis=0)
    r = w2a.transpose(0, 2, 1)
    r = r.reshape(NE, IC, 128, DC, 128)
    w2f = np.ascontiguousarray(
        r.transpose(3, 2, 0, 1, 4).reshape(DC, 128, NE * IC * 128))

    r = gate_w.reshape(E_R, DC, 128)
    gw_img = np.ascontiguousarray(r.transpose(2, 1, 0).reshape(128, DC * E_R))

    pp, ff = np.meshgrid(np.arange(128), np.arange(512), indexing="ij")
    diag = [np.where(pp + 128 * k > ff, np.float32(-1e9), np.float32(0.0))
            for k in range(4)]
    zero = np.zeros((128, 512), np.float32)
    full = np.full((128, 512), np.float32(-1e9), np.float32)
    bmask = {
        0: np.ascontiguousarray(np.stack([zero] * 4 + diag)).astype(ml_dtypes.bfloat16),
        1: np.ascontiguousarray(np.stack(diag + [full] * 4)).astype(ml_dtypes.bfloat16),
    }
    return dict(wq=wq_img, wk=wk_img, wv=wv_img, wo=wo_img, w1f=w1f, w2f=w2f,
                gw=gw_img, bmask=bmask)


_BUILD_CACHE = {}


def prepare_in_maps(cfg, x, W_Q, W_K, W_V, W_O, gw, gb, w1, w2, shu, shd):
    img = host_images(cfg, W_Q, W_K, W_V, W_O, gw, w1, w2, shu, shd)
    idb = np.eye(128).astype(ml_dtypes.bfloat16)
    idf = np.eye(128).astype(np.float32)
    onesb = np.ones((128, 1), ml_dtypes.bfloat16)
    onesf = np.ones((1, 128), np.float32)
    ones8 = np.ones((128, cfg.GSZ), np.float32)
    in_maps = []
    for core in range(cfg.n_cores):
        b = core // 2
        hh = core % 2
        chunks = cfg.own_chunks(hh)
        xo = np.concatenate([x[b, c * 512:(c + 1) * 512] for c in chunks], axis=0)
        in_maps.append({
            "x_in": np.ascontiguousarray(x[b]),
            "xo_in": np.ascontiguousarray(xo),
            "wq_in": img["wq"],
            "wk_in": img["wk"],
            "wv_in": img["wv"],
            "wo_in": img["wo"],
            "w1f_in": img["w1f"],
            "w2f_in": img["w2f"],
            "gwT_in": img["gw"],
            "gb_in": gb.reshape(1, -1).astype(np.float32),
            "bmask_in": img["bmask"][hh],
            "idb_in": idb,
            "idf_in": idf,
            "onesb_in": onesb,
            "onesf_in": onesf,
            "ones8_in": ones8,
        })
    return in_maps


def assemble_out(cfg, results, B):
    out = np.empty((B, cfg.S, cfg.D), np.float32)
    for core in range(cfg.n_cores):
        b = core // 2
        hh = core % 2
        for i, c in enumerate(cfg.own_chunks(hh)):
            out[b, c * 512:(c + 1) * 512] = \
                results[core]["out_half"][i * 512:(i + 1) * 512]
    return out


def kernel(L0_attn_x, L0_attn_W_Q, L0_attn_W_K, L0_attn_W_V, L0_attn_W_O,
           L0_ffn_gate_w, L0_ffn_gate_bias, L0_ffn_w1, L0_ffn_w2,
           L0_ffn_shared_up, L0_ffn_shared_down):
    cfg = CFG()
    x = np.ascontiguousarray(np.asarray(L0_attn_x, np.float32))
    W_Q = np.asarray(L0_attn_W_Q, np.float32)
    W_K = np.asarray(L0_attn_W_K, np.float32)
    W_V = np.asarray(L0_attn_W_V, np.float32)
    W_O = np.asarray(L0_attn_W_O, np.float32)
    gw = np.asarray(L0_ffn_gate_w, np.float32)
    gb = np.asarray(L0_ffn_gate_bias, np.float32)
    w1 = np.asarray(L0_ffn_w1, np.float32)
    w2 = np.asarray(L0_ffn_w2, np.float32)
    shu = np.asarray(L0_ffn_shared_up, np.float32)
    shd = np.asarray(L0_ffn_shared_down, np.float32)

    if "nc" not in _BUILD_CACHE:
        _BUILD_CACHE["nc"] = build_program(cfg)
    nc = _BUILD_CACHE["nc"]
    in_maps = prepare_in_maps(cfg, x, W_Q, W_K, W_V, W_O, gw, gb, w1, w2, shu, shd)
    res = run_bass_kernel_spmd(nc, in_maps, core_ids=list(range(cfg.n_cores)))
    return assemble_out(cfg, res.results, x.shape[0])


# version 05 (final)
# speedup vs baseline: 1.2386x; 1.0530x over previous
import numpy as np
import ml_dtypes
from contextlib import ExitStack

import concourse.bass as bass
import concourse.mybir as mybir
from concourse.bass_utils import run_bass_kernel_spmd

F32 = mybir.dt.float32
BF16 = mybir.dt.bfloat16
AF = mybir.ActivationFunctionType
ALU = mybir.AluOpType
AXX = mybir.AxisListType.X
DBG_A = False


class CFG:
    def __init__(self, S=4096, E_R=64, G_GROUPS=8, TOPK_G=4, TOPK=6, GRP=4,
                 n_cores=8):
        self.S = S
        self.D = 2048
        self.DC = 16
        self.NH = 16
        self.E_R = E_R
        self.G_GROUPS = G_GROUPS
        self.GSZ = E_R // G_GROUPS
        self.TOPK_G = TOPK_G
        self.TOPK = TOPK
        self.RSF = 2.5
        self.NE = E_R + 4
        self.IC = 4
        self.MB = S // 512
        self.OWN = S // 2
        self.OC = self.OWN // 512
        self.TQ = self.OWN // 512
        self.TT = self.OWN // 128
        self.OMB = self.OWN // 512
        self.n_cores = n_cores
        self.groups = []
        e = 0
        while e < E_R:
            self.groups.append(list(range(e, min(e + GRP, E_R))))
            e += GRP
        self.groups.append(list(range(E_R, self.NE)))

    def own_chunks(self, hh):
        return [2 * i + (1 - hh) for i in range(self.OC)]


class Emitter:
    def __init__(self, sems):
        self.sems = sems
        self.pass1 = True
        self.marks = {}
        self.missing = {}
        self.reset()

    def reset(self):
        self.counts = {k: 0 for k in self.sems}
        self.dma_counts = {k: 0 for k in self.sems}

    def inc(self, ch, instr):
        self.counts[ch] += 1
        if not self.pass1 and instr is not None:
            instr.then_inc(self.sems[ch], 1)
        return self.counts[ch]

    def dinc(self, ch, instr):
        self.dma_counts[ch] += 16
        if not self.pass1 and instr is not None:
            instr.then_inc(self.sems[ch], 16)
        return self.dma_counts[ch]

    def mark(self, key, ch):
        if self.pass1:
            self.marks[key] = (ch, self.counts[ch])

    def mark_d(self, key, ch):
        if self.pass1:
            self.marks[key] = (ch, self.dma_counts[ch])

    def _miss(self, key):
        idxs = [k for k in key[1:] if isinstance(k, int)] if isinstance(key, tuple) else []
        if all(i >= 0 for i in idxs):
            self.missing[key] = self.missing.get(key, 0) + 1

    def wait(self, eng, key):
        if not self.pass1:
            if key not in self.marks:
                self._miss(key)
                return
            ch, v = self.marks[key]
            if v > 0:
                eng.wait_ge(self.sems[ch], v)

    def wait_ge(self, eng, ch, v):
        if not self.pass1 and v > 0:
            eng.wait_ge(self.sems[ch], v)


class _Noop:
    def __getattr__(self, name):
        def f(*a, **k):
            return None
        return f


def build_program(cfg: CFG):
    nc = bass.Bass("TRN2", target_bir_lowering=False,
                   detect_race_conditions=not DBG_A)
    S, D, DC, NH = cfg.S, cfg.D, cfg.DC, cfg.NH
    NE, IC, MB = cfg.NE, cfg.IC, cfg.MB
    OWN, TQ, TT, E_R, OMB, OC = cfg.OWN, cfg.TQ, cfg.TT, cfg.E_R, cfg.OMB, cfg.OC
    NTI = S // 128
    GSZ = cfg.GSZ
    SCALE = float(1.0 / np.sqrt(128.0))

    x_in = nc.dram_tensor("x_in", [S, D], F32, kind="ExternalInput").ap()
    xo_in = nc.dram_tensor("xo_in", [OWN, D], F32, kind="ExternalInput").ap()
    wq_in = nc.dram_tensor("wq_in", [NH, 128, D], F32, kind="ExternalInput").ap()
    wk_in = nc.dram_tensor("wk_in", [NH, 128, D], F32, kind="ExternalInput").ap()
    wv_in = nc.dram_tensor("wv_in", [NH, 128, D], F32, kind="ExternalInput").ap()
    wo_in = nc.dram_tensor("wo_in", [DC, 128, NH * 128], F32, kind="ExternalInput").ap()
    w1f_in = nc.dram_tensor("w1f_in", [NE, 2, 128, DC * 2 * 128], F32, kind="ExternalInput").ap()
    w2f_in = nc.dram_tensor("w2f_in", [DC, 128, NE * IC * 128], F32, kind="ExternalInput").ap()
    gwT_in = nc.dram_tensor("gwT_in", [128, DC * E_R], F32, kind="ExternalInput").ap()
    gb_in = nc.dram_tensor("gb_in", [1, E_R], F32, kind="ExternalInput").ap()
    bmask_in = nc.dram_tensor("bmask_in", [8, 128, 512], BF16, kind="ExternalInput").ap()
    idb_in = nc.dram_tensor("idb_in", [128, 128], BF16, kind="ExternalInput").ap()
    idf_in = nc.dram_tensor("idf_in", [128, 128], F32, kind="ExternalInput").ap()
    onesb_in = nc.dram_tensor("onesb_in", [128, 1], BF16, kind="ExternalInput").ap()
    onesf_in = nc.dram_tensor("onesf_in", [1, 128], F32, kind="ExternalInput").ap()
    ones8_in = nc.dram_tensor("ones8_in", [128, GSZ], F32, kind="ExternalInput").ap()
    out_t = nc.dram_tensor("out_half", [OWN, D], F32, kind="ExternalOutput").ap()

    kT_sp = nc.dram_tensor("kT_sp", [NH, 128, S], BF16, kind="Internal").ap()
    v_sp = nc.dram_tensor("v_sp", [NH, 128, S], BF16, kind="Internal").ap()
    q_sp = nc.dram_tensor("q_sp", [NH, 128, OWN], BF16, kind="Internal").ap()
    xnTo_sp = nc.dram_tensor("xnTo_sp", [128, DC * OWN], BF16, kind="Internal").ap()
    wdT_sp = nc.dram_tensor("wdT_sp", [E_R, OWN], F32, kind="Internal").ap()
    sT_sp = nc.dram_tensor("sT_sp", [128, DC * OWN], F32, kind="Internal").ap()
    wkvb_sp = nc.dram_tensor("wkvb_sp", [NH, 2, 128, D], BF16, kind="Internal").ap()
    wqb_sp = nc.dram_tensor("wqb_sp", [NH, 128, D], BF16, kind="Internal").ap()
    wob_sp = nc.dram_tensor("wob_sp", [DC, 128, NH * 128], BF16, kind="Internal").ap()

    ctx = ExitStack()
    sb = lambda n, s, d: ctx.enter_context(nc.sbuf_tensor(n, s, d))
    pst = lambda n, s, d=F32: ctx.enter_context(nc.psum_tensor(n, s, d))
    mksem = lambda n: ctx.enter_context(nc.semaphore(n))

    DCER = DC * E_R
    CB = sb("CB", [128, 4352 + 2 * DCER], BF16)
    bmk = [CB[:, i * 512:(i + 1) * 512] for i in range(8)]
    idb = CB[:, 4096:4224]
    onesb = CB[:, 4224:4225]
    gwhi = CB[:, 4352:4352 + DCER]
    gwlo = CB[:, 4352 + DCER:4352 + 2 * DCER]
    CF = sb("CF", [128, 256 + GSZ + 2 * E_R], F32)
    idf = CF[:, 0:128]
    onesf = CF[0:1, 128:256]
    ones8 = CF[:, 256:256 + GSZ]
    gbbc = CF[:, 256 + GSZ:256 + GSZ + E_R]
    gbrow = CF[0:1, 256 + GSZ + E_R:256 + GSZ + 2 * E_R]

    NBF = 55296
    NF32 = 21504
    ABF = sb("ABF", [128, NBF], BF16)
    AFL = sb("AFL", [128, NF32], F32)

    def bfv(off, n):
        assert off + n <= NBF, (off, n)
        return ABF[:, off:off + n], off + n

    def ffv(off, n):
        assert off + n <= NF32, (off, n)
        return AFL[:, off:off + n], off + n

    o = 0
    wkv_t = []
    for p in range(4):
        t_, o = bfv(o, 2 * D)
        wkv_t.append(t_)
    xnb = []
    for p in range(2):
        t_, o = bfv(o, D)
        xnb.append(t_)
    stg = []
    for p in range(2):
        t_, o = bfv(o, DC * 512)
        stg.append(t_)
    kvst = []
    for p in range(4):
        t_, o = bfv(o, 512)
        kvst.append(t_)
    vtst = []
    for p in range(2):
        t_, o = bfv(o, 512)
        vtst.append(t_)

    o = 0
    cp_hi = []
    for p in range(2):
        t_, o = bfv(o, D)
        cp_hi.append(t_)
    cp_lo = []
    for p in range(2):
        t_, o = bfv(o, D)
        cp_lo.append(t_)
    cp_stg = []
    for p in range(2):
        t_, o = bfv(o, DC * 512)
        cp_stg.append(t_)
    cp_loT, o = bfv(o, DC * 512)
    wq_t = []
    for p in range(2):
        t_, o = bfv(o, D)
        wq_t.append(t_)
    qst = []
    for p in range(2):
        t_, o = bfv(o, 512)
        qst.append(t_)

    n_hsb = max(len(g) for g in cfg.groups)
    o = 0
    kqb, vqb = [], []
    for p in range(4):
        t_, o = bfv(o, 512)
        kqb.append(t_)
    for p in range(4):
        t_, o = bfv(o, 512)
        vqb.append(t_)
    qbuf, Ebuf = [], []
    for p in range(2):
        t_, o = bfv(o, 512)
        qbuf.append(t_)
    for p in range(2):
        t_, o = bfv(o, 512)
        Ebuf.append(t_)
    zAll, o = bfv(o, NH * 512)
    wobd = []
    for p in range(2):
        t_, o = bfv(o, NH * 128)
        wobd.append(t_)
    xnTq, o = bfv(o, DC * 512)
    w1sb = []
    for j in range(4):
        t_, o = bfv(o, DC * 2 * 128)
        w1sb.append(t_)
    w2sbs = []
    for p in range(2):
        t_, o = bfv(o, n_hsb * IC * 128)
        w2sbs.append(t_)
    hsb = []
    for k in range(n_hsb):
        t_, o = bfv(o, IC * 512)
        hsb.append(t_)
    assert o <= NBF, o
    assert NBF - 8704 >= 39936
    p0st = [ABF[:, NBF - 8704:NBF - 4352], ABF[:, NBF - 4352:NBF]]

    o = 0
    xt = []
    for p in range(2):
        t_, o = ffv(o, D)
        xt.append(t_)
    sqscr, o = ffv(o, D)
    xnf, o = ffv(o, D)
    lncol, o = ffv(o, 32)
    A_F32_END = o

    def lnc(j, i):
        return lncol[:, (j % 2) * 16 + i:(j % 2) * 16 + i + 1]

    gwf = AFL[:, A_F32_END:A_F32_END + DC * E_R]

    o = A_F32_END
    lgT_t, o = ffv(o, 512)
    lgT = lgT_t[0:E_R, :]
    lgtok, o = ffv(o, E_R)
    cp_sc, o = ffv(o, E_R)
    cp_sb, o = ffv(o, E_R)
    cp_msk, o = ffv(o, E_R)
    cp_sel, o = ffv(o, E_R)
    cp_t1, o = ffv(o, E_R)
    cp_t8, o = ffv(o, GSZ)
    cp_g1, o = ffv(o, cfg.G_GROUPS)
    cp_g2, o = ffv(o, cfg.G_GROUPS)
    cp_gm, o = ffv(o, cfg.G_GROUPS)
    cp_cols, o = ffv(o, 8)
    wdT_t, o = ffv(o, OWN)
    wdT_sb = wdT_t[0:E_R, :]

    o = 0
    rbc_t, o = ffv(o, 1024)
    rbc2 = [rbc_t[:, 0:512], rbc_t[:, 512:1024]]
    rsum_t, o = ffv(o, 512)
    rsum = rsum_t[0:1, :]
    sTst = []
    for p in range(2):
        t_, o = ffv(o, 512)
        sTst.append(t_)
    routedT, o = ffv(o, DC * 512)
    wbc = []
    for p in range(2):
        t_, o = ffv(o, 512)
        wbc.append(t_)
    rtA_t, o = ffv(o, 512)
    rtA = [rtA_t, rtA_t]
    sTtt = []
    for p in range(2):
        t_, o = ffv(o, DC * 128)
        sTtt.append(t_)
    xct = []
    for p in range(2):
        t_, o = ffv(o, D)
        xct.append(t_)
    wdrow_t, o = ffv(o, 1024)
    wdrow = [wdrow_t[0:1, 0:512], wdrow_t[0:1, 512:1024]]
    assert o <= NF32, o

    pA = pst("pA", [128, 1024])
    pB = pst("pB", [128, 1024])
    pC = pst("pC", [128, 1024])
    pD = pst("pD", [128, 512])
    pE = pst("pE", [128, 512])
    vbfA = pD[:].bitcast(BF16)
    vbfB = pE[:].bitcast(BF16)
    vbfC = pB[:, 512:1024].bitcast(BF16)

    DSEM = ["s_const", "s_gw", "s_xld0", "s_xld1", "s_wkv0", "s_wkv1", "s_wkv2", "s_wkv3",
            "s_wq0", "s_wq1",
            "s_pi0", "s_pi1", "s_po0", "s_po1",
            "s_kv0", "s_kv1", "s_kv2", "s_kv3", "s_q0", "s_q1", "s_wo0", "s_wo1",
            "s_spk0", "s_spk1", "s_spv0", "s_spv1", "s_spq0", "s_spq1",
            "s_spt0", "s_spt1",
            "s_w1l0", "s_w1l1", "s_w1l2", "s_w1l3", "s_w2l0", "s_w2l1",
            "s_xnq", "s_sTl0", "s_sTl1", "s_xc0", "s_xc1",
            "s_out", "s_out1", "s_wd0", "s_wd1", "s_xnsp", "s_xnsp1", "s_wdsp"]
    CSEM = ["t_a", "v_a", "c_a", "t_b", "v_b", "c_b", "t_c", "v_c", "c_c",
            "t_p", "v_p", "c_p"]
    sems = {n: mksem(n) for n in DSEM + CSEM}
    E = Emitter(sems)

    def wdma(eng, ch, key):
        if not E.pass1:
            if key not in E.marks:
                E._miss(key)
                return
            chm, v = E.marks[key]
            if v > 0:
                eng.wait_ge(sems[chm], v)

    p0_slices = []
    for h in range(NH):
        p0_slices.append((("wkv", h, 0), 2048))
        p0_slices.append((("wkv", h, 1), 2048))
    for h in range(NH):
        p0_slices.append((("wq", h), 2048))
    for dc in range(DC):
        p0_slices.append((("wo", dc), NH * 128))

    def p0_aps(key):
        if key[0] == "wkv":
            _, h, j = key
            src = [wk_in, wv_in][j][h]
            dst = wkvb_sp[h, j]
        elif key[0] == "wq":
            src = wq_in[key[1]]
            dst = wqb_sp[key[1]]
        else:
            src = wo_in[key[1]]
            dst = wob_sp[key[1]]
        return dst, src

    def ext_of(qc):
        return 1024 * (qc + 1)


    def b_items(qc):
        out = []
        nmt = ext_of(qc) // 128
        nq = ext_of(qc) // 512
        for h in range(NH):
            out.append(("b_q", qc, h))
            out.append(("b_kvq", qc, h, 0))
            if nq > 1:
                out.append(("b_kvq", qc, h, 1))
            for mt in range(nmt):
                qq = mt // 4 + 2
                if mt % 4 == 0 and qq < nq:
                    out.append(("b_kvq", qc, h, qq))
                out.append(("b_mt", qc, h, mt))
            out.append(("b_fin1", qc, h))
            out.append(("b_fin2", qc, h))
        for dc in range(DC):
            out.append(("b_wo", qc, dc))
        return out

    def c_items(tq):
        out = [("c_xnq", tq)]
        for grp_i, grp in enumerate(cfg.groups):
            for e in grp:
                if e < E_R:
                    out.append(("c_wbc", tq, e))
                for i in range(IC):
                    out.append(("c_p1", tq, grp_i, e, i))
            for dc in range(DC):
                out.append(("c_p2", tq, grp_i, dc))
        for tt in range(4):
            out.append(("c_ep", tq, tt))
        return out

    BC_SCHED = list(b_items(0))
    for tq in range(TQ):
        ci = c_items(tq)
        bi = b_items(tq + 1) if tq + 1 < OC else []
        r = len(bi) / max(1, len(ci))
        acc = 0.0
        k = 0
        for it in ci:
            BC_SCHED.append(it)
            acc += r
            while k < len(bi) and acc >= 1.0:
                BC_SCHED.append(bi[k])
                k += 1
                acc -= 1.0
        BC_SCHED.extend(bi[k:])

    B_STEP = {}
    B_LAST = {}
    GQ = {}
    _n = _ng = 0
    for qc in range(OC):
        nmt = ext_of(qc) // 128
        for h in range(NH):
            for qq in range(ext_of(qc) // 512):
                GQ[(qc, h, qq)] = _ng
                _ng += 1
            for mt in range(nmt):
                B_STEP[(qc, h, mt)] = _n
                _n += 1
            B_LAST[(qc, h)] = _n - 1
    C_P1 = {}
    GWB = {}
    GE = {}
    GW2 = {}
    P1_FIRST = {}
    P1_LAST = {}
    _n1 = _nwb = _nge = _ng2 = 0
    for it in BC_SCHED:
        if it[0] == "c_p1":
            C_P1[it] = _n1
            P1_FIRST.setdefault(it[1], _n1)
            P1_LAST[it[1]] = _n1
            if it[4] == 0:
                GE[(it[1], it[3])] = _nge
                _nge += 1
            _n1 += 1
        elif it[0] == "c_wbc":
            GWB[(it[1], it[2])] = _nwb
            _nwb += 1
        elif it[0] == "c_p2":
            GW2[(it[1], it[2], it[3])] = _ng2
            _ng2 += 1

    def body_sync(sync):
        for i in range(8):
            E.dinc("s_const", sync.dma_start(out=bmk[i][:], in_=bmask_in[i]))
        E.dinc("s_const", sync.dma_start(out=idb[:], in_=idb_in))
        E.dinc("s_const", sync.dma_start(out=idf[:], in_=idf_in))
        E.dinc("s_const", sync.dma_start(out=onesb[:], in_=onesb_in))
        E.dinc("s_const", sync.dma_start(out=onesf[:], in_=onesf_in))
        E.dinc("s_const", sync.dma_start(out=ones8[:], in_=ones8_in))
        E.dinc("s_const", sync.dma_start(out=gbrow[:], in_=gb_in))
        E.dinc("s_gw", sync.dma_start(out=gwf[:], in_=gwT_in))

        for mb in range(MB):
            for jj in range(4):
                j = mb * 4 + jj
                E.wait(sync, ("a_tile_done", j - 2))
                E.wait(sync, ("a_xnb_s", j - 2))
                ch = f"s_xld{j % 2}"
                E.dinc(ch, sync.dma_start(
                    out=xt[j % 2][:], in_=x_in[j * 128:(j + 1) * 128, :]))
                if E.pass1:
                    E.marks[("xld", j)] = (ch, E.dma_counts[ch])
            for h in range(NH):
                gi = mb * NH + h
                E.wait(sync, ("a_vmm", gi - 4))
                wdma(sync, None, ("p0_wkv", h, 0))
                wdma(sync, None, ("p0_wkv", h, 1))
                ch = f"s_wkv{gi % 4}"
                E.dinc(ch, sync.dma_start(
                    out=wkv_t[gi % 4][:].rearrange("p (j d) -> p j d", j=2),
                    in_=wkvb_sp[h].rearrange("j p d -> p j d")))
                if E.pass1:
                    E.marks[("wkvl", gi)] = (ch, E.dma_counts[ch])
                if gi >= 1:
                    pg = gi - 1
                    pmb, ph_ = pg // NH, pg % NH
                    for wi, (which, spt) in enumerate((("k", kT_sp), ("v", v_sp))):
                        E.wait(sync, ("a_evac", pg, which))
                        st_idx = wi * 2 + pg % 2
                        ch = f"s_sp{which}{pg % 2}"
                        E.dinc(ch, sync.dma_start(
                            out=spt[ph_][:, pmb * 512:(pmb + 1) * 512],
                            in_=kvst[st_idx][:]))
                        E.mark_d(("a_spill", pg, which), ch)
        pg = MB * NH - 1
        pmb, ph_ = pg // NH, pg % NH
        for wi, (which, spt) in enumerate((("k", kT_sp), ("v", v_sp))):
            E.wait(sync, ("a_evac", pg, which))
            st_idx = wi * 2 + pg % 2
            ch = f"s_sp{which}{pg % 2}"
            E.dinc(ch, sync.dma_start(
                out=spt[ph_][:, pmb * 512:(pmb + 1) * 512], in_=kvst[st_idx][:]))
            E.mark_d(("a_spill", pg, which), ch)
        for which in ("k", "v"):
            for par in range(2):
                E.mark_d(("a_spfin", which, par), f"s_sp{which}{par}")

        E.wait(sync, ("a_xnb_s", NTI - 1))
        E.wait(sync, ("a_xnb_s", NTI - 2))
        for mb in range(OMB):
            for jj in range(4):
                j = mb * 4 + jj
                E.wait(sync, ("cp_tile_done", j - 2))
                E.wait(sync, ("cp_xnb_s", j - 2))
                ch = f"s_xld{j % 2}"
                E.dinc(ch, sync.dma_start(
                    out=xt[j % 2][:], in_=xo_in[j * 128:(j + 1) * 128, :]))
                if E.pass1:
                    E.marks[("cp_xld", j)] = (ch, E.dma_counts[ch])
            E.wait(sync, ("cp_stage_hl", mb))
            ch = "s_xnsp" if mb % 2 == 0 else "s_xnsp1"
            E.dinc(ch, sync.dma_start(
                out=xnTo_sp.rearrange("p (c s) -> p c s", c=DC)[:, :, mb * 512:(mb + 1) * 512],
                in_=cp_stg[mb % 2][:].rearrange("p (c s) -> p c s", c=DC)))
            E.mark_d(("cp_xnsp", mb), ch)
            for h in range(NH):
                qi = mb * NH + h
                E.wait(sync, ("a2_qmm", qi - 2))
                wdma(sync, None, ("p0_wqA", h, 0))
                wdma(sync, None, ("p0_wqA", h, 1))
                ch = f"s_wq{qi % 2}"
                E.dinc(ch, sync.dma_start(out=wq_t[qi % 2][:], in_=wqb_sp[h]))
                if E.pass1:
                    E.marks[("wql", qi)] = (ch, E.dma_counts[ch])
                if qi >= 1:
                    pq = qi - 1
                    pmbq, phq = pq // NH, pq % NH
                    E.wait(sync, ("a2_qevac", pq))
                    ch = f"s_spq{pq % 2}"
                    E.dinc(ch, sync.dma_start(
                        out=q_sp[phq][:, pmbq * 512:(pmbq + 1) * 512],
                        in_=qst[pq % 2][:]))
                    E.mark_d(("a2_qspill", pq), ch)
        pq = OMB * NH - 1
        pmbq, phq = pq // NH, pq % NH
        E.wait(sync, ("a2_qevac", pq))
        ch = f"s_spq{pq % 2}"
        E.dinc(ch, sync.dma_start(
            out=q_sp[phq][:, pmbq * 512:(pmbq + 1) * 512], in_=qst[pq % 2][:]))
        E.mark_d(("a2_qspill", pq), ch)
        for par in range(2):
            E.mark_d(("a2_qspfin", par), f"s_spq{par}")
        E.wait(sync, "cp_wd_done")
        E.dinc("s_wdsp", sync.dma_start(out=wdT_sp, in_=wdT_sb))
        E.mark_d("cp_wdsp", "s_wdsp")
        for chx in ("s_xnsp", "s_xnsp1", "s_wdsp", "s_xld0", "s_xld1"):
            E.mark_d(("cpdma", chx), chx)

        E.wait(sync, "fin_t_a2")
        E.wait(sync, "fin_v_a2")
        E.wait(sync, "fin_c_a2")
        for chx in ("s_xnsp", "s_xnsp1", "s_wdsp", "s_xld0", "s_xld1"):
            wdma(sync, None, ("cpdma", chx))
        wdma(sync, None, ("p0_wo", 0))
        wdma(sync, None, ("p0_wo", 1))
        for which in ("k", "v"):
            for par in range(2):
                wdma(sync, None, ("a_spfin", which, par))
        for par in range(2):
            wdma(sync, None, ("a2_qspfin", par))
        nwo = 0
        for it in BC_SCHED:
            kind = it[0]
            if kind == "b_q":
                _, qc, h = it
                gh = qc * NH + h
                E.wait(sync, ("b_scdone", gh - 2))
                ch = f"s_q{gh % 2}"
                E.dinc(ch, sync.dma_start(
                    out=qbuf[gh % 2][:], in_=q_sp[h][:, qc * 512:(qc + 1) * 512]))
                if E.pass1:
                    E.marks[("b_ql", gh)] = (ch, E.dma_counts[ch])
            elif kind == "b_kvq":
                _, qc, h, qq = it
                g = GQ[(qc, h, qq)]
                E.wait(sync, ("b_qdone", g - 4))
                ch = f"s_kv{g % 4}"
                E.dinc(ch, sync.dma_start(
                    out=kqb[g % 4][:],
                    in_=kT_sp[h][:, qq * 512:(qq + 1) * 512]))
                E.dinc(ch, sync.dma_start(
                    out=vqb[g % 4][:],
                    in_=v_sp[h][:, qq * 512:(qq + 1) * 512]))
                if E.pass1:
                    E.marks[("b_kvl", g)] = (ch, E.dma_counts[ch])
            elif kind == "b_wo":
                _, qc, dc = it
                E.wait(sync, ("b_woN", nwo - 2))
                ch = f"s_wo{dc % 2}"
                E.dinc(ch, sync.dma_start(out=wobd[dc % 2][:], in_=wob_sp[dc]))
                if E.pass1:
                    E.marks[("b_wol", qc, dc)] = (ch, E.dma_counts[ch])
                E.wait(sync, ("b_sT_evac", qc, dc))
                col = dc * OWN + qc * 512
                ch = f"s_spt{dc % 2}"
                E.dinc(ch, sync.dma_start(out=sT_sp[:, col:col + 512],
                                          in_=sTst[dc % 2][:]))
                E.mark_d(("b_sT_sp", qc, dc), ch)
                nwo += 1
            elif kind == "c_xnq":
                _, tq = it
                E.wait(sync, ("c_p1_done", tq - 1))
                E.wait(sync, ("cp_xnsp", OMB - 1))
                if OMB >= 2:
                    E.wait(sync, ("cp_xnsp", OMB - 2))
                E.dinc("s_xnq", sync.dma_start(
                    out=xnTq[:].rearrange("p (c s) -> p c s", c=DC),
                    in_=xnTo_sp.rearrange("p (c s) -> p c s", c=DC)[:, :, tq * 512:(tq + 1) * 512]))
                if E.pass1:
                    E.marks[("c_xnq", tq)] = ("s_xnq", E.dma_counts["s_xnq"])
            elif kind == "c_wbc":
                _, tq, e = it
                E.wait(sync, ("c_wbcmmG", GWB[(tq, e)] - 2))
                E.wait(sync, "cp_wdsp")
                ch = f"s_wd{e % 2}"
                E.dinc(ch, sync.dma_start(
                    out=wdrow[e % 2], in_=wdT_sp[e:e + 1, tq * 512:(tq + 1) * 512]))
                if E.pass1:
                    E.marks[("c_wdl", tq, e)] = (ch, E.dma_counts[ch])
            elif kind == "c_ep":
                _, tq, tt = it
                gtt = tq * 4 + tt
                E.wait(sync, ("c_ep_tr", (gtt - 2) // 4, (gtt - 2) % 4, 3)
                       if gtt >= 2 else ("x", -1))
                wdma(sync, None, ("b_sT_sp", tq, DC - 1))
                wdma(sync, None, ("b_sT_sp", tq, DC - 2))
                ch = f"s_sTl{tt % 2}"
                E.dinc(ch, sync.dma_start(
                    out=sTtt[tt % 2][:].rearrange("p (c s) -> p c s", c=DC),
                    in_=sT_sp.rearrange("p (c s) -> p c s", c=DC)[:, :, tq * 512 + tt * 128:tq * 512 + (tt + 1) * 128]))
                if E.pass1:
                    E.marks[("c_sTl", tq, tt)] = (ch, E.dma_counts[ch])
                wdma(sync, None, ("c_out", (gtt - 2) // 4, (gtt - 2) % 4)
                     if gtt >= 2 else ("x", -1))
                ch = f"s_xc{tt % 2}"
                E.dinc(ch, sync.dma_start(
                    out=xct[tt % 2][:],
                    in_=xo_in[gtt * 128:(gtt + 1) * 128, :]))
                if E.pass1:
                    E.marks[("c_xc", tq, tt)] = (ch, E.dma_counts[ch])
                E.wait(sync, ("c_ep_out", tq, tt))
                cho = "s_out" if gtt % 2 == 0 else "s_out1"
                E.dinc(cho, sync.dma_start(
                    out=out_t[gtt * 128:(gtt + 1) * 128, :],
                    in_=xct[tt % 2][:]))
                E.mark_d(("c_out", tq, tt), cho)
        wdma(sync, None, ("c_out", TQ - 1, 3))
        wdma(sync, None, ("c_out", TQ - 1, 2))

    def body_gpsimd(gp):
        for j, (key, n) in enumerate(p0_slices):
            par = j % 2
            dst, src = p0_aps(key)
            wdma(gp, None, ("p0_out", j - 2))
            E.dinc(f"s_pi{par}", gp.dma_start(out=p0st[par][:, 0:n], in_=src))
            E.wait_ge(gp, f"s_pi{par}", E.dma_counts[f"s_pi{par}"])
            E.dinc(f"s_po{par}", gp.dma_start(out=dst, in_=p0st[par][:, 0:n]))
            E.mark_d(("p0_out", j), f"s_po{par}")
            if key[0] == "wkv" and key[2] == 1:
                E.mark_d(("p0_wkv", key[1], 0), "s_po0")
                E.mark_d(("p0_wkv", key[1], 1), "s_po1")
            if key[0] == "wq":
                E.mark_d(("p0_wqA", key[1], 0), "s_po0")
                E.mark_d(("p0_wqA", key[1], 1), "s_po1")
            if key[0] == "wo" and key[1] == DC - 1:
                E.mark_d(("p0_wo", 0), "s_po0")
                E.mark_d(("p0_wo", 1), "s_po1")
        E.mark_d(("p0_all", 0), "s_po0")
        E.mark_d(("p0_all", 1), "s_po1")

        E.wait(gp, "fin_t_a2")
        E.wait(gp, "fin_v_a2")
        E.wait(gp, "fin_c_a2")
        for ch in ("s_xnsp", "s_xnsp1", "s_wdsp", "s_xld0", "s_xld1"):
            wdma(gp, None, ("cpdma", ch))
        for which in ("k", "v"):
            for par in range(2):
                wdma(gp, None, ("a_spfin", which, par))
        for par in range(2):
            wdma(gp, None, ("a2_qspfin", par))
        for tq in range(TQ):
            for grp_i, grp in enumerate(cfg.groups):
                for e in grp:
                    for ih in range(2):
                        jslot = (e % 2) * 2 + ih
                        E.wait(gp, ("c_w1_usedG", GE[(tq, e)] - 2))
                        if tq == 0 and grp_i == 0 and e == grp[0] and ih == 0:
                            wdma(gp, None, ("p0_all", 0))
                            wdma(gp, None, ("p0_all", 1))
                        ch = f"s_w1l{jslot}"
                        E.dinc(ch, gp.dma_start(out=w1sb[jslot][:], in_=w1f_in[e, ih]))
                        if E.pass1:
                            E.marks[("c_w1l", tq, e, ih)] = (ch, E.dma_counts[ch])
                for dc in range(DC):
                    E.wait(gp, ("c_w2_usedG", GW2[(tq, grp_i, dc)] - 2))
                    e0, e1 = grp[0], grp[-1] + 1
                    ch = f"s_w2l{dc % 2}"
                    E.dinc(ch, gp.dma_start(
                        out=w2sbs[dc % 2][:, 0:(e1 - e0) * IC * 128],
                        in_=w2f_in[dc, :, e0 * IC * 128:e1 * IC * 128]))
                    if E.pass1:
                        E.marks[("c_w2l", tq, grp_i, dc)] = (ch, E.dma_counts[ch])

    def body_tensor(t):
        E.wait_ge(t, "s_const", 224)
        E.wait_ge(t, "s_gw", 16)
        ins = t.matmul(pC[:, 0:E_R], onesf[:], gbrow[:], start=True, stop=True)
        E.inc("t_a", ins)
        E.mark("gbb_mm", "t_a")

        for mb in range(MB):
            for jj in range(4):
                j = mb * 4 + jj
                E.wait(t, ("a_xnb", j))
                E.wait(t, ("a_trevac", j - 1))
                for dc in range(DC):
                    dst = (vbfA if dc < 8 else vbfB)[:, (dc % 8) * 128:(dc % 8 + 1) * 128]
                    ins = t.transpose(dst, xnb[j % 2][:, dc * 128:(dc + 1) * 128], idb[:])
                    if dc == 7 or dc == 15:
                        E.inc("t_a", ins)
                        E.mark(("a_tr", j, dc // 8), "t_a")
                E.mark(("a_tile_done", j), "t_a")
            for h in range(NH):
                gi = mb * NH + h
                if gi > 0:
                    E.wait(t, ("a_vtevac", gi - 1))
                    E.wait(t, ("a_vtrevac", gi - 2))
                    for i4 in range(4):
                        ins = t.transpose(vbfC[:, i4 * 128:(i4 + 1) * 128],
                                          vtst[(gi - 1) % 2][:, i4 * 128:(i4 + 1) * 128],
                                          idb[:])
                    E.inc("t_a", ins)
                    E.mark(("a_vtr", gi - 1), "t_a")
                wdma(t, None, ("wkvl", gi))
                E.wait(t, ("a_stage", mb))
                wslc = wkv_t[gi % 4]
                for wi, which in enumerate(("k", "v")):
                    dstp = [pA[:, 0:512], pA[:, 512:1024]][wi]
                    E.wait(t, ("a_evacd", gi - 1, which))
                    for dc in range(DC):
                        ins = t.matmul(dstp,
                                       wslc[:, wi * D + dc * 128:wi * D + dc * 128 + 128],
                                       stg[mb % 2][:, dc * 512:(dc + 1) * 512],
                                       start=(dc == 0), stop=(dc == DC - 1))
                    E.inc("t_a", ins)
                    E.mark(("a_mm", gi, which), "t_a")
                E.mark(("a_vmm", gi), "t_a")
            E.mark(("a_blk_done", mb), "t_a")
        gi = MB * NH
        E.wait(t, ("a_vtevac", gi - 1))
        E.wait(t, ("a_vtrevac", gi - 2))
        for i4 in range(4):
            ins = t.transpose(vbfC[:, i4 * 128:(i4 + 1) * 128],
                              vtst[(gi - 1) % 2][:, i4 * 128:(i4 + 1) * 128], idb[:])
        E.inc("t_a", ins)
        E.mark(("a_vtr", gi - 1), "t_a")
        E.mark("fin_t_a", "t_a")

        E.wait(t, "fin_v_a")
        E.wait(t, "fin_c_a")
        for mb in range(OMB):
            for jj in range(4):
                j = mb * 4 + jj
                E.wait(t, ("cp_xnb", j))
                E.wait(t, ("cp_trevacL", j - 1))
                for dc in range(DC):
                    dst = (vbfA if dc < 8 else vbfB)[:, (dc % 8) * 128:(dc % 8 + 1) * 128]
                    ins = t.transpose(dst, cp_hi[j % 2][:, dc * 128:(dc + 1) * 128], idb[:])
                    if dc == 7 or dc == 15:
                        E.inc("t_p", ins)
                        E.mark(("cp_trh", j, dc // 8), "t_p")
                E.wait(t, ("cp_lo", j))
                E.wait(t, ("cp_trevacH", j))
                for dc in range(DC):
                    dst = (vbfA if dc < 8 else vbfB)[:, (dc % 8) * 128:(dc % 8 + 1) * 128]
                    ins = t.transpose(dst, cp_lo[j % 2][:, dc * 128:(dc + 1) * 128], idb[:])
                    if dc == 7 or dc == 15:
                        E.inc("t_p", ins)
                        E.mark(("cp_trl", j, dc // 8), "t_p")
                E.mark(("cp_tile_done", j), "t_p")
            E.wait(t, ("cp_stage_hl", mb))
            for h in range(NH):
                qi = mb * NH + h
                wdma(t, None, ("wql", qi))
                E.wait(t, ("a2_qevacd", qi - 1))
                for dc in range(DC):
                    ins = t.matmul(pA[:, 0:512],
                                   wq_t[qi % 2][:, dc * 128:(dc + 1) * 128],
                                   cp_stg[mb % 2][:, dc * 512:(dc + 1) * 512],
                                   start=(dc == 0), stop=(dc == DC - 1))
                E.inc("t_p", ins)
                E.mark(("a2_qmmD", qi), "t_p")
                E.mark(("a2_qmm", qi), "t_p")
            E.wait(t, ("cp_lgevac", mb - 1))
            E.wait(t, "gw_split")
            for dc in range(DC):
                t.matmul(pC[0:E_R, 0:512], gwhi[:, dc * E_R:(dc + 1) * E_R],
                         cp_stg[mb % 2][:, dc * 512:(dc + 1) * 512],
                         start=(dc == 0), stop=False)
                t.matmul(pC[0:E_R, 0:512], gwlo[:, dc * E_R:(dc + 1) * E_R],
                         cp_stg[mb % 2][:, dc * 512:(dc + 1) * 512],
                         start=False, stop=False)
                ins = t.matmul(pC[0:E_R, 0:512], gwhi[:, dc * E_R:(dc + 1) * E_R],
                               cp_loT[:, dc * 512:(dc + 1) * 512],
                               start=False, stop=(dc == DC - 1))
            E.inc("t_p", ins)
            E.mark(("cp_lg", mb), "t_p")
            for jj in range(4):
                j = mb * 4 + jj
                E.wait(t, ("cp_lgTevac", mb))
                E.wait(t, ("cp_lgtokevac", j - 1))
                ins = t.transpose(pB[0:128, 0:E_R], lgT[:, jj * 128:(jj + 1) * 128],
                                  idf[0:E_R, 0:E_R])
                E.inc("t_p", ins)
                E.mark(("cp_lgtr", j), "t_p")
                E.wait(t, ("cp_wd", j))
                E.wait(t, ("cp_wdevac", j - 1))
                ins = t.transpose(pB[0:E_R, 512:640], cp_t1[:, 0:E_R], idf[:])
                E.inc("t_p", ins)
                E.mark(("cp_wdtr", j), "t_p")
        E.mark("fin_t_a2", "t_p")
        E.mark("fin_t_cp", "t_p")

        E.wait(t, "fin_v_a2")
        E.wait(t, "fin_c_a2")
        for ch in ("s_xnsp", "s_xnsp1", "s_wdsp", "s_xld0", "s_xld1"):
            wdma(t, None, ("cpdma", ch))
        for par in range(2):
            wdma(t, None, ("a2_qspfin", par))
        st = {"np2": 0, "npe": 0, "nwo": 0, "np1_last": -1}

        def emit_pv(qc, h, mt):
            nbs = B_STEP[(qc, h, mt)]
            gh = qc * NH + h
            nmt = ext_of(qc) // 128
            st_ = mt == 0
            sp_ = mt == nmt - 1
            if mt >= nmt - 8:
                E.wait(t, ("b_maskd", nbs))
            else:
                E.wait(t, ("b_expd", nbs))
            if st_:
                E.wait(t, ("b_zn", gh - 1))
                E.wait(t, ("b_recipd", gh - 1))
            g4 = GQ[(qc, h, mt // 4)]
            t.matmul(pC[:, 512:1024],
                     vqb[g4 % 4][:, (mt % 4) * 128:(mt % 4 + 1) * 128],
                     Ebuf[nbs % 2][:], start=st_, stop=sp_)
            ins2 = t.matmul(pD[0:1, 0:512], onesb[:],
                            Ebuf[nbs % 2][:], start=st_, stop=sp_)
            E.inc("t_b", ins2)
            E.mark(("b_pv", nbs), "t_b")
            if sp_ or mt % 4 == 3:
                E.mark(("b_qdone", g4), "t_b")

        for it in BC_SCHED:
            kind = it[0]
            if kind == "b_mt":
                _, qc, h, mt = it
                gh = qc * NH + h
                nbs = B_STEP[(qc, h, mt)]
                nmt = ext_of(qc) // 128
                g4 = GQ[(qc, h, mt // 4)]
                if mt % 4 == 0:
                    wdma(t, None, ("b_kvl", g4))
                if mt == 0:
                    wdma(t, None, ("b_ql", gh))
                    if h == 0 and qc > 0:
                        E.wait(t, ("b_sT_evac", qc - 1, DC - 1))
                E.wait(t, ("b_expd", nbs - 1))
                ins = t.matmul(pC[:, 0:512],
                               kqb[g4 % 4][:, (mt % 4) * 128:(mt % 4 + 1) * 128],
                               qbuf[gh % 2][:], start=True, stop=True)
                E.inc("t_b", ins)
                E.mark(("b_sc", nbs), "t_b")
                if mt == nmt - 1:
                    E.mark(("b_scdone", gh), "t_b")
                if mt > 0:
                    emit_pv(qc, h, mt - 1)
            elif kind == "b_fin1":
                _, qc, h = it
                emit_pv(qc, h, ext_of(qc) // 128 - 1)
            elif kind == "b_fin2":
                _, qc, h = it
                gh = qc * NH + h
                E.wait(t, ("b_recip", gh))
                E.wait(t, ("pe_copy", st["npe"] - 1))
                ins = t.matmul(pE[:, 0:512], onesf[:], rsum, start=True, stop=True)
                E.inc("t_b", ins)
                E.mark(("pe_mm", st["npe"]), "t_b")
                st["npe"] += 1
            elif kind == "b_wo":
                _, qc, dc = it
                E.wait(t, ("b_zn", qc * NH + NH - 1))
                if st["nwo"] > 0:
                    pw = st["nwo"] - 1
                    E.wait(t, ("b_sT_evac", pw // DC, pw % DC))
                wdma(t, None, ("b_wol", qc, dc))
                for h in range(NH):
                    ins = t.matmul(pC[:, 0:512],
                                   wobd[dc % 2][:, h * 128:(h + 1) * 128],
                                   zAll[:, h * 512:(h + 1) * 512],
                                   start=(h == 0), stop=(h == NH - 1))
                E.inc("t_b", ins)
                E.mark(("b_wo", qc, dc), "t_b")
                E.mark(("b_woN", st["nwo"]), "t_b")
                st["nwo"] += 1
            elif kind == "c_xnq":
                wdma(t, None, ("c_xnq", it[1]))
            elif kind == "c_wbc":
                _, tq, e = it
                wdma(t, None, ("c_wdl", tq, e))
                E.wait(t, ("pe_copy", st["npe"] - 1))
                ins = t.matmul(pE[:, 0:512], onesf[:], wdrow[e % 2],
                               start=True, stop=True)
                E.inc("t_c", ins)
                E.mark(("pe_mm", st["npe"]), "t_c")
                E.mark(("c_wbcmmG", GWB[(tq, e)]), "t_c")
                st["npe"] += 1
            elif kind == "c_p1":
                _, tq, grp_i, e, i = it
                np1 = C_P1[it]
                ih = i // 2
                jslot = (e % 2) * 2 + ih
                wdma(t, None, ("c_w1l", tq, e, ih))
                E.wait(t, ("c_h_evac", np1 - 2))
                if np1 == P1_FIRST[tq] and tq > 0:
                    E.wait(t, ("c_ep_add", tq - 1, 3, 3))
                bank = [pA[:, 0:512], pA[:, 512:1024]][np1 % 2]
                for dc in range(DC):
                    col = dc * 2 * 128 + (i % 2) * 128
                    ins = t.matmul(bank, w1sb[jslot][:, col:col + 128],
                                   xnTq[:, dc * 512:(dc + 1) * 512],
                                   start=(dc == 0), stop=(dc == DC - 1))
                E.inc("t_c", ins)
                E.mark(("c_p1", np1), "t_c")
                if i == IC - 1:
                    E.mark(("c_w1_usedG", GE[(tq, e)]), "t_c")
                if np1 == P1_LAST[tq]:
                    E.mark(("c_p1_done", tq), "t_c")
                st["np1_last"] = np1
            elif kind == "c_p2":
                _, tq, grp_i, dc = it
                grp = cfg.groups[grp_i]
                np2 = st["np2"]
                wdma(t, None, ("c_w2l", tq, grp_i, dc))
                E.wait(t, ("c_h_evac", st["np1_last"]))
                E.wait(t, ("c_radd", np2 - 2))
                bank = [pB[:, 0:512], pB[:, 512:1024]][np2 % 2]
                for ki, e in enumerate(grp):
                    for i in range(IC):
                        col = (ki * IC + i) * 128
                        ins = t.matmul(bank, w2sbs[dc % 2][:, col:col + 128],
                                       hsb[ki][:, i * 512:(i + 1) * 512],
                                       start=(ki == 0 and i == 0),
                                       stop=(ki == len(grp) - 1 and i == IC - 1))
                E.inc("t_c", ins)
                E.mark(("c_p2", np2), "t_c")
                E.mark(("c_w2_usedG", GW2[(tq, grp_i, dc)]), "t_c")
                st["np2"] += 1
            elif kind == "c_ep":
                _, tq, tt = it
                gtt = tq * 4 + tt
                E.wait(t, ("c_ep_mul", tq, tt))
                if tt == 0:
                    E.wait(t, ("c_h_evac", P1_LAST[tq]))
                for dcg in range(4):
                    gg = gtt * 4 + dcg
                    E.wait(t, ("c_ep_add", (gg - 2) // 16, ((gg - 2) % 16) // 4,
                               (gg - 2) % 4) if gg >= 2 else ("x", -1))
                    bank = [pA[:, 0:512], pA[:, 512:1024]][dcg % 2]
                    for q4 in range(4):
                        dc = dcg * 4 + q4
                        ins = t.transpose(bank[:, q4 * 128:(q4 + 1) * 128],
                                          sTtt[tt % 2][:, dc * 128:(dc + 1) * 128],
                                          idf[:])
                    E.inc("t_c", ins)
                    E.mark(("c_ep_tr", tq, tt, dcg), "t_c")

    def body_scalar(s):
        def ln_tile(j, ch, ldkey, tdkey, out_bf, out_f32, pfx):
            wdma(s, None, (ldkey, j))
            E.wait(s, (pfx + "_statsdone", j - 1))
            ins = s.activation(sqscr[:], xt[j % 2][:], AF.Square,
                               accum_out=lnc(j, 1))
            E.inc(ch, ins)
            E.mark((pfx + "_sq", j), ch)
            E.wait(s, (pfx + "_stats", j))
            ins = s.activation(lnc(j, 4), lnc(j, 3), AF.Sqrt)
            E.inc(ch, ins)
            E.mark((pfx + "_sqrt", j), ch)
            E.wait(s, (pfx + "_rnb", j))
            E.wait(s, (tdkey, j - 1))
            if out_f32 is not None:
                ins = s.activation(out_f32[:], xt[j % 2][:], AF.Identity,
                                   bias=lnc(j, 6), scale=lnc(j, 5))
                E.inc(ch, ins)
                E.mark((pfx + "_xnf", j), ch)
                E.wait_ge(s, ch, E.counts[ch])
                ins = s.copy(out_bf[:], out_f32[:])
                E.inc(ch, ins)
            else:
                ins = s.activation(out_bf[:], xt[j % 2][:], AF.Identity,
                                   bias=lnc(j, 6), scale=lnc(j, 5))
                E.inc(ch, ins)
            E.mark((pfx + "_xnb", j), ch)
            E.mark((pfx + "_xnb_s", j), ch)

        for j in range(NTI):
            ln_tile(j, "c_a", "xld", "a_tile_done", xnb[j % 2], None, "a")
            E.mark(("a_xnb", j), "c_a")
            E.mark(("a_xnb_s", j), "c_a")
        E.mark("fin_c_a", "c_a")

        E.wait(s, "fin_t_a")
        E.wait(s, "fin_v_a")
        for mb in range(OMB):
            for jj in range(4):
                j = mb * 4 + jj
                ln_tile(j, "c_p", "cp_xld", "cp_hi_done", cp_hi[j % 2], xnf, "cp")
                E.mark(("cp_xnb", j), "c_p")
                E.mark(("cp_xnb_s", j), "c_p")
            for jj in range(4):
                j = mb * 4 + jj
                E.wait(s, ("cp_lgtokevac", j))
                ins = s.activation(cp_sc[:], lgtok[:], AF.Sigmoid)
                E.inc("c_p", ins)
                E.mark(("cp_sig", j), "c_p")
        E.mark("fin_c_a2", "c_p")
        E.mark("fin_c_cp", "c_p")

        E.wait(s, "fin_t_a2")
        E.wait(s, "fin_v_a2")
        for chx in ("s_xnsp", "s_xnsp1", "s_wdsp", "s_xld0", "s_xld1"):
            wdma(s, None, ("cpdma", chx))
        npe = 0
        for it in BC_SCHED:
            kind = it[0]
            if kind == "b_mt":
                _, qc, h, mt = it
                nbs = B_STEP[(qc, h, mt)]
                E.wait(s, ("b_sc", nbs))
                E.wait(s, ("b_pv", nbs - 2))
                ins = s.activation(Ebuf[nbs % 2][:], pC[:, 0:512], AF.Exp,
                                   scale=SCALE)
                E.inc("c_b", ins)
                E.mark(("b_expd", nbs), "c_b")
            elif kind == "b_fin2":
                _, qc, h = it
                gh = qc * NH + h
                E.wait(s, ("pe_mm", npe))
                E.wait(s, ("b_zn", gh - 2))
                ins = s.copy(rbc2[h % 2][:], pE[:, 0:512])
                E.inc("c_b", ins)
                E.mark(("b_rbccopy", gh), "c_b")
                E.mark(("pe_copy", npe), "c_b")
                npe += 1
            elif kind == "c_wbc":
                _, tq, e = it
                gwb = GWB[(tq, e)]
                E.wait(s, ("pe_mm", npe))
                E.wait(s, ("c_wbc_usedG", gwb - 2))
                ins = s.copy(wbc[e % 2][:], pE[:, 0:512])
                E.inc("c_c", ins)
                E.mark(("c_wbccopy", tq, e), "c_c")
                E.mark(("pe_copy", npe), "c_c")
                npe += 1
            elif kind == "c_ep" and it[2] == 0:
                tq = it[1]
                E.wait(s, ("c_radd_all", tq))
                ins = s.activation(routedT[:], routedT[:], AF.Silu)
                E.inc("c_c", ins)
                E.mark(("c_silu", tq), "c_c")

    def body_vector(v):
        E.wait_ge(v, "s_gw", 16)
        ins = v.tensor_copy(gwhi[:], gwf[:])
        c = E.inc("v_a", ins)
        E.wait_ge(v, "v_a", c)
        ins = v.tensor_tensor(gwlo[:], gwf[:], gwhi[:], ALU.subtract)
        E.inc("v_a", ins)
        E.mark("gw_split", "v_a")
        E.wait(v, "gbb_mm")
        ins = v.tensor_copy(gbbc[:], pC[:, 0:E_R])
        E.inc("v_a", ins)
        E.mark("gbb_done", "v_a")

        def ln_stats(j, ch, ldkey, pfx):
            wdma(v, None, (ldkey, j))
            E.wait(v, (pfx + "_xnb_s", j - 2))
            ins = v.reduce_sum(lnc(j, 0), xt[j % 2][:], axis=AXX)
            c = E.inc(ch, ins)
            E.wait(v, (pfx + "_sq", j))
            E.wait_ge(v, ch, c)
            ins = v.tensor_scalar(lnc(j, 2), lnc(j, 0), 1.0 / D, None, ALU.mult)
            c = E.inc(ch, ins)
            ins = v.tensor_scalar(lnc(j, 7), lnc(j, 1), 1.0 / D, None, ALU.mult)
            c = E.inc(ch, ins)
            E.wait_ge(v, ch, c)
            ins = v.tensor_scalar(lnc(j, 3), lnc(j, 2), lnc(j, 2), -1.0,
                                  ALU.mult, ALU.mult)
            c = E.inc(ch, ins)
            E.wait_ge(v, ch, c)
            ins = v.tensor_tensor(lnc(j, 3), lnc(j, 3), lnc(j, 7), ALU.add)
            c = E.inc(ch, ins)
            E.wait_ge(v, ch, c)
            ins = v.tensor_scalar(lnc(j, 3), lnc(j, 3), 1e-5, None, ALU.add)
            E.inc(ch, ins)
            E.mark((pfx + "_stats", j), ch)
            E.wait(v, (pfx + "_sqrt", j))
            ins = v.reciprocal(lnc(j, 5), lnc(j, 4))
            c = E.inc(ch, ins)
            E.wait_ge(v, ch, c)
            ins = v.tensor_scalar(lnc(j, 6), lnc(j, 2), lnc(j, 5), -1.0,
                                  ALU.mult, ALU.mult)
            E.inc(ch, ins)
            E.mark((pfx + "_rnb", j), ch)
            E.mark((pfx + "_statsdone", j), ch)

        for mb in range(MB):
            for jj in range(4):
                j = mb * 4 + jj
                ln_stats(j, "v_a", "xld", "a")
                for half in range(2):
                    E.wait(v, ("a_tr", j, half))
                    E.wait(v, ("a_blk_done", mb - 2))
                    src = [vbfA, vbfB][half]
                    dst = stg[mb % 2][:].rearrange("p (c s) -> p c s", c=DC)[
                        :, half * 8:(half + 1) * 8, jj * 128:(jj + 1) * 128]
                    ins = v.tensor_copy(dst, src[:].rearrange("p (c s) -> p c s", c=8))
                    E.inc("v_a", ins)
                E.mark(("a_trevac", j), "v_a")
            E.mark(("a_stage", mb), "v_a")
            for h in range(NH):
                gi = mb * NH + h
                for wi, which in enumerate(("k", "v")):
                    E.wait(v, ("a_mm", gi, which))
                    src = [pA[:, 0:512], pA[:, 512:1024]][wi]
                    if which == "v":
                        ins = v.tensor_copy(vtst[gi % 2][:], src)
                        E.inc("v_a", ins)
                        E.mark(("a_vtevac", gi), "v_a")
                    else:
                        wdma(v, None, ("a_spill", gi - 2, which))
                        st_idx = wi * 2 + gi % 2
                        ins = v.tensor_copy(kvst[st_idx][:], src)
                        E.inc("v_a", ins)
                        E.mark(("a_evac", gi, which), "v_a")
                    E.mark(("a_evacd", gi, which), "v_a")
                if gi > 0:
                    E.wait(v, ("a_vtr", gi - 1))
                    wdma(v, None, ("a_spill", gi - 3, "v"))
                    st_idx = 2 + (gi - 1) % 2
                    ins = v.tensor_copy(kvst[st_idx][:], vbfC[:, 0:512])
                    E.inc("v_a", ins)
                    E.mark(("a_evac", gi - 1, "v"), "v_a")
                    E.mark(("a_vtrevac", gi - 1), "v_a")
        gi = MB * NH
        E.wait(v, ("a_vtr", gi - 1))
        wdma(v, None, ("a_spill", gi - 3, "v"))
        ins = v.tensor_copy(kvst[2 + (gi - 1) % 2][:], vbfC[:, 0:512])
        E.inc("v_a", ins)
        E.mark(("a_evac", gi - 1, "v"), "v_a")
        E.mark("fin_v_a", "v_a")

        E.wait(v, "fin_t_a")
        E.wait(v, "fin_c_a")
        for which in ("k", "v"):
            for par in range(2):
                wdma(v, None, ("a_spfin", which, par))
        for mb in range(OMB):
            for jj in range(4):
                j = mb * 4 + jj
                ln_stats(j, "v_p", "cp_xld", "cp")
                E.wait(v, ("cp_xnb", j))
                E.wait(v, ("cp_tile_done", j - 1))
                ins = v.tensor_tensor(cp_lo[j % 2][:], xnf[:], cp_hi[j % 2][:],
                                      ALU.subtract)
                E.inc("v_p", ins)
                E.mark(("cp_lo", j), "v_p")
                E.mark(("cp_hi_done", j), "v_p")
                for half in range(2):
                    E.wait(v, ("cp_trh", j, half))
                    E.wait(v, ("cp_lg", mb - 2))
                    E.wait(v, ("a2_qmmD", (mb - 2) * NH + NH - 1))
                    wdma(v, None, ("cp_xnsp", mb - 2))
                    src = [vbfA, vbfB][half]
                    dst = cp_stg[mb % 2][:].rearrange("p (c s) -> p c s", c=DC)[
                        :, half * 8:(half + 1) * 8, jj * 128:(jj + 1) * 128]
                    ins = v.tensor_copy(dst, src[:].rearrange("p (c s) -> p c s", c=8))
                    E.inc("v_p", ins)
                E.mark(("cp_trevacH", j), "v_p")
                for half in range(2):
                    E.wait(v, ("cp_trl", j, half))
                    E.wait(v, ("cp_lg", mb - 1))
                    src = [vbfA, vbfB][half]
                    dst = cp_loT[:].rearrange("p (c s) -> p c s", c=DC)[
                        :, half * 8:(half + 1) * 8, jj * 128:(jj + 1) * 128]
                    ins = v.tensor_copy(dst, src[:].rearrange("p (c s) -> p c s", c=8))
                    E.inc("v_p", ins)
                E.mark(("cp_trevacL", j), "v_p")
                if jj == 3:
                    E.mark(("cp_stage_hl", mb), "v_p")
            for h in range(NH):
                qi = mb * NH + h
                E.wait(v, ("a2_qmm", qi))
                wdma(v, None, ("a2_qspill", qi - 2))
                ins = v.tensor_copy(qst[qi % 2][:], pA[:, 0:512])
                E.inc("v_p", ins)
                E.mark(("a2_qevac", qi), "v_p")
                E.mark(("a2_qevacd", qi), "v_p")
            E.wait(v, ("cp_lg", mb))
            E.wait(v, ("cp_lgtr", 4 * mb - 1))
            ins = v.tensor_copy(lgT[:], pC[0:E_R, 0:512])
            E.inc("v_p", ins)
            E.mark(("cp_lgTevac", mb), "v_p")
            E.mark(("cp_lgevac", mb), "v_p")
            for jj in range(4):
                j = mb * 4 + jj
                E.wait(v, ("cp_lgtr", j))
                E.wait(v, ("cp_sig", j - 1))
                ins = v.tensor_copy(lgtok[:], pB[0:128, 0:E_R])
                E.inc("v_p", ins)
                E.mark(("cp_lgtokevac", j), "v_p")

                def vch(ins_):
                    cc = E.inc("v_p", ins_)
                    E.wait_ge(v, "v_p", cc)
                    return cc

                E.wait(v, ("cp_sig", j))
                E.wait(v, "gbb_done")
                vch(v.tensor_tensor(cp_sb[:], cp_sc[:], gbbc[:], ALU.add))
                for g_ in range(cfg.G_GROUPS):
                    sl = cp_sb[:, g_ * GSZ:(g_ + 1) * GSZ]
                    vch(v.reduce_max(cp_cols[:, 0:1], sl, axis=AXX))
                    vch(v.tensor_scalar(cp_t8[:], sl, cp_cols[:, 0:1], None, ALU.is_ge))
                    vch(v.scalar_tensor_tensor(cp_t8[:], cp_t8[:], -1e9, sl,
                                               ALU.mult, ALU.add))
                    vch(v.reduce_max(cp_cols[:, 1:2], cp_t8[:], axis=AXX))
                    vch(v.tensor_tensor(cp_g1[:, g_:g_ + 1], cp_cols[:, 0:1],
                                        cp_cols[:, 1:2], ALU.add))
                vch(v.tensor_copy(cp_g2[:], cp_g1[:]))
                vch(v.memset(cp_gm[:], 0.0))
                for it in range(cfg.TOPK_G):
                    vch(v.reduce_max(cp_cols[:, 2:3], cp_g2[:], axis=AXX))
                    vch(v.tensor_scalar(cp_g1[:], cp_g2[:], cp_cols[:, 2:3], None,
                                        ALU.is_ge))
                    vch(v.tensor_tensor(cp_gm[:], cp_gm[:], cp_g1[:], ALU.max))
                    vch(v.scalar_tensor_tensor(cp_g2[:], cp_g1[:], -1e9, cp_g2[:],
                                               ALU.mult, ALU.add))
                for g_ in range(cfg.G_GROUPS):
                    vch(v.tensor_scalar(cp_msk[:, g_ * GSZ:(g_ + 1) * GSZ], ones8[:],
                                        cp_gm[:, g_:g_ + 1], None, ALU.mult))
                vch(v.tensor_scalar(cp_t1[:], cp_msk[:], 1.0, 1e9,
                                    ALU.subtract, ALU.mult))
                vch(v.tensor_tensor(cp_t1[:], cp_sb[:], cp_t1[:], ALU.add))
                vch(v.memset(cp_sel[:], 0.0))
                for it in range(cfg.TOPK):
                    vch(v.reduce_max(cp_cols[:, 3:4], cp_t1[:], axis=AXX))
                    vch(v.tensor_scalar(cp_msk[:], cp_t1[:], cp_cols[:, 3:4], None,
                                        ALU.is_ge))
                    vch(v.tensor_tensor(cp_sel[:], cp_sel[:], cp_msk[:], ALU.max))
                    vch(v.scalar_tensor_tensor(cp_t1[:], cp_msk[:], -1e9, cp_t1[:],
                                               ALU.mult, ALU.add))
                vch(v.tensor_tensor(cp_sel[:], cp_sc[:], cp_sel[:], ALU.mult))
                vch(v.reduce_sum(cp_cols[:, 4:5], cp_sel[:], axis=AXX))
                vch(v.tensor_scalar(cp_cols[:, 5:6], cp_cols[:, 4:5], 1e-20, None,
                                    ALU.add))
                vch(v.reciprocal(cp_cols[:, 6:7], cp_cols[:, 5:6]))
                ins = v.tensor_scalar(cp_t1[:], cp_sel[:], cp_cols[:, 6:7], cfg.RSF,
                                      ALU.mult, ALU.mult)
                E.inc("v_p", ins)
                E.mark(("cp_wd", j), "v_p")
                E.wait(v, ("cp_wdtr", j))
                ins = v.tensor_copy(wdT_sb[:, j * 128:(j + 1) * 128],
                                    pB[0:E_R, 512:640])
                E.inc("v_p", ins)
                E.mark(("cp_wdevac", j), "v_p")
        E.mark("cp_wd_done", "v_p")
        E.mark("fin_v_a2", "v_p")
        E.mark("fin_v_cp", "v_p")

        E.wait(v, "fin_t_a2")
        E.wait(v, "fin_c_a2")
        for chx in ("s_xnsp", "s_xnsp1", "s_wdsp", "s_xld0", "s_xld1"):
            wdma(v, None, ("cpdma", chx))
        for par in range(2):
            wdma(v, None, ("p0_all", par))
        np1 = 0
        np2 = 0
        nwo = 0
        for it in BC_SCHED:
            kind = it[0]
            if kind == "b_mt":
                _, qc, h, mt = it
                nbs = B_STEP[(qc, h, mt)]
                nmt = ext_of(qc) // 128
                if mt >= nmt - 8:
                    E.wait(v, ("b_expd", nbs))
                    ins = v.tensor_tensor(Ebuf[nbs % 2][:], Ebuf[nbs % 2][:],
                                          bmk[mt - (nmt - 8)][:], ALU.mult)
                    E.inc("v_b", ins)
                    E.mark(("b_maskd", nbs), "v_b")
            elif kind == "b_fin1":
                _, qc, h = it
                gh = qc * NH + h
                E.wait(v, ("b_pv", B_LAST[(qc, h)]))
                ins = v.reciprocal(rsum, pD[0:1, 0:512])
                E.inc("v_b", ins)
                E.mark(("b_recip", gh), "v_b")
                E.mark(("b_recipd", gh), "v_b")
            elif kind == "b_fin2":
                _, qc, h = it
                gh = qc * NH + h
                E.wait(v, ("b_rbccopy", gh))
                if qc > 0:
                    E.wait(v, ("b_woN", qc * DC - 1))
                ins = v.tensor_tensor(zAll[:, h * 512:(h + 1) * 512],
                                      pC[:, 512:1024], rbc2[h % 2][:], ALU.mult)
                E.inc("v_b", ins)
                E.mark(("b_zn", gh), "v_b")
            elif kind == "b_wo":
                _, qc, dc = it
                E.wait(v, ("b_wo", qc, dc))
                gw = nwo - 2
                wdma(v, None, ("b_sT_sp", gw // DC, gw % DC) if gw >= 0 else ("x", -1))
                ins = v.tensor_copy(sTst[dc % 2][:], pC[:, 0:512])
                E.inc("v_b", ins)
                E.mark(("b_sT_evac", qc, dc), "v_b")
                nwo += 1
            elif kind == "c_p1":
                _, tq, grp_i, e, i = it
                routed = e < E_R
                E.wait(v, ("c_p1", np1))
                E.wait(v, ("c_h_evac", np1 - 1))
                if routed:
                    E.wait(v, ("c_wbccopy", tq, e))
                bank = [pA[:, 0:512], pA[:, 512:1024]][np1 % 2]
                a = rtA[np1 % 2]
                if routed:
                    ins = v.scalar_tensor_tensor(
                        a[:], bank, 0.0, wbc[e % 2][:], ALU.max, ALU.mult)
                else:
                    ins = v.tensor_scalar(a[:], bank, 0.0, None, ALU.max)
                cc = E.inc("v_c", ins)
                E.wait_ge(v, "v_c", cc)
                ki = cfg.groups[grp_i].index(e)
                ins = v.scalar_tensor_tensor(
                    hsb[ki][:, i * 512:(i + 1) * 512], bank, 0.0, a[:],
                    ALU.max, ALU.mult)
                E.inc("v_c", ins)
                E.mark(("c_h_evac", np1), "v_c")
                if routed and i == IC - 1:
                    E.mark(("c_wbc_usedG", GWB[(tq, e)]), "v_c")
                np1 += 1
            elif kind == "c_p2":
                _, tq, grp_i, dc = it
                E.wait(v, ("c_p2", np2))
                bank = [pB[:, 0:512], pB[:, 512:1024]][np2 % 2]
                dst = routedT[:, dc * 512:(dc + 1) * 512]
                if grp_i == 0:
                    E.wait(v, ("c_ep_mul", tq - 1, 3))
                    E.wait(v, ("c_silu", tq - 1))
                    ins = v.tensor_copy(dst, bank)
                else:
                    ins = v.tensor_tensor(dst, dst, bank, ALU.add)
                E.inc("v_c", ins)
                E.mark(("c_radd", np2), "v_c")
                np2 += 1
                if grp_i == len(cfg.groups) - 1 and dc == DC - 1:
                    E.mark(("c_radd_all", tq), "v_c")
            elif kind == "c_ep":
                _, tq, tt = it
                E.wait(v, ("c_silu", tq))
                wdma(v, None, ("c_sTl", tq, tt))
                rsl = routedT[:].rearrange("p (c s) -> p c s", c=DC)[
                    :, :, tt * 128:(tt + 1) * 128]
                dtt = sTtt[tt % 2][:].rearrange("p (c s) -> p c s", c=DC)
                ins = v.tensor_tensor(dtt, rsl, dtt, ALU.mult)
                E.inc("v_c", ins)
                E.mark(("c_ep_mul", tq, tt), "v_c")
                wdma(v, None, ("c_xc", tq, tt))
                for dcg in range(4):
                    E.wait(v, ("c_ep_tr", tq, tt, dcg))
                    bank = [pA[:, 0:512], pA[:, 512:1024]][dcg % 2]
                    xsl = xct[tt % 2][:, dcg * 512:(dcg + 1) * 512]
                    ins = v.tensor_tensor(xsl, bank, xsl, ALU.add)
                    E.inc("v_c", ins)
                    E.mark(("c_ep_add", tq, tt, dcg), "v_c")
                E.mark(("c_ep_out", tq, tt), "v_c")

    def run_all(engs):
        body_sync(engs["sync"])
        body_gpsimd(engs["gpsimd"])
        body_tensor(engs["tensor"])
        body_scalar(engs["scalar"])
        body_vector(engs["vector"])

    E.pass1 = True
    run_all({k: _Noop() for k in ("sync", "gpsimd", "tensor", "scalar", "vector")})
    E.pass1 = False
    E.reset()

    with nc.Block() as block:
        @block.sync
        def _(sync):
            body_sync(sync)

        @block.gpsimd
        def _(gp):
            body_gpsimd(gp)

        @block.tensor
        def _(t):
            body_tensor(t)

        @block.scalar
        def _(s):
            body_scalar(s)

        @block.vector
        def _(v):
            body_vector(v)

    ctx.close()
    if E.missing:
        import sys
        print("MISSING MARKS:", dict(list(E.missing.items())[:20]), file=sys.stderr)
    return nc



def host_images(cfg, W_Q, W_K, W_V, W_O, gate_w, w1, w2, shared_up, shared_down):
    DC, IC, NE, E_R = cfg.DC, cfg.IC, cfg.NE, cfg.E_R
    D = cfg.D

    def qkv_img(W):
        Hn = W.shape[0]
        r = W.reshape(Hn, 128, DC, 128)
        return np.ascontiguousarray(r.transpose(0, 3, 2, 1).reshape(Hn, 128, DC * 128))

    wq_img = qkv_img(W_Q)
    wk_img = qkv_img(W_K)
    wv_img = qkv_img(W_V)
    Hn = W_O.shape[0]
    r = W_O.reshape(Hn, DC, 128, 128)
    wo_img = np.ascontiguousarray(r.transpose(1, 3, 0, 2).reshape(DC, 128, Hn * 128))

    w1a = np.concatenate([w1, shared_up.reshape(4, 512, D)], axis=0)
    r = w1a.transpose(0, 2, 1)
    r = r.reshape(NE, DC, 128, 2, 2, 128)
    w1f = np.ascontiguousarray(
        r.transpose(0, 3, 2, 1, 4, 5).reshape(NE, 2, 128, DC * 2 * 128))

    w2a = np.concatenate(
        [w2, shared_down.reshape(D, 4, 512).transpose(1, 0, 2)], axis=0)
    r = w2a.transpose(0, 2, 1)
    r = r.reshape(NE, IC, 128, DC, 128)
    w2f = np.ascontiguousarray(
        r.transpose(3, 2, 0, 1, 4).reshape(DC, 128, NE * IC * 128))

    r = gate_w.reshape(E_R, DC, 128)
    gw_img = np.ascontiguousarray(r.transpose(2, 1, 0).reshape(128, DC * E_R))

    pp, ff = np.meshgrid(np.arange(128), np.arange(512), indexing="ij")
    diag = [np.where(pp + 128 * k > ff, np.float32(0.0), np.float32(1.0))
            for k in range(4)]
    ones_t = np.ones((128, 512), np.float32)
    zeros_t = np.zeros((128, 512), np.float32)
    bmask = {
        0: np.ascontiguousarray(np.stack([ones_t] * 4 + diag)).astype(ml_dtypes.bfloat16),
        1: np.ascontiguousarray(np.stack(diag + [zeros_t] * 4)).astype(ml_dtypes.bfloat16),
    }
    return dict(wq=wq_img, wk=wk_img, wv=wv_img, wo=wo_img, w1f=w1f, w2f=w2f,
                gw=gw_img, bmask=bmask)


_BUILD_CACHE = {}


def prepare_in_maps(cfg, x, W_Q, W_K, W_V, W_O, gw, gb, w1, w2, shu, shd):
    img = host_images(cfg, W_Q, W_K, W_V, W_O, gw, w1, w2, shu, shd)
    idb = np.eye(128).astype(ml_dtypes.bfloat16)
    idf = np.eye(128).astype(np.float32)
    onesb = np.ones((128, 1), ml_dtypes.bfloat16)
    onesf = np.ones((1, 128), np.float32)
    ones8 = np.ones((128, cfg.GSZ), np.float32)
    in_maps = []
    for core in range(cfg.n_cores):
        b = core // 2
        hh = core % 2
        chunks = cfg.own_chunks(hh)
        xo = np.concatenate([x[b, c * 512:(c + 1) * 512] for c in chunks], axis=0)
        in_maps.append({
            "x_in": np.ascontiguousarray(x[b]),
            "xo_in": np.ascontiguousarray(xo),
            "wq_in": img["wq"],
            "wk_in": img["wk"],
            "wv_in": img["wv"],
            "wo_in": img["wo"],
            "w1f_in": img["w1f"],
            "w2f_in": img["w2f"],
            "gwT_in": img["gw"],
            "gb_in": gb.reshape(1, -1).astype(np.float32),
            "bmask_in": img["bmask"][hh],
            "idb_in": idb,
            "idf_in": idf,
            "onesb_in": onesb,
            "onesf_in": onesf,
            "ones8_in": ones8,
        })
    return in_maps


def assemble_out(cfg, results, B):
    out = np.empty((B, cfg.S, cfg.D), np.float32)
    for core in range(cfg.n_cores):
        b = core // 2
        hh = core % 2
        for i, c in enumerate(cfg.own_chunks(hh)):
            out[b, c * 512:(c + 1) * 512] = \
                results[core]["out_half"][i * 512:(i + 1) * 512]
    return out


def kernel(L0_attn_x, L0_attn_W_Q, L0_attn_W_K, L0_attn_W_V, L0_attn_W_O,
           L0_ffn_gate_w, L0_ffn_gate_bias, L0_ffn_w1, L0_ffn_w2,
           L0_ffn_shared_up, L0_ffn_shared_down):
    cfg = CFG()
    x = np.ascontiguousarray(np.asarray(L0_attn_x, np.float32))
    W_Q = np.asarray(L0_attn_W_Q, np.float32)
    W_K = np.asarray(L0_attn_W_K, np.float32)
    W_V = np.asarray(L0_attn_W_V, np.float32)
    W_O = np.asarray(L0_attn_W_O, np.float32)
    gw = np.asarray(L0_ffn_gate_w, np.float32)
    gb = np.asarray(L0_ffn_gate_bias, np.float32)
    w1 = np.asarray(L0_ffn_w1, np.float32)
    w2 = np.asarray(L0_ffn_w2, np.float32)
    shu = np.asarray(L0_ffn_shared_up, np.float32)
    shd = np.asarray(L0_ffn_shared_down, np.float32)

    if "nc" not in _BUILD_CACHE:
        _BUILD_CACHE["nc"] = build_program(cfg)
    nc = _BUILD_CACHE["nc"]
    in_maps = prepare_in_maps(cfg, x, W_Q, W_K, W_V, W_O, gw, gb, w1, w2, shu, shd)
    res = run_bass_kernel_spmd(nc, in_maps, core_ids=list(range(cfg.n_cores)))
    return assemble_out(cfg, res.results, x.shape[0])
